# version 08
# baseline (speedup 1.0000x reference)
import numpy as np
import ml_dtypes
from contextlib import ExitStack

import concourse.bass as bass
import concourse.mybir as mybir
from concourse.bass_utils import run_bass_kernel_spmd

F32 = mybir.dt.float32
BF16 = mybir.dt.bfloat16
U8 = mybir.dt.uint8
AF = mybir.ActivationFunctionType
ALU = mybir.AluOpType
AX = mybir.AxisListType

S, Dm = 4096, 1024
NBLK, BLKT, NWIN, WT, NTL = 4, 1024, 2, 512, 8
NEXP, EH = 32, 512
NANC, AH = 16, 256
NHD, DK = 16, 64
NUNIT = NEXP + 1 + NANC
EPS = 1e-5
BIG = 1e30
INV_D = 1.0 / Dm

WNAMES = [
    "L0_attn_Wq", "L0_attn_Wk", "L0_attn_Wv", "L0_attn_Wo",
    "L0_ffn_gate_weight", "L0_ffn_gate_bias",
    "L0_ffn_w_gate", "L0_ffn_w_up", "L0_ffn_w_down",
    "L0_ffn_shared_w_gate", "L0_ffn_shared_w_up", "L0_ffn_shared_w_down",
    "L0_ffn_anchor_w_gate", "L0_ffn_anchor_w_up", "L0_ffn_anchor_w_down",
]

ENGS = ("tensor", "vector", "scalar", "sync", "gpsimd")


class Sched:
    def __init__(self):
        self.q = {e: [] for e in ENGS}
        self.sems = {}
        self.ct = {}
        self.last_wait = {}
        self.trace = {e: [] for e in ENGS}

    def add_sem(self, name, handle):
        self.sems[name] = handle
        self.ct[name] = 0

    def wait(self, eng, sem, val):
        if not val or val <= 0:
            return
        key = (eng, sem)
        if self.last_wait.get(key, 0) >= val:
            return
        self.last_wait[key] = val
        h = self.sems[sem]
        self.trace[eng].append(("wait", sem, val))
        self.q[eng].append(lambda E, h=h, v=val: E.wait_ge(h, v))

    def waitp(self, eng, pair):
        if pair is not None:
            self.wait(eng, pair[0], pair[1])

    def op(self, eng, fn, inc=None, n=1):
        if inc is None:
            self.q[eng].append(fn)
            self.trace[eng].append(("op", None, 0))
            return None
        h = self.sems[inc]

        def g(E, fn=fn, h=h, n=n):
            fn(E).then_inc(h, n)

        self.q[eng].append(g)
        self.ct[inc] += n
        self.trace[eng].append(("inc", inc, n))
        return self.ct[inc]

    def vop(self, fn):
        v = self.op("vector", fn, inc="sv")
        self.wait("vector", "sv", v)
        return v

    def sop(self, fn):
        v = self.op("scalar", fn, inc="ss")
        self.wait("scalar", "ss", v)
        return v

    def full_barrier(self):
        for e in ENGS:
            for s in self.sems:
                self.wait(e, s, self.ct[s])


def build_program(debug_stop=None):
    nc = bass.Bass("TRN2", target_bir_lowering=False)
    k = Sched()

    d = {}
    d["x"] = nc.dram_tensor("L0_attn_x", [S, Dm], F32, kind="ExternalInput").ap()
    shapes = {
        "L0_attn_Wq": [Dm, Dm], "L0_attn_Wk": [Dm, Dm], "L0_attn_Wv": [Dm, Dm],
        "L0_attn_Wo": [Dm, Dm],
        "L0_ffn_gate_weight": [NEXP, Dm], "L0_ffn_gate_bias": [NEXP],
        "L0_ffn_w_gate": [NEXP, Dm, EH], "L0_ffn_w_up": [NEXP, Dm, EH],
        "L0_ffn_w_down": [NEXP, EH, Dm],
        "L0_ffn_shared_w_gate": [Dm, EH], "L0_ffn_shared_w_up": [Dm, EH],
        "L0_ffn_shared_w_down": [EH, Dm],
        "L0_ffn_anchor_w_gate": [NANC, Dm, AH], "L0_ffn_anchor_w_up": [NANC, Dm, AH],
        "L0_ffn_anchor_w_down": [NANC, AH, Dm],
    }
    for n, sh in shapes.items():
        d[n] = nc.dram_tensor(n, sh, F32, kind="ExternalInput").ap()
    dout = nc.dram_tensor("out", [S, Dm], F32, kind="ExternalOutput").ap()
    dr1 = nc.dram_tensor("r1_scr", [S, Dm], F32, kind="Internal").ap()
    dr2 = nc.dram_tensor("r2_scr", [S, Dm], F32, kind="Internal").ap()
    dat = nc.dram_tensor("attnT_scr", [32, 128, 8, 128], BF16, kind="Internal").ap()

    idn_c = nc.inline_tensor(np.eye(128, dtype=np.float32), name="idn_c").ap()
    ones_c = nc.inline_tensor(np.ones((1, 128), dtype=np.float32), name="ones_c").ap()
    onescol_c = nc.inline_tensor(np.ones((128, 1), dtype=np.float32), name="onescol_c").ap()
    consts_c = nc.inline_tensor(
        np.tile(np.array([[EPS, 0.125, 0.5, -1.0]], dtype=np.float32), (128, 1)),
        name="consts_c").ap()
    mnp = np.zeros((128, 4, 512), dtype=ml_dtypes.bfloat16)
    for m in range(4):
        for p in range(128):
            mnp[p, m, 128 * m + p:] = 1.0
    masks_c = nc.inline_tensor(mnp, name="masks_c").ap()

    with ExitStack() as ctx:
        def sbuf(name, shape, dt=F32):
            return ctx.enter_context(nc.sbuf_tensor(name, shape, dt))

        idn = sbuf("idn", [128, 128])
        onesr = sbuf("onesr", [1, 128])
        onescol = sbuf("onescol", [128, 1])
        cst = sbuf("cst", [128, 4])
        gb_sb = sbuf("gb_sb", [1, NEXP])
        gwT = sbuf("gwT", [128, 8, NEXP], BF16)
        biasbc = sbuf("biasbc", [128, NEXP])
        sum_sl = sbuf("sum_sl", [128, 32]); sumsq_sl = sbuf("sumsq_sl", [128, 32])
        mean_sl = sbuf("mean_sl", [128, 32]); var_sl = sbuf("var_sl", [128, 32])
        std_sl = sbuf("std_sl", [128, 32]); rstd_sl = sbuf("rstd_sl", [128, 32])
        nb_sl = sbuf("nb_sl", [128, 32]); st_tmp = sbuf("st_tmp", [128, 1])
        xt2 = [sbuf(f"xt{i}", [128, Dm]) for i in range(2)]
        yf2 = [sbuf(f"yf{i}", [128, Dm]) for i in range(2)]
        scrA = sbuf("scrA", [128, Dm])
        ro2 = [sbuf(f"ro{i}", [128, Dm]) for i in range(2)]
        sc_t = sbuf("sc_t", [128, NEXP])
        m8g = sbuf("m8g", [128, 4, 8]); gs8 = sbuf("gs8", [128, 8])
        pen = sbuf("pen", [128, 4]); m8 = sbuf("m8", [128, 8])
        w6 = sbuf("w6", [128, NEXP]); pm = sbuf("pm", [128, 16])
        ssum = sbuf("ssum", [128, 1]); rec1 = sbuf("rec1", [128, 1]); rec2 = sbuf("rec2", [128, 1])
        cw2 = [sbuf(f"cw{i}", [128, NTL, NEXP]) for i in range(2)]
        sel2 = [sbuf(f"sel{i}", [128, NTL, NANC]) for i in range(2)]
        logits_sb = sbuf("logits_sb", [128, NTL, NEXP])
        scores_sb = sbuf("scores_sb", [128, NTL, NEXP])
        ht2 = [sbuf(f"ht{i}", [128, 4, 512], BF16) for i in range(2)]
        sg2 = [sbuf(f"sg{i}", [128, 512], BF16) for i in range(2)]
        csq = sbuf("csq", [1, 1]); csc = sbuf("csc", [1, 1])
        csq2 = sbuf("csq2", [1, 1]); csc2 = sbuf("csc2", [1, 1])
        negc2 = [sbuf(f"negc{i}", [128, 1]) for i in range(2)]
        rrs = sbuf("rrs", [1, 512]); rscr = sbuf("rscr", [1, 512])
        rb_sb = sbuf("rb_sb", [64, 512])
        ao2 = [sbuf(f"ao{i}", [64, 512], BF16) for i in range(2)]

        arena_big = sbuf("arena_big", [128, 65536], U8)
        arena_w = sbuf("arena_w", [128, 49152], U8)
        arena_y = sbuf("arena_y", [128, 32768], U8)
        ab = nc.lookup_mloc(arena_big).addr
        aw = nc.lookup_mloc(arena_w).addr
        ay = nc.lookup_mloc(arena_y).addr

        def ov(base, name, shape, dt, off):
            return nc.alloc_sbuf_tensor_at(name, shape, dt, offset=base + off)

        acc2 = [ov(ab, f"acc{i}", [128, NTL, Dm], F32, 32768 * i) for i in range(2)]
        yT_full = ov(ab, "yT_full", [128, 8, S], BF16, 0)
        graw = ov(ab, "graw", [NEXP, Dm], F32, 0)
        wg_b = [ov(aw, f"wg_b{i}", [128, 8, 512], BF16, 24576 * i) for i in range(2)]
        wu_b = [ov(aw, f"wu_b{i}", [128, 8, 512], BF16, 24576 * i + 8192) for i in range(2)]
        wd_b = [ov(aw, f"wd_b{i}", [128, 4, 1024], BF16, 24576 * i + 16384) for i in range(2)]
        qT2 = ov(aw, "qT2", [128, S], BF16, 0)
        kT2 = ov(aw, "kT2", [128, S], BF16, 8192)
        v2p = ov(aw, "v2p", [128, 32, 2, 65], BF16, 16384)
        wo_sb = ov(aw, "wo_sb", [128, 8, Dm], BF16, 24704)
        wq_sl = ov(aw, "wq_sl", [128, 8, 128], BF16, 41088)
        wk_sl = ov(aw, "wk_sl", [128, 8, 128], BF16, 43136)
        wv_sl = ov(aw, "wv_sl", [128, 8, 128], BF16, 45184)
        yT2 = [ov(ay, f"yTb{i}", [128, 8, BLKT], BF16, 16384 * i) for i in range(2)]
        qsq = ov(ay, "qsq", [128, 2048], F32, 0)
        at_sb = [ov(ay, f"at_sb{i}", [128, 8, 128], BF16, 8192 + 2048 * i)
                 for i in range(2)]
        masks = ov(ay, "masks", [128, 4, 512], BF16, 16384)
        expT = ov(ay, "expT", [128, 12, 512], BF16, 20480)

        ps = ctx.enter_context(nc.psum_tensor("ps", [128, 8, 512], F32))

        for nm in ("sx", "sw0", "sw1", "sxt0", "sxt1", "sro0", "sro1",
                   "sao0", "sao1", "sat0", "sat1", "st", "sv", "ss"):
            k.add_sem(nm, ctx.enter_context(nc.semaphore(nm)))

        T, V, Sc, Sy, G = "tensor", "vector", "scalar", "sync", "gpsimd"

        st8 = {
            "bank_free": [None] * 8,
            "ht_used_t": [0, 0],
            "sg_free_v": [0, 0],
            "wbuf_free_t": [0, 0],
            "wready": {},
            "yf_free": [None, None],
            "xt_free": [None, None],
            "ro_free": [None, None],
            "ao_free": [None, None],
            "at_free": [None, None],
            "scores_free_v": [0, 0],
            "eslot_t": [0] * 12,
            "yT_ready": [0, 0],
            "rrs_free": None,
        }

        pro_dma = 0
        for dst, src in ((idn, idn_c), (onesr, ones_c), (onescol, onescol_c),
                         (cst, consts_c),
                         (graw, d["L0_ffn_gate_weight"]),
                         (gb_sb, d["L0_ffn_gate_bias"].rearrange("(a b) -> a b", a=1))):
            pro_dma = k.op(Sy, lambda E, dst=dst, src=src: E.dma_start(
                out=dst[:], in_=src), inc="sx", n=16)
        k.wait(T, "sx", pro_dma)
        k.wait(Sc, "sx", pro_dma)
        k.wait(V, "sx", pro_dma)
        for kk in range(8):
            bb = kk % 2
            k.waitp(T, st8["bank_free"][bb])
            t_tr = k.op(T, lambda E, kk=kk, bb=bb: E.transpose(
                out=ps[:, bb, 0:32],
                in_=graw[0:32, 128 * kk:128 * (kk + 1)],
                identity=idn[0:32, 0:32]), inc="st")
            k.wait(V, "st", t_tr)
            v_tr = k.op(V, lambda E, kk=kk, bb=bb: E.tensor_copy(
                out=gwT[:, kk, :], in_=ps[:, bb, 0:32]), inc="sv")
            st8["bank_free"][bb] = ("sv", v_tr)
        k.waitp(T, st8["bank_free"][1])
        t_bc = k.op(T, lambda E: E.matmul(out=ps[:, 1, 0:NEXP], lhsT=onesr[:],
                                          rhs=gb_sb[:], start=True, stop=True), inc="st")
        k.wait(V, "st", t_bc)
        k.op(V, lambda E: E.tensor_copy(out=biasbc[:], in_=ps[:, 1, 0:NEXP]), inc="sv")
        v_pro = k.op(V, lambda E: E.memset(gs8[:, 4:8], -BIG), inc="sv")
        st8["bank_free"][0] = ("sv", v_pro)
        st8["bank_free"][1] = ("sv", v_pro)

        def ln_pass(src, ntiles, yT, slot0):
            for i in range(ntiles):
                t = i % 2
                sl = slot0 + i
                rows = 128 * sl
                k.waitp(Sy, st8["xt_free"][t])
                xdma = k.op(Sy, lambda E, t=t, rows=rows, src=src: E.dma_start(
                    out=xt2[t][:], in_=src[rows:rows + 128, :]), inc=f"sxt{t}", n=16)
                k.wait(V, f"sxt{t}", xdma)
                k.op(V, lambda E, t=t, sl=sl: E.tensor_scalar(
                    out=scrA[:], in0=xt2[t][:], scalar1=1.0, scalar2=0.0,
                    op0=ALU.mult, op1=ALU.add, accum_out=sum_sl[:, sl:sl + 1]))
                k.op(V, lambda E, t=t, sl=sl: E.scalar_tensor_tensor(
                    out=scrA[:], in0=xt2[t][:], scalar=1.0, in1=xt2[t][:],
                    op0=ALU.mult, op1=ALU.mult, accum_out=sumsq_sl[:, sl:sl + 1]))
                k.vop(lambda E, sl=sl: E.tensor_scalar(
                    out=mean_sl[:, sl:sl + 1], in0=sum_sl[:, sl:sl + 1],
                    scalar1=INV_D, scalar2=0.0, op0=ALU.mult, op1=ALU.add))
                k.vop(lambda E, sl=sl: E.tensor_scalar(
                    out=var_sl[:, sl:sl + 1], in0=sumsq_sl[:, sl:sl + 1],
                    scalar1=INV_D, scalar2=0.0, op0=ALU.mult, op1=ALU.add))
                k.vop(lambda E, sl=sl: E.tensor_mul(
                    out=st_tmp[:], in0=mean_sl[:, sl:sl + 1], in1=mean_sl[:, sl:sl + 1]))
                v_st = k.vop(lambda E, sl=sl: E.tensor_sub(
                    out=var_sl[:, sl:sl + 1], in0=var_sl[:, sl:sl + 1],
                    in1=st_tmp[:]))
                k.wait(Sc, "sv", v_st)
                s_sq = k.op(Sc, lambda E, sl=sl: E.activation(
                    out=std_sl[:, sl:sl + 1], in_=var_sl[:, sl:sl + 1], func=AF.Sqrt,
                    bias=cst[:, 0:1], scale=1.0), inc="ss")
                k.wait(V, "ss", s_sq)
                k.vop(lambda E, sl=sl: E.reciprocal(
                    out=rstd_sl[:, sl:sl + 1], in_=std_sl[:, sl:sl + 1]))
                v_nb = k.vop(lambda E, sl=sl: E.scalar_tensor_tensor(
                    out=nb_sl[:, sl:sl + 1], in0=mean_sl[:, sl:sl + 1], scalar=-1.0,
                    in1=rstd_sl[:, sl:sl + 1], op0=ALU.mult, op1=ALU.mult))
                k.wait(Sc, "sv", v_nb)
                k.waitp(Sc, st8["yf_free"][t])
                y_s = k.op(Sc, lambda E, t=t, sl=sl: E.activation(
                    out=yf2[t][:], in_=xt2[t][:], func=AF.Identity,
                    bias=nb_sl[:, sl:sl + 1], scale=rstd_sl[:, sl:sl + 1]), inc="ss")
                st8["xt_free"][t] = ("ss", y_s)
                k.wait(T, "ss", y_s)
                last_tr = 0
                for kk in range(8):
                    bb = kk % 2
                    k.waitp(T, st8["bank_free"][bb])
                    t_tr = k.op(T, lambda E, t=t, kk=kk, bb=bb: E.transpose(
                        out=ps[:, bb, 0:128], in_=yf2[t][:, 128 * kk:128 * (kk + 1)],
                        identity=idn[:]), inc="st")
                    last_tr = t_tr
                    k.wait(V, "st", t_tr)
                    v_ev = k.op(V, lambda E, kk=kk, bb=bb, i=i, yT=yT: E.tensor_copy(
                        out=yT[:, kk, 128 * i:128 * (i + 1)], in_=ps[:, bb, 0:128]),
                        inc="sv")
                    st8["bank_free"][bb] = ("sv", v_ev)
                st8["yf_free"][t] = ("st", last_tr)
            return k.ct["sv"]

        sc3 = sc_t[:].rearrange("p (a b) -> p a b", b=2)
        pm3 = pm[:].rearrange("p (a b) -> p a b", b=1)
        g3 = gs8[:, 0:4].rearrange("p (a b) -> p a b", b=1)

        def routing_tile(i, par):
            k.vop(lambda E, i=i: E.tensor_add(
                out=sc_t[:], in0=scores_sb[:, i, :], in1=biasbc[:]))
            for g in range(4):
                k.vop(lambda E, g=g: E.max(
                    out=m8g[:, g, :], in_=sc_t[:, 8 * g:8 * (g + 1)]))
            k.vop(lambda E: E.tensor_add(
                out=g3, in0=m8g[:, :, 0:1], in1=m8g[:, :, 1:2]))
            k.vop(lambda E: E.max(out=m8[:], in_=gs8[:]))
            k.vop(lambda E: E.tensor_scalar(
                out=pen[:], in0=gs8[:, 0:4], scalar1=m8[:, 1:2], scalar2=-BIG,
                op0=ALU.is_lt, op1=ALU.mult))
            for g in range(4):
                k.vop(lambda E, g=g: E.tensor_scalar_add(
                    out=sc_t[:, 8 * g:8 * (g + 1)], in0=sc_t[:, 8 * g:8 * (g + 1)],
                    scalar1=pen[:, g:g + 1]))
            k.vop(lambda E: E.max(out=m8[:], in_=sc_t[:]))
            k.vop(lambda E, i=i: E.tensor_single_scalar(
                out=w6[:], in_=sc_t[:], scalar=m8[:, 5:6], op=ALU.is_ge))
            k.vop(lambda E, i=i: E.tensor_mul(
                out=w6[:], in0=w6[:], in1=scores_sb[:, i, :]))
            k.vop(lambda E: E.reduce_sum(out=ssum[:], in_=w6[:], axis=AX.X))
            k.vop(lambda E: E.tensor_scalar_add(
                out=rec1[:], in0=ssum[:], scalar1=1e-20))
            k.vop(lambda E: E.reciprocal(out=rec2[:], in_=rec1[:]))
            k.vop(lambda E: E.tensor_scalar_mul(
                out=rec1[:], in0=rec2[:], scalar1=0.2))
            k.vop(lambda E, i=i, par=par: E.tensor_scalar_mul(
                out=cw2[par][:, i, :], in0=w6[:], scalar1=rec1[:]))
            k.vop(lambda E: E.tensor_tensor(
                out=pm3, in0=sc3[:, :, 0:1], in1=sc3[:, :, 1:2], op=ALU.max))
            k.vop(lambda E, i=i, par=par: E.tensor_scalar(
                out=sel2[par][:, i, :], in0=pm[:], scalar1=m8[:, 0:1], scalar2=0.1,
                op0=ALU.is_ge, op1=ALU.mult))

        def prep_block(src, blk, par, out):
            base = blk * BLKT
            sl0 = (blk % 4) * NTL

            def p_memset():
                k.op(V, lambda E, par=par: E.memset(acc2[par][:], 0.0), inc="sv")
            out.append(p_memset)

            def p_stats(i):
                def go(i=i):
                    t = i % 2
                    sl = sl0 + i
                    rows = base + 128 * i
                    k.waitp(Sy, st8["xt_free"][t])
                    xdma = k.op(Sy, lambda E, t=t, rows=rows: E.dma_start(
                        out=xt2[t][:], in_=src[rows:rows + 128, :]),
                        inc=f"sxt{t}", n=16)
                    k.wait(V, f"sxt{t}", xdma)
                    k.op(V, lambda E, t=t, sl=sl: E.tensor_scalar(
                        out=scrA[:], in0=xt2[t][:], scalar1=1.0, scalar2=0.0,
                        op0=ALU.mult, op1=ALU.add, accum_out=sum_sl[:, sl:sl + 1]))
                    v_s = k.op(V, lambda E, t=t, sl=sl: E.scalar_tensor_tensor(
                        out=scrA[:], in0=xt2[t][:], scalar=1.0, in1=xt2[t][:],
                        op0=ALU.mult, op1=ALU.mult,
                        accum_out=sumsq_sl[:, sl:sl + 1]), inc="sv")
                    st8["xt_free"][t] = ("sv", v_s)
                    k.wait(V, "sv", v_s)
                    k.vop(lambda E, sl=sl: E.tensor_scalar(
                        out=mean_sl[:, sl:sl + 1], in0=sum_sl[:, sl:sl + 1],
                        scalar1=INV_D, scalar2=0.0, op0=ALU.mult, op1=ALU.add))
                    k.vop(lambda E, sl=sl: E.tensor_scalar(
                        out=var_sl[:, sl:sl + 1], in0=sumsq_sl[:, sl:sl + 1],
                        scalar1=INV_D, scalar2=0.0, op0=ALU.mult, op1=ALU.add))
                    k.vop(lambda E, sl=sl: E.tensor_mul(
                        out=st_tmp[:], in0=mean_sl[:, sl:sl + 1],
                        in1=mean_sl[:, sl:sl + 1]))
                    k.vop(lambda E, sl=sl: E.tensor_sub(
                        out=var_sl[:, sl:sl + 1], in0=var_sl[:, sl:sl + 1],
                        in1=st_tmp[:]))
                return go
            for i in range(NTL):
                out.append(p_stats(i))

            def p_rstd():
                v_st = k.ct["sv"]
                k.wait(Sc, "sv", v_st)
                s_sq = k.op(Sc, lambda E: E.activation(
                    out=std_sl[:, sl0:sl0 + NTL], in_=var_sl[:, sl0:sl0 + NTL],
                    func=AF.Sqrt, bias=cst[:, 0:1], scale=1.0), inc="ss")
                k.wait(V, "ss", s_sq)
                k.vop(lambda E: E.reciprocal(
                    out=rstd_sl[:, sl0:sl0 + NTL], in_=std_sl[:, sl0:sl0 + NTL]))
                k.vop(lambda E: E.scalar_tensor_tensor(
                    out=nb_sl[:, sl0:sl0 + NTL], in0=mean_sl[:, sl0:sl0 + NTL],
                    scalar=-1.0, in1=rstd_sl[:, sl0:sl0 + NTL],
                    op0=ALU.mult, op1=ALU.mult))
            out.append(p_rstd)

            def p_norm(i):
                def go(i=i):
                    t = i % 2
                    sl = sl0 + i
                    rows = base + 128 * i
                    k.waitp(Sy, st8["xt_free"][t])
                    xdma = k.op(Sy, lambda E, t=t, rows=rows: E.dma_start(
                        out=xt2[t][:], in_=src[rows:rows + 128, :]),
                        inc=f"sxt{t}", n=16)
                    k.wait(Sc, f"sxt{t}", xdma)
                    k.wait(Sc, "sv", k.ct["sv"])
                    k.waitp(Sc, st8["yf_free"][t])
                    y_s = k.op(Sc, lambda E, t=t, sl=sl: E.activation(
                        out=yf2[t][:], in_=xt2[t][:], func=AF.Identity,
                        bias=nb_sl[:, sl:sl + 1], scale=rstd_sl[:, sl:sl + 1]),
                        inc="ss")
                    st8["xt_free"][t] = ("ss", y_s)
                    k.wait(T, "ss", y_s)
                    last_tr = 0
                    for kk in range(8):
                        bb = 6 + kk % 2
                        k.waitp(T, st8["bank_free"][bb])
                        t_tr = k.op(T, lambda E, t=t, kk=kk, bb=bb: E.transpose(
                            out=ps[:, bb, 0:128],
                            in_=yf2[t][:, 128 * kk:128 * (kk + 1)],
                            identity=idn[:]), inc="st")
                        last_tr = t_tr
                        k.wait(V, "st", t_tr)
                        v_ev = k.op(V, lambda E, kk=kk, bb=bb, i=i, par=par:
                                    E.tensor_copy(
                                        out=yT2[par][:, kk, 128 * i:128 * (i + 1)],
                                        in_=ps[:, bb, 0:128]), inc="sv")
                        st8["bank_free"][bb] = ("sv", v_ev)
                    st8["yf_free"][t] = ("st", last_tr)
                return go
            for i in range(NTL):
                out.append(p_norm(i))

            def p_logits():
                for i in range(NTL):
                    bb = 6 + i % 2
                    k.waitp(T, st8["bank_free"][bb])
                    t_lg = None
                    for kk in range(8):
                        t_lg = k.op(T, lambda E, kk=kk, i=i, bb=bb, par=par: E.matmul(
                            out=ps[:, bb, 0:NEXP],
                            lhsT=yT2[par][:, kk, 128 * i:128 * (i + 1)],
                            rhs=gwT[:, kk, :], start=(kk == 0), stop=(kk == 7)),
                            inc="st" if kk == 7 else None)
                    k.wait(V, "st", t_lg)
                    v_lg = k.op(V, lambda E, i=i, bb=bb: E.tensor_copy(
                        out=logits_sb[:, i, :], in_=ps[:, bb, 0:NEXP]), inc="sv")
                    st8["bank_free"][bb] = ("sv", v_lg)
                k.wait(Sc, "sv", k.ct["sv"])
                k.op(Sc, lambda E: E.activation(
                    out=scores_sb[:], in_=logits_sb[:], func=AF.Sigmoid), inc="ss")
            out.append(p_logits)

            def p_route(i):
                def go(i=i):
                    if i == 0:
                        k.wait(V, "ss", k.ct["ss"])
                    routing_tile(i, par)
                return go
            for i in range(NTL):
                out.append(p_route(i))

            def p_finish():
                st8["yT_ready"][par] = k.ct["sv"]
            out.append(p_finish)

        def unit_weight_dma(gu, after_t):
            par = gu % 2
            u = gu % NUNIT
            k.wait(G, "st", after_t)
            if u < NEXP:
                srcs = [(wg_b[par][:], d["L0_ffn_w_gate"][u]),
                        (wu_b[par][:], d["L0_ffn_w_up"][u]),
                        (wd_b[par][:], d["L0_ffn_w_down"][u])]
            elif u == NEXP:
                srcs = [(wg_b[par][:], d["L0_ffn_shared_w_gate"]),
                        (wu_b[par][:], d["L0_ffn_shared_w_up"]),
                        (wd_b[par][:], d["L0_ffn_shared_w_down"])]
            else:
                l = u - NEXP - 1
                srcs = [(wg_b[par][:, :, 0:AH], d["L0_ffn_anchor_w_gate"][l]),
                        (wu_b[par][:, :, 0:AH], d["L0_ffn_anchor_w_up"][l]),
                        (wd_b[par][:, 0:2, :], d["L0_ffn_anchor_w_down"][l])]
            last = 0
            for dst, srcw in srcs:
                srcw = srcw.rearrange("(a p) h -> p a h", p=128)
                last = k.op(G, lambda E, dst=dst, srcw=srcw: E.dma_start(
                    out=dst, in_=srcw), inc=f"sw{par}", n=16)
            st8["wready"][gu] = last

        def gu_quarters(yT, w, par_step, par_u, nq, yT_ready, wrdy):
            htr = 0
            for q in range(nq):
                b0 = (2 * q) % 6
                k.waitp(T, st8["bank_free"][b0])
                k.waitp(T, st8["bank_free"][b0 + 1])
                k.wait(T, f"sw{par_u}", wrdy)
                k.wait(T, "sv", yT_ready)

                def t_gu(E, q=q, w=w, par_u=par_u, yT=yT, b0=b0):
                    for kk in range(8):
                        E.matmul(ps[:, b0, :],
                                 lhsT=wg_b[par_u][:, kk, 128 * q:128 * (q + 1)],
                                 rhs=yT[:, kk, WT * w:WT * (w + 1)],
                                 start=(kk == 0), stop=(kk == 7))
                    ins = None
                    for kk in range(8):
                        ins = E.matmul(ps[:, b0 + 1, :],
                                       lhsT=wu_b[par_u][:, kk, 128 * q:128 * (q + 1)],
                                       rhs=yT[:, kk, WT * w:WT * (w + 1)],
                                       start=(kk == 0), stop=(kk == 7))
                    return ins
                t_q = k.op(T, t_gu, inc="st")
                k.wait(Sc, "st", t_q)
                k.wait(Sc, "sv", st8["sg_free_v"][q % 2])
                s_q = k.op(Sc, lambda E, q=q, b0=b0: E.activation(
                    out=sg2[q % 2][:], in_=ps[:, b0, :], func=AF.Silu), inc="ss")
                st8["bank_free"][b0] = ("ss", s_q)
                k.wait(V, "ss", s_q)
                k.wait(V, "st", max(t_q, st8["ht_used_t"][par_step]))
                v_q = k.op(V, lambda E, q=q, par_step=par_step, b0=b0: E.tensor_mul(
                    out=ht2[par_step][:, q, :], in0=sg2[q % 2][:],
                    in1=ps[:, b0 + 1, :]), inc="sv")
                htr = v_q
                st8["bank_free"][b0 + 1] = ("sv", v_q)
                st8["sg_free_v"][q % 2] = v_q
            return htr

        def down_unit(w, par_step, par_u, nk, ht_ready, cw_src, par_blk):
            last_t = 0
            for r in range(8):
                tt, j = r // 2, r % 2
                bank = r % 6
                k.waitp(T, st8["bank_free"][bank])
                k.wait(T, "sv", ht_ready)

                def t_dn(E, r=r, tt=tt, j=j, par_step=par_step, par_u=par_u,
                         nk=nk, bank=bank):
                    ins = None
                    for kk in range(nk):
                        ins = E.matmul(ps[:, bank, :],
                                       lhsT=ht2[par_step][:, kk, 128 * tt:128 * (tt + 1)],
                                       rhs=wd_b[par_u][:, kk, WT * j:WT * (j + 1)],
                                       start=(kk == 0), stop=(kk == nk - 1))
                    return ins
                t_r = k.op(T, t_dn, inc="st")
                last_t = t_r
                k.wait(V, "st", t_r)
                tile = 4 * w + tt
                sc_ap = cw_src(tile)
                v_r = k.op(V, lambda E, bank=bank, tile=tile, j=j, sc_ap=sc_ap,
                           par_blk=par_blk: E.scalar_tensor_tensor(
                               out=acc2[par_blk][:, tile, WT * j:WT * (j + 1)],
                               in0=ps[:, bank, :], scalar=sc_ap,
                               in1=acc2[par_blk][:, tile, WT * j:WT * (j + 1)],
                               op0=ALU.mult, op1=ALU.add), inc="sv")
                st8["bank_free"][bank] = ("sv", v_r)
            return last_t

        def cw_source(u, par_blk):
            if u < NEXP:
                return lambda tile, u=u: cw2[par_blk][:, tile, u:u + 1]
            if u == NEXP:
                return lambda tile: 1.0
            l = u - NEXP - 1
            return lambda tile, l=l: sel2[par_blk][:, tile, l:l + 1]

        def emit_epilogue(src, dst, blk, par_blk):
            base = blk * BLKT
            k.wait(V, "sv", k.ct["sv"])
            for i in range(NTL):
                t = i % 2
                rows = base + 128 * i
                k.waitp(Sy, st8["xt_free"][t])
                xdma = k.op(Sy, lambda E, t=t, rows=rows, src=src: E.dma_start(
                    out=xt2[t][:], in_=src[rows:rows + 128, :]), inc=f"sxt{t}", n=16)
                k.wait(V, f"sxt{t}", xdma)
                k.waitp(V, st8["ro_free"][t])
                v_e = k.op(V, lambda E, t=t, i=i, par_blk=par_blk:
                           E.scalar_tensor_tensor(
                               out=ro2[t][:], in0=acc2[par_blk][:, i, :], scalar=0.5,
                               in1=xt2[t][:], op0=ALU.mult, op1=ALU.add), inc="sv")
                st8["xt_free"][t] = ("sv", v_e)
                k.wait(Sy, "sv", v_e)
                odma = k.op(Sy, lambda E, t=t, rows=rows, dst=dst: E.dma_start(
                    out=dst[rows:rows + 128, :], in_=ro2[t][:]), inc=f"sro{t}", n=16)
                st8["ro_free"][t] = (f"sro{t}", odma)

        def ffn_phase(src, dst, gu0):
            gu = gu0
            pieces = []
            prep_block(src, 0, 0, pieces)
            for p in pieces:
                p()
            for blk in range(NBLK):
                par_blk = blk % 2
                yT_ready = st8["yT_ready"][par_blk]
                if blk > 0:
                    emit_epilogue(src, dst, blk - 1, (blk - 1) % 2)
                np_pieces = []
                if blk + 1 < NBLK:
                    prep_block(src, blk + 1, (blk + 1) % 2, np_pieces)
                pi = 0
                prev = None
                step = 0
                for u in range(NUNIT):
                    nq = 4 if u <= NEXP else 2
                    nk = 4 if u <= NEXP else 2
                    par_u = gu % 2
                    if gu == gu0 and blk == 0:
                        unit_weight_dma(gu, after_t=st8["wbuf_free_t"][gu % 2])
                        unit_weight_dma(gu + 1, after_t=st8["wbuf_free_t"][(gu + 1) % 2])
                    wrdy = st8["wready"][gu]
                    for w in range(NWIN):
                        par_step = step % 2
                        htr = gu_quarters(yT2[par_blk], w, par_step, par_u, nq,
                                          yT_ready, wrdy)
                        if prev is not None:
                            lt = down_unit(*prev)
                            st8["ht_used_t"][prev[1]] = lt
                            if prev[0] == NWIN - 1:
                                st8["wbuf_free_t"][(gu - 1) % 2] = lt
                                nxt = gu + 1
                                if nxt < gu0 + NBLK * NUNIT and nxt not in st8["wready"]:
                                    unit_weight_dma(nxt, after_t=st8["wbuf_free_t"][nxt % 2])
                        prev = (w, par_step, par_u, nk, htr, cw_source(u, par_blk),
                                par_blk)
                        step += 1
                    gu += 1
                    while pi < len(np_pieces) and pi * NUNIT <= (u + 1) * len(np_pieces):
                        np_pieces[pi]()
                        pi += 1
                lt = down_unit(*prev)
                st8["ht_used_t"][prev[1]] = lt
                st8["wbuf_free_t"][(gu - 1) % 2] = lt
                nxt = gu + 1
                if nxt < gu0 + NBLK * NUNIT and nxt not in st8["wready"]:
                    unit_weight_dma(nxt, after_t=st8["wbuf_free_t"][nxt % 2])
                while pi < len(np_pieces):
                    np_pieces[pi]()
                    pi += 1
            emit_epilogue(src, dst, NBLK - 1, (NBLK - 1) % 2)
            return gu

        def headnorm_sq(srcT, pb, dst01):
            t_last_prev = st8.get("qsq_free_t", 0)
            for half in range(2):
                colsl = slice(2048 * half, 2048 * (half + 1))
                k.wait(V, "st", t_last_prev)
                v_sq = k.op(V, lambda E, colsl=colsl, pb=pb, srcT=srcT: E.tensor_mul(
                    out=qsq[pb:pb + 64, :], in0=srcT[pb:pb + 64, colsl],
                    in1=srcT[pb:pb + 64, colsl]), inc="sv")
                k.wait(T, "sv", v_sq)
                t_n = 0
                for c in range(4):
                    k.waitp(T, st8["bank_free"][4 + c])
                    t_n = k.op(T, lambda E, c=c, pb=pb: E.matmul(
                        out=ps[0:1, 4 + c, :], lhsT=onescol[pb:pb + 64, :],
                        rhs=qsq[pb:pb + 64, WT * c:WT * (c + 1)],
                        start=True, stop=True), inc="st")
                t_last_prev = t_n
                k.wait(V, "st", t_n)
                tgt = st_tmp[0:1, 0:1] if half == 0 else dst01[:]
                v_red = k.vop(lambda E, tgt=tgt: E.tensor_reduce(
                    out=tgt, in_=ps[0:1, 4:8, :], axis=AX.XY, op=ALU.max))
                for c in range(4):
                    st8["bank_free"][4 + c] = ("sv", v_red)
            st8["qsq_free_t"] = t_last_prev
            k.vop(lambda E, dst01=dst01: E.tensor_tensor(
                out=dst01[:], in0=dst01[:], in1=st_tmp[0:1, 0:1], op=ALU.max))

        def attention_phase():
            mask_dma = k.op(Sy, lambda E: E.dma_start(out=masks[:], in_=masks_c),
                            inc="sx", n=16)
            k.wait(V, "sx", mask_dma)
            yT_ready = ln_pass(dr1, 32, yT_full, slot0=0)
            v_ones = k.op(V, lambda E: E.memset(v2p[:, :, :, 64:65], 1.0), inc="sv")
            last_qkv_t = 0
            for hp in range(8):
                k.wait(G, "st", last_qkv_t)
                wsl_last = 0
                for dstw, nm in ((wq_sl, "L0_attn_Wq"), (wk_sl, "L0_attn_Wk"),
                                 (wv_sl, "L0_attn_Wv")):
                    srcw = d[nm][:, 128 * hp:128 * (hp + 1)].rearrange(
                        "(a p) c -> p a c", p=128)
                    wsl_last = k.op(G, lambda E, dstw=dstw, srcw=srcw: E.dma_start(
                        out=dstw[:], in_=srcw), inc="sw0", n=16)
                for wsl, dstt, scale in ((wq_sl, qT2, 0.125), (wk_sl, kT2, 1.0)):
                    for w8 in range(8):
                        bb = 6 + w8 % 2
                        k.waitp(T, st8["bank_free"][bb])
                        k.wait(T, "sw0", wsl_last)
                        k.wait(T, "sv", yT_ready)

                        def t_q(E, wsl=wsl, w8=w8, bb=bb):
                            ins = None
                            for kk in range(8):
                                ins = E.matmul(ps[:, bb, :], lhsT=wsl[:, kk, :],
                                               rhs=yT_full[:, kk, WT * w8:WT * (w8 + 1)],
                                               start=(kk == 0), stop=(kk == 7))
                            return ins
                        t_w = k.op(T, t_q, inc="st")
                        k.wait(Sc, "st", t_w)
                        scap = cst[:, 1:2] if scale == 0.125 else 1.0
                        s_w = k.op(Sc, lambda E, dstt=dstt, w8=w8, bb=bb, scap=scap:
                                   E.activation(
                                       out=dstt[:, WT * w8:WT * (w8 + 1)],
                                       in_=ps[:, bb, :], func=AF.Identity,
                                       scale=scap), inc="ss")
                        st8["bank_free"][bb] = ("ss", s_w)
                for tt in range(32):
                    bb = 6 + tt % 2
                    k.waitp(T, st8["bank_free"][bb])
                    k.wait(T, "sw0", wsl_last)
                    k.wait(T, "sv", v_ones)

                    def t_v(E, tt=tt, bb=bb):
                        ins = None
                        for kk in range(8):
                            ins = E.matmul(ps[:, bb, 0:128],
                                           lhsT=yT_full[:, kk, 128 * tt:128 * (tt + 1)],
                                           rhs=wv_sl[:, kk, :],
                                           start=(kk == 0), stop=(kk == 7))
                        return ins
                    t_w = k.op(T, t_v, inc="st")
                    k.wait(V, "st", t_w)
                    v_w = k.op(V, lambda E, tt=tt, bb=bb: E.tensor_copy(
                        out=v2p[:, tt, :, 0:64],
                        in_=ps[:, bb, 0:128].rearrange("p (h c) -> p h c", h=2)),
                        inc="sv")
                    st8["bank_free"][bb] = ("sv", v_w)
                qkv_done_v = k.ct["sv"]
                last_qkv_t = k.ct["st"]
                k.wait(V, "ss", k.ct["ss"])
                headnorm_sq(qT2, 0, csq)
                headnorm_sq(kT2, 0, csc)
                headnorm_sq(qT2, 64, csq2)
                headnorm_sq(kT2, 64, csc2)
                k.vop(lambda E: E.tensor_mul(out=csq[:], in0=csq[:], in1=csc[:]))
                k.vop(lambda E: E.tensor_mul(out=csq2[:], in0=csq2[:], in1=csc2[:]))
                v_c = k.vop(lambda E: E.tensor_tensor(
                    out=csq[:], in0=csq[:], in1=csq2[:], op=ALU.max))
                k.wait(Sc, "sv", v_c)
                s_ln = k.sop(lambda E: E.activation(
                    out=csc[:], in_=csq[:], func=AF.Ln))
                s_c = k.sop(lambda E: E.activation(
                    out=csq[:], in_=csc[:], func=AF.Exp, scale=cst[0:1, 2:3]))
                k.wait(V, "ss", s_c)
                v_c2 = k.vop(lambda E: E.tensor_scalar(
                    out=csc[:], in0=csq[:], scalar1=-1.0, scalar2=-0.1,
                    op0=ALU.mult, op1=ALU.add))
                k.wait(T, "sv", v_c2)
                k.waitp(T, st8["bank_free"][6])
                t_nc = k.op(T, lambda E: E.matmul(
                    out=ps[:, 6, 0:1], lhsT=onesr[:], rhs=csc[:],
                    start=True, stop=True), inc="st")
                k.wait(V, "st", t_nc)
                v_nc = k.vop(lambda E: E.tensor_copy(
                    out=negc2[0][:], in_=ps[:, 6, 0:1]))
                st8["bank_free"][6] = ("sv", v_nc)
                negc_v = k.ct["sv"]
                for w in range(8):
                    nkt = 4 * w + 4
                    nb2 = nkt // 2
                    exp_s = [[0] * nb2, [0] * nb2]
                    msk_v = [[0] * nkt, [0] * nkt]

                    def slot(h01, b2, kt):
                        return 6 * h01 + 2 * (b2 % 3) + kt % 2

                    def emit_scores(h01, b2, w=w):
                        pb = 64 * h01
                        for kt in (2 * b2, 2 * b2 + 1):
                            bb = 2 * h01 + kt % 2
                            if b2 >= 1:
                                k.wait(T, "ss", exp_s[h01][b2 - 1])
                            else:
                                k.waitp(T, st8["bank_free"][bb])
                            k.wait(T, "sv", qkv_done_v)
                            k.op(T, lambda E, kt=kt, bb=bb, pb=pb, w=w: E.matmul(
                                ps[:, bb, :],
                                lhsT=kT2[pb:pb + 64, 128 * kt:128 * (kt + 1)],
                                rhs=qT2[pb:pb + 64, WT * w:WT * (w + 1)],
                                start=True, stop=True), inc="st")
                        return k.ct["st"]

                    def emit_exp(h01, b2, sc_done, w=w):
                        bb0 = 2 * h01
                        sl0 = slot(h01, b2, 0)
                        k.wait(Sc, "st", sc_done)
                        k.wait(Sc, "sv", negc_v)
                        k.wait(Sc, "st", max(st8["eslot_t"][sl0],
                                             st8["eslot_t"][sl0 + 1]))
                        s_e = k.op(Sc, lambda E, bb0=bb0, sl0=sl0: E.activation(
                            out=expT[:, sl0:sl0 + 2, :], in_=ps[:, bb0:bb0 + 2, :],
                            func=AF.Exp, bias=negc2[0][:], scale=1.0), inc="ss")
                        exp_s[h01][b2] = s_e
                        st8["bank_free"][bb0] = ("ss", s_e)
                        st8["bank_free"][bb0 + 1] = ("ss", s_e)
                        for kt in (2 * b2, 2 * b2 + 1):
                            o = 128 * kt - 512 * w
                            if o >= 0:
                                mi = o // 128
                                sl = slot(h01, b2, kt)
                                k.wait(V, "ss", s_e)
                                msk_v[h01][kt] = k.op(V, lambda E, sl=sl, mi=mi:
                                                      E.tensor_mul(
                                                          out=expT[:, sl, :],
                                                          in0=expT[:, sl, :],
                                                          in1=masks[:, mi, :]), inc="sv")

                    def emit_attnv(b2, w=w, nkt=nkt):
                        for h01 in range(2):
                            ob = 4 + h01
                            for kt in (2 * b2, 2 * b2 + 1):
                                sl = slot(h01, b2, kt)
                                k.wait(T, "ss", exp_s[h01][b2])
                                if msk_v[h01][kt]:
                                    k.wait(T, "sv", msk_v[h01][kt])
                                if kt == 0:
                                    k.waitp(T, st8["bank_free"][ob])
                                t_a = k.op(T, lambda E, kt=kt, h01=h01, nkt=nkt, ob=ob, sl=sl:
                                           E.matmul(
                                               ps[0:65, ob, :], lhsT=v2p[:, kt, h01, :],
                                               rhs=expT[:, sl, :],
                                               start=(kt == 0), stop=(kt == nkt - 1)),
                                           inc="st")
                                st8["eslot_t"][sl] = t_a

                    for b2 in range(nb2):
                        scd0 = emit_scores(0, b2)
                        scd1 = emit_scores(1, b2)
                        if b2 >= 2:
                            emit_attnv(b2 - 2)
                        emit_exp(0, b2, scd0)
                        emit_exp(1, b2, scd1)
                    for b2 in range(max(nb2 - 2, 0), nb2):
                        emit_attnv(b2)
                    t_fl = k.ct["st"]
                    for h01 in range(2):
                        head = 2 * hp + h01
                        ob = 4 + h01
                        k.wait(Sc, "st", t_fl)
                        k.waitp(Sc, st8["rrs_free"])
                        k.sop(lambda E, ob=ob: E.activation(
                            out=rscr[:], in_=ps[64:65, ob, :], func=AF.Ln))
                        s_rr = k.sop(lambda E: E.activation(
                            out=rrs[:], in_=rscr[:], func=AF.Exp, scale=cst[0:1, 3:4]))
                        k.wait(T, "ss", s_rr)
                        k.waitp(T, st8["bank_free"][6])
                        t_b2 = k.op(T, lambda E: E.matmul(
                            out=ps[0:64, 6, :], lhsT=onesr[0:1, 0:64], rhs=rrs[:],
                            start=True, stop=True), inc="st")
                        st8["rrs_free"] = ("st", t_b2)
                        k.wait(V, "st", t_b2)
                        v_rb = k.vop(lambda E: E.tensor_copy(
                            out=rb_sb[:], in_=ps[0:64, 6, :]))
                        st8["bank_free"][6] = ("sv", v_rb)
                        tb = h01
                        k.waitp(V, st8["ao_free"][tb])
                        v_ao = k.op(V, lambda E, tb=tb, ob=ob: E.tensor_mul(
                            out=ao2[tb][:], in0=ps[0:64, ob, :], in1=rb_sb[:]),
                            inc="sv")
                        st8["bank_free"][ob] = ("sv", v_ao)
                        k.wait(Sy, "sv", v_ao)
                        pr0 = 64 * (head % 2)
                        dst_ap = dat[4 * w:4 * (w + 1), pr0:pr0 + 64, head // 2, :]\
                            .rearrange("t r c -> r t c")
                        src_ap = ao2[tb][:].rearrange("r (t c) -> r t c", c=128)
                        o_dma = k.op(Sy, lambda E, dst_ap=dst_ap, src_ap=src_ap:
                                     E.dma_start(out=dst_ap, in_=src_ap),
                                     inc=f"sao{tb}", n=16)
                        st8["ao_free"][tb] = (f"sao{tb}", o_dma)
            k.wait(G, "st", k.ct["st"])
            wo_last = k.op(G, lambda E: E.dma_start(
                out=wo_sb[:],
                in_=d["L0_attn_Wo"].rearrange("(a p) h -> p a h", p=128)),
                inc="sw1", n=16)
            r2dst = dout if debug_stop == "B" else dr2
            for i in range(32):
                t = i % 2
                rows = 128 * i
                k.wait(Sy, "sao0", k.ct["sao0"])
                k.wait(Sy, "sao1", k.ct["sao1"])
                k.waitp(Sy, st8["at_free"][t])
                a_dma = k.op(Sy, lambda E, t=t, i=i: E.dma_start(
                    out=at_sb[t][:], in_=dat[i]), inc=f"sat{t}", n=16)
                k.waitp(Sy, st8["xt_free"][t])
                x_dma = k.op(Sy, lambda E, t=t, rows=rows: E.dma_start(
                    out=xt2[t][:], in_=dr1[rows:rows + 128, :]), inc=f"sxt{t}", n=16)
                t_half = [0, 0]
                for j in range(2):
                    bb = 6 + j
                    k.waitp(T, st8["bank_free"][bb])
                    k.wait(T, f"sat{t}", a_dma)
                    k.wait(T, "sw1", wo_last)

                    def t_wo(E, t=t, j=j, bb=bb):
                        ins = None
                        for kk in range(8):
                            ins = E.matmul(ps[:, bb, :], lhsT=at_sb[t][:, kk, :],
                                           rhs=wo_sb[:, kk, WT * j:WT * (j + 1)],
                                           start=(kk == 0), stop=(kk == 7))
                        return ins
                    t_half[j] = k.op(T, t_wo, inc="st")
                st8["at_free"][t] = ("st", t_half[1])
                k.wait(Sc, f"sxt{t}", x_dma)
                k.waitp(Sc, st8["yf_free"][t])
                s_y = k.op(Sc, lambda E, t=t, i=i: E.activation(
                    out=yf2[t][:], in_=xt2[t][:], func=AF.Identity,
                    bias=nb_sl[:, i:i + 1], scale=rstd_sl[:, i:i + 1]), inc="ss")
                st8["xt_free"][t] = ("ss", s_y)
                k.wait(V, "ss", s_y)
                v_t1 = k.op(V, lambda E, t=t: E.tensor_add(
                    out=scrA[:], in0=yf2[t][:], in1=xt2[t][:]), inc="sv")
                st8["yf_free"][t] = ("sv", v_t1)
                k.wait(V, "sv", v_t1)
                k.waitp(V, st8["ro_free"][t])
                v_r = 0
                for j in range(2):
                    k.wait(V, "st", t_half[j])
                    v_r = k.op(V, lambda E, t=t, j=j: E.tensor_add(
                        out=ro2[t][:, WT * j:WT * (j + 1)],
                        in0=scrA[:, WT * j:WT * (j + 1)], in1=ps[:, 6 + j, :]),
                        inc="sv")
                    st8["bank_free"][6 + j] = ("sv", v_r)
                k.wait(Sy, "sv", v_r)
                o_dma = k.op(Sy, lambda E, t=t, rows=rows, r2dst=r2dst: E.dma_start(
                    out=r2dst[rows:rows + 128, :], in_=ro2[t][:]), inc=f"sro{t}", n=16)
                st8["ro_free"][t] = (f"sro{t}", o_dma)

        gu = ffn_phase(d["x"], dout if debug_stop == "A" else dr1, 0)
        if debug_stop != "A":
            k.full_barrier()
            attention_phase()
            if debug_stop != "B":
                k.full_barrier()
                ffn_phase(dr2, dout, gu)
        for s in k.sems:
            k.wait(Sy, s, k.ct[s])

        with nc.Block() as block:
            @block.sync
            def _(E):
                for fn in k.q["sync"]:
                    fn(E)

            @block.gpsimd
            def _(E):
                for fn in k.q["gpsimd"]:
                    fn(E)

            @block.vector
            def _(E):
                for fn in k.q["vector"]:
                    fn(E)

            @block.scalar
            def _(E):
                for fn in k.q["scalar"]:
                    fn(E)

            @block.tensor
            def _(E):
                for fn in k.q["tensor"]:
                    fn(E)

    return nc


def kernel(**inputs):
    x = np.ascontiguousarray(np.asarray(inputs["L0_attn_x"], dtype=np.float32))
    base = {n: np.ascontiguousarray(np.asarray(inputs[n], dtype=np.float32))
            for n in WNAMES}
    nc = build_program()
    in_maps = []
    for c in range(8):
        m = dict(base)
        m["L0_attn_x"] = np.ascontiguousarray(x[c])
        in_maps.append(m)
    res = run_bass_kernel_spmd(nc, in_maps, core_ids=list(range(8)))
    return np.stack([r["out"] for r in res.results], axis=0)


# version 09
# speedup vs baseline: 1.0689x; 1.0689x over previous
import numpy as np
import ml_dtypes
from contextlib import ExitStack

import concourse.bass as bass
import concourse.mybir as mybir
from concourse.bass_utils import run_bass_kernel_spmd

F32 = mybir.dt.float32
BF16 = mybir.dt.bfloat16
U8 = mybir.dt.uint8
AF = mybir.ActivationFunctionType
ALU = mybir.AluOpType
AX = mybir.AxisListType

S, Dm = 4096, 1024
NBLK, BLKT, NWIN, WT, NTL = 2, 2048, 4, 512, 16
NEXP, EH = 32, 512
NANC, AH = 16, 256
NHD, DK = 16, 64
NUNIT = NEXP + 1 + NANC
EPS = 1e-5
BIG = 1e30
INV_D = 1.0 / Dm

WNAMES = [
    "L0_attn_Wq", "L0_attn_Wk", "L0_attn_Wv", "L0_attn_Wo",
    "L0_ffn_gate_weight", "L0_ffn_gate_bias",
    "L0_ffn_w_gate", "L0_ffn_w_up", "L0_ffn_w_down",
    "L0_ffn_shared_w_gate", "L0_ffn_shared_w_up", "L0_ffn_shared_w_down",
    "L0_ffn_anchor_w_gate", "L0_ffn_anchor_w_up", "L0_ffn_anchor_w_down",
]

ENGS = ("tensor", "vector", "scalar", "sync", "gpsimd")


class Sched:
    def __init__(self):
        self.q = {e: [] for e in ENGS}
        self.sems = {}
        self.ct = {}
        self.last_wait = {}
        self.trace = {e: [] for e in ENGS}

    def add_sem(self, name, handle):
        self.sems[name] = handle
        self.ct[name] = 0

    def wait(self, eng, sem, val):
        if not val or val <= 0:
            return
        key = (eng, sem)
        if self.last_wait.get(key, 0) >= val:
            return
        self.last_wait[key] = val
        h = self.sems[sem]
        self.trace[eng].append(("wait", sem, val))
        self.q[eng].append(lambda E, h=h, v=val: E.wait_ge(h, v))

    def waitp(self, eng, pair):
        if pair is not None:
            self.wait(eng, pair[0], pair[1])

    def op(self, eng, fn, inc=None, n=1):
        if inc is None:
            self.q[eng].append(fn)
            self.trace[eng].append(("op", None, 0))
            return None
        h = self.sems[inc]

        def g(E, fn=fn, h=h, n=n):
            fn(E).then_inc(h, n)

        self.q[eng].append(g)
        self.ct[inc] += n
        self.trace[eng].append(("inc", inc, n))
        return self.ct[inc]

    def vop(self, fn):
        v = self.op("vector", fn, inc="sv")
        self.wait("vector", "sv", v)
        return v

    def sop(self, fn):
        v = self.op("scalar", fn, inc="ss")
        self.wait("scalar", "ss", v)
        return v

    def full_barrier(self):
        for e in ENGS:
            for s in self.sems:
                self.wait(e, s, self.ct[s])


def build_program(debug_stop=None):
    nc = bass.Bass("TRN2", target_bir_lowering=False)
    k = Sched()

    d = {}
    d["x"] = nc.dram_tensor("L0_attn_x", [S, Dm], F32, kind="ExternalInput").ap()
    shapes = {
        "L0_attn_Wq": [Dm, Dm], "L0_attn_Wk": [Dm, Dm], "L0_attn_Wv": [Dm, Dm],
        "L0_attn_Wo": [Dm, Dm],
        "L0_ffn_gate_weight": [NEXP, Dm], "L0_ffn_gate_bias": [NEXP],
        "L0_ffn_w_gate": [NEXP, Dm, EH], "L0_ffn_w_up": [NEXP, Dm, EH],
        "L0_ffn_w_down": [NEXP, EH, Dm],
        "L0_ffn_shared_w_gate": [Dm, EH], "L0_ffn_shared_w_up": [Dm, EH],
        "L0_ffn_shared_w_down": [EH, Dm],
        "L0_ffn_anchor_w_gate": [NANC, Dm, AH], "L0_ffn_anchor_w_up": [NANC, Dm, AH],
        "L0_ffn_anchor_w_down": [NANC, AH, Dm],
    }
    for n, sh in shapes.items():
        d[n] = nc.dram_tensor(n, sh, F32, kind="ExternalInput").ap()
    dout = nc.dram_tensor("out", [S, Dm], F32, kind="ExternalOutput").ap()
    dr1 = nc.dram_tensor("r1_scr", [S, Dm], F32, kind="Internal").ap()
    dr2 = nc.dram_tensor("r2_scr", [S, Dm], F32, kind="Internal").ap()
    dat = nc.dram_tensor("attnT_scr", [32, 128, 8, 128], BF16, kind="Internal").ap()

    idn_c = nc.inline_tensor(np.eye(128, dtype=np.float32), name="idn_c").ap()
    ones_c = nc.inline_tensor(np.ones((1, 128), dtype=np.float32), name="ones_c").ap()
    onescol_c = nc.inline_tensor(np.ones((128, 1), dtype=np.float32), name="onescol_c").ap()
    consts_c = nc.inline_tensor(
        np.tile(np.array([[EPS, 0.125, 0.5, -1.0]], dtype=np.float32), (128, 1)),
        name="consts_c").ap()
    mnp = np.zeros((128, 4, 512), dtype=ml_dtypes.bfloat16)
    for m in range(4):
        for p in range(128):
            mnp[p, m, 128 * m + p:] = 1.0
    masks_c = nc.inline_tensor(mnp, name="masks_c").ap()

    with ExitStack() as ctx:
        def sbuf(name, shape, dt=F32):
            return ctx.enter_context(nc.sbuf_tensor(name, shape, dt))

        idn = sbuf("idn", [128, 128])
        onesr = sbuf("onesr", [1, 128])
        onescol = sbuf("onescol", [128, 1])
        cst = sbuf("cst", [128, 4])
        gb_sb = sbuf("gb_sb", [1, NEXP])
        gwT = sbuf("gwT", [128, 8, NEXP], BF16)
        biasbc = sbuf("biasbc", [128, NEXP])
        sum_sl = sbuf("sum_sl", [128, 32]); sumsq_sl = sbuf("sumsq_sl", [128, 32])
        mean_sl = sbuf("mean_sl", [128, 32]); var_sl = sbuf("var_sl", [128, 32])
        std_sl = sbuf("std_sl", [128, 32]); rstd_sl = sbuf("rstd_sl", [128, 32])
        nb_sl = sbuf("nb_sl", [128, 32]); st_tmp = sbuf("st_tmp", [128, 1])
        xt2 = [sbuf(f"xt{i}", [128, Dm]) for i in range(2)]
        yf2 = [sbuf(f"yf{i}", [128, Dm]) for i in range(2)]
        scrA = sbuf("scrA", [128, Dm])
        ro2 = [sbuf(f"ro{i}", [128, Dm]) for i in range(2)]
        scores2 = [sbuf(f"scores{i}", [128, NEXP]) for i in range(2)]
        sc_t = sbuf("sc_t", [128, NEXP])
        m8g = sbuf("m8g", [128, 4, 8]); gs8 = sbuf("gs8", [128, 8])
        pen = sbuf("pen", [128, 4]); m8 = sbuf("m8", [128, 8])
        w6 = sbuf("w6", [128, NEXP]); pm = sbuf("pm", [128, 16])
        ssum = sbuf("ssum", [128, 1]); rec1 = sbuf("rec1", [128, 1]); rec2 = sbuf("rec2", [128, 1])
        cw = sbuf("cw", [128, NTL, NEXP])
        sel = sbuf("sel", [128, NTL, NANC])
        ht2 = [sbuf(f"ht{i}", [128, 4, 512], BF16) for i in range(2)]
        sg2 = [sbuf(f"sg{i}", [128, 512], BF16) for i in range(2)]
        csq = sbuf("csq", [1, 1]); csc = sbuf("csc", [1, 1])
        csq2 = sbuf("csq2", [1, 1]); csc2 = sbuf("csc2", [1, 1])
        negc2 = [sbuf(f"negc{i}", [128, 1]) for i in range(2)]
        rrs = sbuf("rrs", [1, 512]); rscr = sbuf("rscr", [1, 512])
        rb_sb = sbuf("rb_sb", [64, 512])
        ao2 = [sbuf(f"ao{i}", [64, 512], BF16) for i in range(2)]

        arena_big = sbuf("arena_big", [128, 65536], U8)
        arena_w = sbuf("arena_w", [128, 49152], U8)
        arena_y = sbuf("arena_y", [128, 32768], U8)
        ab = nc.lookup_mloc(arena_big).addr
        aw = nc.lookup_mloc(arena_w).addr
        ay = nc.lookup_mloc(arena_y).addr

        def ov(base, name, shape, dt, off):
            return nc.alloc_sbuf_tensor_at(name, shape, dt, offset=base + off)

        acc = ov(ab, "acc", [128, NTL, Dm], F32, 0)
        yT_full = ov(ab, "yT_full", [128, 8, S], BF16, 0)
        graw = ov(ab, "graw", [NEXP, Dm], F32, 0)
        wg_b = [ov(aw, f"wg_b{i}", [128, 8, 512], BF16, 24576 * i) for i in range(2)]
        wu_b = [ov(aw, f"wu_b{i}", [128, 8, 512], BF16, 24576 * i + 8192) for i in range(2)]
        wd_b = [ov(aw, f"wd_b{i}", [128, 4, 1024], BF16, 24576 * i + 16384) for i in range(2)]
        qT2 = ov(aw, "qT2", [128, S], BF16, 0)
        kT2 = ov(aw, "kT2", [128, S], BF16, 8192)
        v2p = ov(aw, "v2p", [128, 32, 2, 65], BF16, 16384)
        wo_sb = ov(aw, "wo_sb", [128, 8, Dm], BF16, 24704)
        wq_sl = ov(aw, "wq_sl", [128, 8, 128], BF16, 41088)
        wk_sl = ov(aw, "wk_sl", [128, 8, 128], BF16, 43136)
        wv_sl = ov(aw, "wv_sl", [128, 8, 128], BF16, 45184)
        yT_blk = ov(ay, "yT_blk", [128, 8, BLKT], BF16, 0)
        qsq = ov(ay, "qsq", [128, 2048], F32, 0)
        at_sb = [ov(ay, f"at_sb{i}", [128, 8, 128], BF16, 8192 + 2048 * i)
                 for i in range(2)]
        masks = ov(ay, "masks", [128, 4, 512], BF16, 16384)
        expT = ov(ay, "expT", [128, 12, 512], BF16, 20480)

        ps = ctx.enter_context(nc.psum_tensor("ps", [128, 8, 512], F32))

        for nm in ("sx", "sw0", "sw1", "sxt0", "sxt1", "sro0", "sro1",
                   "sao0", "sao1", "sat0", "sat1", "st", "sv", "ss"):
            k.add_sem(nm, ctx.enter_context(nc.semaphore(nm)))

        T, V, Sc, Sy, G = "tensor", "vector", "scalar", "sync", "gpsimd"

        st8 = {
            "bank_free": [None] * 8,
            "ht_used_t": [0, 0],
            "sg_free_v": [0, 0],
            "wbuf_free_t": [0, 0],
            "wready": {},
            "yf_free": [None, None],
            "xt_free": [None, None],
            "ro_free": [None, None],
            "ao_free": [None, None],
            "at_free": [None, None],
            "scores_free_v": [0, 0],
            "eslot_t": [0] * 12,
            "rrs_free": None,
        }

        pro_dma = 0
        for dst, src in ((idn, idn_c), (onesr, ones_c), (onescol, onescol_c),
                         (cst, consts_c),
                         (graw, d["L0_ffn_gate_weight"]),
                         (gb_sb, d["L0_ffn_gate_bias"].rearrange("(a b) -> a b", a=1))):
            pro_dma = k.op(Sy, lambda E, dst=dst, src=src: E.dma_start(
                out=dst[:], in_=src), inc="sx", n=16)
        k.wait(T, "sx", pro_dma)
        k.wait(Sc, "sx", pro_dma)
        k.wait(V, "sx", pro_dma)
        for kk in range(8):
            bb = kk % 2
            k.waitp(T, st8["bank_free"][bb])
            t_tr = k.op(T, lambda E, kk=kk, bb=bb: E.transpose(
                out=ps[:, bb, 0:32],
                in_=graw[0:32, 128 * kk:128 * (kk + 1)],
                identity=idn[0:32, 0:32]), inc="st")
            k.wait(V, "st", t_tr)
            v_tr = k.op(V, lambda E, kk=kk, bb=bb: E.tensor_copy(
                out=gwT[:, kk, :], in_=ps[:, bb, 0:32]), inc="sv")
            st8["bank_free"][bb] = ("sv", v_tr)
        k.waitp(T, st8["bank_free"][1])
        t_bc = k.op(T, lambda E: E.matmul(out=ps[:, 1, 0:NEXP], lhsT=onesr[:],
                                          rhs=gb_sb[:], start=True, stop=True), inc="st")
        k.wait(V, "st", t_bc)
        k.op(V, lambda E: E.tensor_copy(out=biasbc[:], in_=ps[:, 1, 0:NEXP]), inc="sv")
        v_pro = k.op(V, lambda E: E.memset(gs8[:, 4:8], -BIG), inc="sv")
        st8["bank_free"][0] = ("sv", v_pro)
        st8["bank_free"][1] = ("sv", v_pro)

        def ln_pass(src, ntiles, yT, slot0):
            for i in range(ntiles):
                t = i % 2
                sl = slot0 + i
                rows = 128 * sl
                k.waitp(Sy, st8["xt_free"][t])
                xdma = k.op(Sy, lambda E, t=t, rows=rows, src=src: E.dma_start(
                    out=xt2[t][:], in_=src[rows:rows + 128, :]), inc=f"sxt{t}", n=16)
                k.wait(V, f"sxt{t}", xdma)
                k.op(V, lambda E, t=t, sl=sl: E.tensor_scalar(
                    out=scrA[:], in0=xt2[t][:], scalar1=1.0, scalar2=0.0,
                    op0=ALU.mult, op1=ALU.add, accum_out=sum_sl[:, sl:sl + 1]))
                k.op(V, lambda E, t=t, sl=sl: E.scalar_tensor_tensor(
                    out=scrA[:], in0=xt2[t][:], scalar=1.0, in1=xt2[t][:],
                    op0=ALU.mult, op1=ALU.mult, accum_out=sumsq_sl[:, sl:sl + 1]))
                k.vop(lambda E, sl=sl: E.tensor_scalar(
                    out=mean_sl[:, sl:sl + 1], in0=sum_sl[:, sl:sl + 1],
                    scalar1=INV_D, scalar2=0.0, op0=ALU.mult, op1=ALU.add))
                k.vop(lambda E, sl=sl: E.tensor_scalar(
                    out=var_sl[:, sl:sl + 1], in0=sumsq_sl[:, sl:sl + 1],
                    scalar1=INV_D, scalar2=0.0, op0=ALU.mult, op1=ALU.add))
                k.vop(lambda E, sl=sl: E.tensor_mul(
                    out=st_tmp[:], in0=mean_sl[:, sl:sl + 1], in1=mean_sl[:, sl:sl + 1]))
                v_st = k.vop(lambda E, sl=sl: E.tensor_sub(
                    out=var_sl[:, sl:sl + 1], in0=var_sl[:, sl:sl + 1],
                    in1=st_tmp[:]))
                k.wait(Sc, "sv", v_st)
                s_sq = k.op(Sc, lambda E, sl=sl: E.activation(
                    out=std_sl[:, sl:sl + 1], in_=var_sl[:, sl:sl + 1], func=AF.Sqrt,
                    bias=cst[:, 0:1], scale=1.0), inc="ss")
                k.wait(V, "ss", s_sq)
                k.vop(lambda E, sl=sl: E.reciprocal(
                    out=rstd_sl[:, sl:sl + 1], in_=std_sl[:, sl:sl + 1]))
                v_nb = k.vop(lambda E, sl=sl: E.scalar_tensor_tensor(
                    out=nb_sl[:, sl:sl + 1], in0=mean_sl[:, sl:sl + 1], scalar=-1.0,
                    in1=rstd_sl[:, sl:sl + 1], op0=ALU.mult, op1=ALU.mult))
                k.wait(Sc, "sv", v_nb)
                k.waitp(Sc, st8["yf_free"][t])
                y_s = k.op(Sc, lambda E, t=t, sl=sl: E.activation(
                    out=yf2[t][:], in_=xt2[t][:], func=AF.Identity,
                    bias=nb_sl[:, sl:sl + 1], scale=rstd_sl[:, sl:sl + 1]), inc="ss")
                st8["xt_free"][t] = ("ss", y_s)
                k.wait(T, "ss", y_s)
                last_tr = 0
                for kk in range(8):
                    bb = kk % 2
                    k.waitp(T, st8["bank_free"][bb])
                    t_tr = k.op(T, lambda E, t=t, kk=kk, bb=bb: E.transpose(
                        out=ps[:, bb, 0:128], in_=yf2[t][:, 128 * kk:128 * (kk + 1)],
                        identity=idn[:]), inc="st")
                    last_tr = t_tr
                    k.wait(V, "st", t_tr)
                    v_ev = k.op(V, lambda E, kk=kk, bb=bb, i=i, yT=yT: E.tensor_copy(
                        out=yT[:, kk, 128 * i:128 * (i + 1)], in_=ps[:, bb, 0:128]),
                        inc="sv")
                    st8["bank_free"][bb] = ("sv", v_ev)
                st8["yf_free"][t] = ("st", last_tr)
            return k.ct["sv"]

        def routing_pass(yT, yT_ready):
            sc3 = sc_t[:].rearrange("p (a b) -> p a b", b=2)
            pm3 = pm[:].rearrange("p (a b) -> p a b", b=1)
            g3 = gs8[:, 0:4].rearrange("p (a b) -> p a b", b=1)
            for i in range(NTL):
                t = i % 2
                bb = 2 + i % 2
                k.wait(T, "sv", yT_ready)
                k.waitp(T, st8["bank_free"][bb])
                t_lg = None
                for kk in range(8):
                    t_lg = k.op(T, lambda E, kk=kk, i=i, bb=bb, yT=yT: E.matmul(
                        out=ps[:, bb, 0:NEXP], lhsT=yT[:, kk, 128 * i:128 * (i + 1)],
                        rhs=gwT[:, kk, :], start=(kk == 0), stop=(kk == 7)),
                        inc="st" if kk == 7 else None)
                k.wait(Sc, "st", t_lg)
                k.wait(Sc, "sv", st8["scores_free_v"][t])
                s_sg = k.op(Sc, lambda E, t=t, bb=bb: E.activation(
                    out=scores2[t][:], in_=ps[:, bb, 0:NEXP], func=AF.Sigmoid),
                    inc="ss")
                st8["bank_free"][bb] = ("ss", s_sg)
                k.wait(V, "ss", s_sg)
                k.vop(lambda E, t=t: E.tensor_add(
                    out=sc_t[:], in0=scores2[t][:], in1=biasbc[:]))
                for g in range(4):
                    k.vop(lambda E, g=g: E.max(
                        out=m8g[:, g, :], in_=sc_t[:, 8 * g:8 * (g + 1)]))
                k.vop(lambda E: E.tensor_add(
                    out=g3, in0=m8g[:, :, 0:1], in1=m8g[:, :, 1:2]))
                k.vop(lambda E: E.max(out=m8[:], in_=gs8[:]))
                k.vop(lambda E: E.tensor_scalar(
                    out=pen[:], in0=gs8[:, 0:4], scalar1=m8[:, 1:2], scalar2=-BIG,
                    op0=ALU.is_lt, op1=ALU.mult))
                for g in range(4):
                    k.vop(lambda E, g=g: E.tensor_scalar_add(
                        out=sc_t[:, 8 * g:8 * (g + 1)], in0=sc_t[:, 8 * g:8 * (g + 1)],
                        scalar1=pen[:, g:g + 1]))
                k.vop(lambda E: E.max(out=m8[:], in_=sc_t[:]))
                k.vop(lambda E: E.tensor_single_scalar(
                    out=w6[:], in_=sc_t[:], scalar=m8[:, 5:6], op=ALU.is_ge))
                v_w = k.vop(lambda E, t=t: E.tensor_mul(
                    out=w6[:], in0=w6[:], in1=scores2[t][:]))
                st8["scores_free_v"][t] = v_w
                k.vop(lambda E: E.reduce_sum(out=ssum[:], in_=w6[:], axis=AX.X))
                k.vop(lambda E: E.tensor_scalar_add(
                    out=rec1[:], in0=ssum[:], scalar1=1e-20))
                k.vop(lambda E: E.reciprocal(out=rec2[:], in_=rec1[:]))
                k.vop(lambda E: E.tensor_scalar_mul(
                    out=rec1[:], in0=rec2[:], scalar1=0.2))
                k.vop(lambda E, i=i: E.tensor_scalar_mul(
                    out=cw[:, i, :], in0=w6[:], scalar1=rec1[:]))
                k.vop(lambda E: E.tensor_tensor(
                    out=pm3, in0=sc3[:, :, 0:1], in1=sc3[:, :, 1:2], op=ALU.max))
                k.vop(lambda E, i=i: E.tensor_scalar(
                    out=sel[:, i, :], in0=pm[:], scalar1=m8[:, 0:1], scalar2=0.1,
                    op0=ALU.is_ge, op1=ALU.mult))
            return k.ct["sv"]

        def unit_weight_dma(gu, after_t):
            par = gu % 2
            u = gu % NUNIT
            k.wait(G, "st", after_t)
            if u < NEXP:
                srcs = [(wg_b[par][:], d["L0_ffn_w_gate"][u]),
                        (wu_b[par][:], d["L0_ffn_w_up"][u]),
                        (wd_b[par][:], d["L0_ffn_w_down"][u])]
            elif u == NEXP:
                srcs = [(wg_b[par][:], d["L0_ffn_shared_w_gate"]),
                        (wu_b[par][:], d["L0_ffn_shared_w_up"]),
                        (wd_b[par][:], d["L0_ffn_shared_w_down"])]
            else:
                l = u - NEXP - 1
                srcs = [(wg_b[par][:, :, 0:AH], d["L0_ffn_anchor_w_gate"][l]),
                        (wu_b[par][:, :, 0:AH], d["L0_ffn_anchor_w_up"][l]),
                        (wd_b[par][:, 0:2, :], d["L0_ffn_anchor_w_down"][l])]
            last = 0
            for dst, src in srcs:
                src = src.rearrange("(a p) h -> p a h", p=128)
                last = k.op(G, lambda E, dst=dst, src=src: E.dma_start(
                    out=dst, in_=src), inc=f"sw{par}", n=16)
            st8["wready"][gu] = last

        def gu_quarters(yT, w, par_step, par_u, nq, yT_ready, wrdy):
            htr = 0
            for q in range(nq):
                k.waitp(T, st8["bank_free"][2 * q])
                k.waitp(T, st8["bank_free"][2 * q + 1])
                k.wait(T, f"sw{par_u}", wrdy)
                k.wait(T, "sv", yT_ready)

                def t_gu(E, q=q, w=w, par_u=par_u, yT=yT):
                    for kk in range(8):
                        E.matmul(ps[:, 2 * q, :],
                                 lhsT=wg_b[par_u][:, kk, 128 * q:128 * (q + 1)],
                                 rhs=yT[:, kk, WT * w:WT * (w + 1)],
                                 start=(kk == 0), stop=(kk == 7))
                    ins = None
                    for kk in range(8):
                        ins = E.matmul(ps[:, 2 * q + 1, :],
                                       lhsT=wu_b[par_u][:, kk, 128 * q:128 * (q + 1)],
                                       rhs=yT[:, kk, WT * w:WT * (w + 1)],
                                       start=(kk == 0), stop=(kk == 7))
                    return ins
                t_q = k.op(T, t_gu, inc="st")
                k.wait(Sc, "st", t_q)
                k.wait(Sc, "sv", st8["sg_free_v"][q % 2])
                s_q = k.op(Sc, lambda E, q=q: E.activation(
                    out=sg2[q % 2][:], in_=ps[:, 2 * q, :], func=AF.Silu), inc="ss")
                st8["bank_free"][2 * q] = ("ss", s_q)
                k.wait(V, "ss", s_q)
                k.wait(V, "st", max(t_q, st8["ht_used_t"][par_step]))
                v_q = k.op(V, lambda E, q=q, par_step=par_step: E.tensor_mul(
                    out=ht2[par_step][:, q, :], in0=sg2[q % 2][:],
                    in1=ps[:, 2 * q + 1, :]), inc="sv")
                htr = v_q
                st8["bank_free"][2 * q + 1] = ("sv", v_q)
                st8["sg_free_v"][q % 2] = v_q
            return htr

        def down_unit(w, par_step, par_u, nk, ht_ready, cw_src):
            last_t = 0
            for r in range(8):
                tt, j = r // 2, r % 2
                k.waitp(T, st8["bank_free"][r])
                k.wait(T, "sv", ht_ready)

                def t_dn(E, r=r, tt=tt, j=j, par_step=par_step, par_u=par_u, nk=nk):
                    ins = None
                    for kk in range(nk):
                        ins = E.matmul(ps[:, r, :],
                                       lhsT=ht2[par_step][:, kk, 128 * tt:128 * (tt + 1)],
                                       rhs=wd_b[par_u][:, kk, WT * j:WT * (j + 1)],
                                       start=(kk == 0), stop=(kk == nk - 1))
                    return ins
                t_r = k.op(T, t_dn, inc="st")
                last_t = t_r
                k.wait(V, "st", t_r)
                tile = 4 * w + tt
                sc_ap = cw_src(tile)
                v_r = k.op(V, lambda E, r=r, tile=tile, j=j, sc_ap=sc_ap:
                           E.scalar_tensor_tensor(
                               out=acc[:, tile, WT * j:WT * (j + 1)], in0=ps[:, r, :],
                               scalar=sc_ap, in1=acc[:, tile, WT * j:WT * (j + 1)],
                               op0=ALU.mult, op1=ALU.add), inc="sv")
                st8["bank_free"][r] = ("sv", v_r)
            return last_t

        def cw_source(u):
            if u < NEXP:
                return lambda tile, u=u: cw[:, tile, u:u + 1]
            if u == NEXP:
                return lambda tile: 1.0
            l = u - NEXP - 1
            return lambda tile, l=l: sel[:, tile, l:l + 1]

        def ffn_phase(src, dst, gu0):
            gu = gu0
            for blk in range(NBLK):
                base = blk * BLKT
                yT_ready = ln_pass(src, NTL, yT_blk, slot0=blk * NTL)
                routing_pass(yT_blk, yT_ready)
                k.op(V, lambda E: E.memset(acc[:], 0.0), inc="sv")
                prev = None
                step = 0
                for u in range(NUNIT):
                    nq = 4 if u <= NEXP else 2
                    nk = 4 if u <= NEXP else 2
                    par_u = gu % 2
                    if gu == gu0 and blk == 0:
                        unit_weight_dma(gu, after_t=st8["wbuf_free_t"][gu % 2])
                        unit_weight_dma(gu + 1, after_t=st8["wbuf_free_t"][(gu + 1) % 2])
                    wrdy = st8["wready"][gu]
                    for w in range(NWIN):
                        par_step = step % 2
                        htr = gu_quarters(yT_blk, w, par_step, par_u, nq, yT_ready, wrdy)
                        if prev is not None:
                            lt = down_unit(*prev)
                            st8["ht_used_t"][prev[1]] = lt
                            if prev[0] == NWIN - 1:
                                st8["wbuf_free_t"][(gu - 1) % 2] = lt
                                nxt = gu + 1
                                if nxt < gu0 + NBLK * NUNIT and nxt not in st8["wready"]:
                                    unit_weight_dma(nxt, after_t=st8["wbuf_free_t"][nxt % 2])
                        prev = (w, par_step, par_u, nk, htr, cw_source(u))
                        step += 1
                    gu += 1
                lt = down_unit(*prev)
                st8["ht_used_t"][prev[1]] = lt
                st8["wbuf_free_t"][(gu - 1) % 2] = lt
                nxt = gu + 1
                if nxt < gu0 + NBLK * NUNIT and nxt not in st8["wready"]:
                    unit_weight_dma(nxt, after_t=st8["wbuf_free_t"][nxt % 2])
                prev = None
                k.wait(V, "sv", k.ct["sv"])
                for i in range(NTL):
                    t = i % 2
                    rows = base + 128 * i
                    k.waitp(Sy, st8["xt_free"][t])
                    xdma = k.op(Sy, lambda E, t=t, rows=rows, src=src: E.dma_start(
                        out=xt2[t][:], in_=src[rows:rows + 128, :]), inc=f"sxt{t}", n=16)
                    k.wait(V, f"sxt{t}", xdma)
                    k.waitp(V, st8["ro_free"][t])
                    v_e = k.op(V, lambda E, t=t, i=i: E.scalar_tensor_tensor(
                        out=ro2[t][:], in0=acc[:, i, :], scalar=0.5, in1=xt2[t][:],
                        op0=ALU.mult, op1=ALU.add), inc="sv")
                    st8["xt_free"][t] = ("sv", v_e)
                    k.wait(Sy, "sv", v_e)
                    odma = k.op(Sy, lambda E, t=t, rows=rows, dst=dst: E.dma_start(
                        out=dst[rows:rows + 128, :], in_=ro2[t][:]), inc=f"sro{t}", n=16)
                    st8["ro_free"][t] = (f"sro{t}", odma)
            return gu

        def headnorm_sq(srcT, pb, dst01):
            t_last_prev = st8.get("qsq_free_t", 0)
            for half in range(2):
                colsl = slice(2048 * half, 2048 * (half + 1))
                k.wait(V, "st", t_last_prev)
                v_sq = k.op(V, lambda E, colsl=colsl, pb=pb, srcT=srcT: E.tensor_mul(
                    out=qsq[pb:pb + 64, :], in0=srcT[pb:pb + 64, colsl],
                    in1=srcT[pb:pb + 64, colsl]), inc="sv")
                k.wait(T, "sv", v_sq)
                t_n = 0
                for c in range(4):
                    k.waitp(T, st8["bank_free"][4 + c])
                    t_n = k.op(T, lambda E, c=c, pb=pb: E.matmul(
                        out=ps[0:1, 4 + c, :], lhsT=onescol[pb:pb + 64, :],
                        rhs=qsq[pb:pb + 64, WT * c:WT * (c + 1)],
                        start=True, stop=True), inc="st")
                t_last_prev = t_n
                k.wait(V, "st", t_n)
                tgt = st_tmp[0:1, 0:1] if half == 0 else dst01[:]
                v_red = k.vop(lambda E, tgt=tgt: E.tensor_reduce(
                    out=tgt, in_=ps[0:1, 4:8, :], axis=AX.XY, op=ALU.max))
                for c in range(4):
                    st8["bank_free"][4 + c] = ("sv", v_red)
            st8["qsq_free_t"] = t_last_prev
            k.vop(lambda E, dst01=dst01: E.tensor_tensor(
                out=dst01[:], in0=dst01[:], in1=st_tmp[0:1, 0:1], op=ALU.max))

        def attention_phase():
            mask_dma = k.op(Sy, lambda E: E.dma_start(out=masks[:], in_=masks_c),
                            inc="sx", n=16)
            k.wait(V, "sx", mask_dma)
            yT_ready = ln_pass(dr1, 32, yT_full, slot0=0)
            v_ones = k.op(V, lambda E: E.memset(v2p[:, :, :, 64:65], 1.0), inc="sv")
            last_qkv_t = 0
            for hp in range(8):
                k.wait(G, "st", last_qkv_t)
                wsl_last = 0
                for dstw, nm in ((wq_sl, "L0_attn_Wq"), (wk_sl, "L0_attn_Wk"),
                                 (wv_sl, "L0_attn_Wv")):
                    srcw = d[nm][:, 128 * hp:128 * (hp + 1)].rearrange(
                        "(a p) c -> p a c", p=128)
                    wsl_last = k.op(G, lambda E, dstw=dstw, srcw=srcw: E.dma_start(
                        out=dstw[:], in_=srcw), inc="sw0", n=16)
                for wsl, dstt, scale in ((wq_sl, qT2, 0.125), (wk_sl, kT2, 1.0)):
                    for w8 in range(8):
                        bb = 6 + w8 % 2
                        k.waitp(T, st8["bank_free"][bb])
                        k.wait(T, "sw0", wsl_last)
                        k.wait(T, "sv", yT_ready)

                        def t_q(E, wsl=wsl, w8=w8, bb=bb):
                            ins = None
                            for kk in range(8):
                                ins = E.matmul(ps[:, bb, :], lhsT=wsl[:, kk, :],
                                               rhs=yT_full[:, kk, WT * w8:WT * (w8 + 1)],
                                               start=(kk == 0), stop=(kk == 7))
                            return ins
                        t_w = k.op(T, t_q, inc="st")
                        k.wait(Sc, "st", t_w)
                        scap = cst[:, 1:2] if scale == 0.125 else 1.0
                        s_w = k.op(Sc, lambda E, dstt=dstt, w8=w8, bb=bb, scap=scap:
                                   E.activation(
                                       out=dstt[:, WT * w8:WT * (w8 + 1)],
                                       in_=ps[:, bb, :], func=AF.Identity,
                                       scale=scap), inc="ss")
                        st8["bank_free"][bb] = ("ss", s_w)
                for tt in range(32):
                    bb = 6 + tt % 2
                    k.waitp(T, st8["bank_free"][bb])
                    k.wait(T, "sw0", wsl_last)
                    k.wait(T, "sv", v_ones)

                    def t_v(E, tt=tt, bb=bb):
                        ins = None
                        for kk in range(8):
                            ins = E.matmul(ps[:, bb, 0:128],
                                           lhsT=yT_full[:, kk, 128 * tt:128 * (tt + 1)],
                                           rhs=wv_sl[:, kk, :],
                                           start=(kk == 0), stop=(kk == 7))
                        return ins
                    t_w = k.op(T, t_v, inc="st")
                    k.wait(V, "st", t_w)
                    v_w = k.op(V, lambda E, tt=tt, bb=bb: E.tensor_copy(
                        out=v2p[:, tt, :, 0:64],
                        in_=ps[:, bb, 0:128].rearrange("p (h c) -> p h c", h=2)),
                        inc="sv")
                    st8["bank_free"][bb] = ("sv", v_w)
                qkv_done_v = k.ct["sv"]
                last_qkv_t = k.ct["st"]
                k.wait(V, "ss", k.ct["ss"])
                headnorm_sq(qT2, 0, csq)
                headnorm_sq(kT2, 0, csc)
                headnorm_sq(qT2, 64, csq2)
                headnorm_sq(kT2, 64, csc2)
                k.vop(lambda E: E.tensor_mul(out=csq[:], in0=csq[:], in1=csc[:]))
                k.vop(lambda E: E.tensor_mul(out=csq2[:], in0=csq2[:], in1=csc2[:]))
                v_c = k.vop(lambda E: E.tensor_tensor(
                    out=csq[:], in0=csq[:], in1=csq2[:], op=ALU.max))
                k.wait(Sc, "sv", v_c)
                s_ln = k.sop(lambda E: E.activation(
                    out=csc[:], in_=csq[:], func=AF.Ln))
                s_c = k.sop(lambda E: E.activation(
                    out=csq[:], in_=csc[:], func=AF.Exp, scale=cst[0:1, 2:3]))
                k.wait(V, "ss", s_c)
                v_c2 = k.vop(lambda E: E.tensor_scalar(
                    out=csc[:], in0=csq[:], scalar1=-1.0, scalar2=-0.1,
                    op0=ALU.mult, op1=ALU.add))
                k.wait(T, "sv", v_c2)
                k.waitp(T, st8["bank_free"][6])
                t_nc = k.op(T, lambda E: E.matmul(
                    out=ps[:, 6, 0:1], lhsT=onesr[:], rhs=csc[:],
                    start=True, stop=True), inc="st")
                k.wait(V, "st", t_nc)
                v_nc = k.vop(lambda E: E.tensor_copy(
                    out=negc2[0][:], in_=ps[:, 6, 0:1]))
                st8["bank_free"][6] = ("sv", v_nc)
                negc_v = k.ct["sv"]
                def normalize_store(w, t_fl, hp=hp):
                    for h01 in range(2):
                        head = 2 * hp + h01
                        ob = 4 + h01
                        k.wait(Sc, "st", t_fl)
                        k.waitp(Sc, st8["rrs_free"])
                        k.sop(lambda E, ob=ob: E.activation(
                            out=rscr[:], in_=ps[64:65, ob, :], func=AF.Ln))
                        s_rr = k.sop(lambda E: E.activation(
                            out=rrs[:], in_=rscr[:], func=AF.Exp, scale=cst[0:1, 3:4]))
                        k.wait(T, "ss", s_rr)
                        k.waitp(T, st8["bank_free"][6])
                        t_b2 = k.op(T, lambda E: E.matmul(
                            out=ps[0:64, 6, :], lhsT=onesr[0:1, 0:64], rhs=rrs[:],
                            start=True, stop=True), inc="st")
                        st8["rrs_free"] = ("st", t_b2)
                        k.wait(V, "st", t_b2)
                        v_rb = k.vop(lambda E: E.tensor_copy(
                            out=rb_sb[:], in_=ps[0:64, 6, :]))
                        st8["bank_free"][6] = ("sv", v_rb)
                        tb = h01
                        k.waitp(V, st8["ao_free"][tb])
                        v_ao = k.op(V, lambda E, tb=tb, ob=ob: E.tensor_mul(
                            out=ao2[tb][:], in0=ps[0:64, ob, :], in1=rb_sb[:]),
                            inc="sv")
                        st8["bank_free"][ob] = ("sv", v_ao)
                        k.wait(Sy, "sv", v_ao)
                        pr0 = 64 * (head % 2)
                        dst_ap = dat[4 * w:4 * (w + 1), pr0:pr0 + 64, head // 2, :]\
                            .rearrange("t r c -> r t c")
                        src_ap = ao2[tb][:].rearrange("r (t c) -> r t c", c=128)
                        o_dma = k.op(Sy, lambda E, dst_ap=dst_ap, src_ap=src_ap:
                                     E.dma_start(out=dst_ap, in_=src_ap),
                                     inc=f"sao{tb}", n=16)
                        st8["ao_free"][tb] = (f"sao{tb}", o_dma)
                for w in range(8):
                    nkt = 4 * w + 4
                    nb2 = nkt // 2
                    exp_s = [[0] * nb2, [0] * nb2]
                    msk_v = [[0] * nkt, [0] * nkt]

                    def slot(h01, b2, kt):
                        return 6 * h01 + 2 * (b2 % 3) + kt % 2

                    def emit_scores(h01, b2, w=w):
                        pb = 64 * h01
                        for kt in (2 * b2, 2 * b2 + 1):
                            bb = 2 * h01 + kt % 2
                            if b2 >= 1:
                                k.wait(T, "ss", exp_s[h01][b2 - 1])
                            else:
                                k.waitp(T, st8["bank_free"][bb])
                            k.wait(T, "sv", qkv_done_v)
                            k.op(T, lambda E, kt=kt, bb=bb, pb=pb, w=w: E.matmul(
                                ps[:, bb, :],
                                lhsT=kT2[pb:pb + 64, 128 * kt:128 * (kt + 1)],
                                rhs=qT2[pb:pb + 64, WT * w:WT * (w + 1)],
                                start=True, stop=True), inc="st")
                        return k.ct["st"]

                    def emit_exp(h01, b2, sc_done, w=w):
                        bb0 = 2 * h01
                        sl0 = slot(h01, b2, 0)
                        k.wait(Sc, "st", sc_done)
                        k.wait(Sc, "sv", negc_v)
                        k.wait(Sc, "st", max(st8["eslot_t"][sl0],
                                             st8["eslot_t"][sl0 + 1]))
                        s_e = k.op(Sc, lambda E, bb0=bb0, sl0=sl0: E.activation(
                            out=expT[:, sl0:sl0 + 2, :], in_=ps[:, bb0:bb0 + 2, :],
                            func=AF.Exp, bias=negc2[0][:], scale=1.0), inc="ss")
                        exp_s[h01][b2] = s_e
                        st8["bank_free"][bb0] = ("ss", s_e)
                        st8["bank_free"][bb0 + 1] = ("ss", s_e)
                        for kt in (2 * b2, 2 * b2 + 1):
                            o = 128 * kt - 512 * w
                            if o >= 0:
                                mi = o // 128
                                sl = slot(h01, b2, kt)
                                k.wait(V, "ss", s_e)
                                msk_v[h01][kt] = k.op(V, lambda E, sl=sl, mi=mi:
                                                      E.tensor_mul(
                                                          out=expT[:, sl, :],
                                                          in0=expT[:, sl, :],
                                                          in1=masks[:, mi, :]), inc="sv")

                    def emit_attnv(b2, w=w, nkt=nkt):
                        for h01 in range(2):
                            ob = 4 + h01
                            for kt in (2 * b2, 2 * b2 + 1):
                                sl = slot(h01, b2, kt)
                                k.wait(T, "ss", exp_s[h01][b2])
                                if msk_v[h01][kt]:
                                    k.wait(T, "sv", msk_v[h01][kt])
                                if kt == 0:
                                    k.waitp(T, st8["bank_free"][ob])
                                t_a = k.op(T, lambda E, kt=kt, h01=h01, nkt=nkt, ob=ob, sl=sl:
                                           E.matmul(
                                               ps[0:65, ob, :], lhsT=v2p[:, kt, h01, :],
                                               rhs=expT[:, sl, :],
                                               start=(kt == 0), stop=(kt == nkt - 1)),
                                           inc="st")
                                st8["eslot_t"][sl] = t_a

                    for b2 in range(nb2):
                        scd0 = emit_scores(0, b2)
                        scd1 = emit_scores(1, b2)
                        if b2 >= 2:
                            emit_attnv(b2 - 2)
                        emit_exp(0, b2, scd0)
                        emit_exp(1, b2, scd1)
                        if b2 == 1 and st8.get("pend_norm"):
                            st8["pend_norm"]()
                            st8["pend_norm"] = None
                    if st8.get("pend_norm"):
                        st8["pend_norm"]()
                        st8["pend_norm"] = None
                    for b2 in range(max(nb2 - 2, 0), nb2):
                        emit_attnv(b2)
                    t_fl = k.ct["st"]

                    def norm_emit(w=w, t_fl=t_fl):
                        normalize_store(w, t_fl)
                    if w == 7:
                        norm_emit()
                    else:
                        st8["pend_norm"] = norm_emit

            k.wait(G, "st", k.ct["st"])
            wo_last = k.op(G, lambda E: E.dma_start(
                out=wo_sb[:],
                in_=d["L0_attn_Wo"].rearrange("(a p) h -> p a h", p=128)),
                inc="sw1", n=16)
            r2dst = dout if debug_stop == "B" else dr2
            for i in range(32):
                t = i % 2
                rows = 128 * i
                k.wait(Sy, "sao0", k.ct["sao0"])
                k.wait(Sy, "sao1", k.ct["sao1"])
                k.waitp(Sy, st8["at_free"][t])
                a_dma = k.op(Sy, lambda E, t=t, i=i: E.dma_start(
                    out=at_sb[t][:], in_=dat[i]), inc=f"sat{t}", n=16)
                k.waitp(Sy, st8["xt_free"][t])
                x_dma = k.op(Sy, lambda E, t=t, rows=rows: E.dma_start(
                    out=xt2[t][:], in_=dr1[rows:rows + 128, :]), inc=f"sxt{t}", n=16)
                t_half = [0, 0]
                for j in range(2):
                    bb = 6 + j
                    k.waitp(T, st8["bank_free"][bb])
                    k.wait(T, f"sat{t}", a_dma)
                    k.wait(T, "sw1", wo_last)

                    def t_wo(E, t=t, j=j, bb=bb):
                        ins = None
                        for kk in range(8):
                            ins = E.matmul(ps[:, bb, :], lhsT=at_sb[t][:, kk, :],
                                           rhs=wo_sb[:, kk, WT * j:WT * (j + 1)],
                                           start=(kk == 0), stop=(kk == 7))
                        return ins
                    t_half[j] = k.op(T, t_wo, inc="st")
                st8["at_free"][t] = ("st", t_half[1])
                k.wait(Sc, f"sxt{t}", x_dma)
                k.waitp(Sc, st8["yf_free"][t])
                s_y = k.op(Sc, lambda E, t=t, i=i: E.activation(
                    out=yf2[t][:], in_=xt2[t][:], func=AF.Identity,
                    bias=nb_sl[:, i:i + 1], scale=rstd_sl[:, i:i + 1]), inc="ss")
                st8["xt_free"][t] = ("ss", s_y)
                k.wait(V, "ss", s_y)
                v_t1 = k.op(V, lambda E, t=t: E.tensor_add(
                    out=scrA[:], in0=yf2[t][:], in1=xt2[t][:]), inc="sv")
                st8["yf_free"][t] = ("sv", v_t1)
                k.wait(V, "sv", v_t1)
                k.waitp(V, st8["ro_free"][t])
                v_r = 0
                for j in range(2):
                    k.wait(V, "st", t_half[j])
                    v_r = k.op(V, lambda E, t=t, j=j: E.tensor_add(
                        out=ro2[t][:, WT * j:WT * (j + 1)],
                        in0=scrA[:, WT * j:WT * (j + 1)], in1=ps[:, 6 + j, :]),
                        inc="sv")
                    st8["bank_free"][6 + j] = ("sv", v_r)
                k.wait(Sy, "sv", v_r)
                o_dma = k.op(Sy, lambda E, t=t, rows=rows, r2dst=r2dst: E.dma_start(
                    out=r2dst[rows:rows + 128, :], in_=ro2[t][:]), inc=f"sro{t}", n=16)
                st8["ro_free"][t] = (f"sro{t}", o_dma)

        if debug_stop == "R":
            yT_ready = ln_pass(d["x"], NTL, yT_blk, slot0=0)
            rv = routing_pass(yT_blk, yT_ready)
            k.wait(V, "sv", rv)
            k.op(V, lambda E: E.tensor_copy(
                out=scrA[:, 0:512], in_=cw[:].rearrange("p a b -> p (a b)")))
            k.op(V, lambda E: E.tensor_copy(
                out=scrA[:, 512:768], in_=sel[:].rearrange("p a b -> p (a b)")))
            k.op(V, lambda E: E.tensor_copy(out=scrA[:, 768:784], in_=rstd_sl[:, 0:16]))
            k.op(V, lambda E: E.tensor_copy(out=scrA[:, 800:816], in_=nb_sl[:, 0:16]))
            k.op(V, lambda E: E.tensor_copy(out=scrA[:, 832:848], in_=sum_sl[:, 0:16]))
            k.op(V, lambda E: E.tensor_copy(out=scrA[:, 896:912], in_=mean_sl[:, 0:16]))
            k.op(V, lambda E: E.tensor_copy(out=scrA[:, 912:928], in_=var_sl[:, 0:16]))
            k.op(V, lambda E: E.tensor_copy(out=scrA[:, 928:944], in_=std_sl[:, 0:16]))
            v_d = k.op(V, lambda E: E.tensor_copy(out=scrA[:, 864:880],
                                                  in_=sumsq_sl[:, 0:16]), inc="sv")
            k.wait(Sy, "sv", v_d)
            k.op(Sy, lambda E: E.dma_start(out=dout[0:128, :], in_=scrA[:]),
                 inc="sx", n=16)
            for kk in range(8):
                k.op(V, lambda E, kk=kk: E.tensor_copy(
                    out=ro2[0][:, 128 * kk:128 * (kk + 1)], in_=yT_blk[:, kk, 0:128]))
            v_y = k.op(V, lambda E: E.engine_nop(), inc="sv")
            k.wait(Sy, "sv", v_y)
            k.op(Sy, lambda E: E.dma_start(out=dout[128:256, :], in_=ro2[0][:]),
                 inc="sx", n=16)
            for s in k.sems:
                k.wait(Sy, s, k.ct[s])
            with nc.Block() as block:
                for eng, name in ((block.sync, "sync"), (block.gpsimd, "gpsimd"),
                                  (block.vector, "vector"), (block.scalar, "scalar"),
                                  (block.tensor, "tensor")):
                    def mk(name):
                        def f(E):
                            for fn in k.q[name]:
                                fn(E)
                        return f
                    eng(mk(name))
            return nc
        gu = ffn_phase(d["x"], dout if debug_stop == "A" else dr1, 0)
        if debug_stop != "A":
            k.full_barrier()
            attention_phase()
            if debug_stop != "B":
                k.full_barrier()
                ffn_phase(dr2, dout, gu)
        for s in k.sems:
            k.wait(Sy, s, k.ct[s])

        with nc.Block() as block:
            @block.sync
            def _(E):
                for fn in k.q["sync"]:
                    fn(E)

            @block.gpsimd
            def _(E):
                for fn in k.q["gpsimd"]:
                    fn(E)

            @block.vector
            def _(E):
                for fn in k.q["vector"]:
                    fn(E)

            @block.scalar
            def _(E):
                for fn in k.q["scalar"]:
                    fn(E)

            @block.tensor
            def _(E):
                for fn in k.q["tensor"]:
                    fn(E)

    return nc


def kernel(**inputs):
    x = np.ascontiguousarray(np.asarray(inputs["L0_attn_x"], dtype=np.float32))
    base = {n: np.ascontiguousarray(np.asarray(inputs[n], dtype=np.float32))
            for n in WNAMES}
    nc = build_program()
    in_maps = []
    for c in range(8):
        m = dict(base)
        m["L0_attn_x"] = np.ascontiguousarray(x[c])
        in_maps.append(m)
    res = run_bass_kernel_spmd(nc, in_maps, core_ids=list(range(8)))
    return np.stack([r["out"] for r in res.results], axis=0)


# version 10
# speedup vs baseline: 1.1015x; 1.0305x over previous
import numpy as np
import ml_dtypes
from contextlib import ExitStack

import concourse.bass as bass
import concourse.mybir as mybir
from concourse.bass_utils import run_bass_kernel_spmd

F32 = mybir.dt.float32
BF16 = mybir.dt.bfloat16
U8 = mybir.dt.uint8
AF = mybir.ActivationFunctionType
ALU = mybir.AluOpType
AX = mybir.AxisListType

S, Dm = 4096, 1024
NBLK, BLKT, NWIN, WT, NTL = 2, 2048, 4, 512, 16
NEXP, EH = 32, 512
NANC, AH = 16, 256
NHD, DK = 16, 64
NUNIT = NEXP + 1 + NANC
EPS = 1e-5
BIG = 1e30
INV_D = 1.0 / Dm

WNAMES = [
    "L0_attn_Wq", "L0_attn_Wk", "L0_attn_Wv", "L0_attn_Wo",
    "L0_ffn_gate_weight", "L0_ffn_gate_bias",
    "L0_ffn_w_gate", "L0_ffn_w_up", "L0_ffn_w_down",
    "L0_ffn_shared_w_gate", "L0_ffn_shared_w_up", "L0_ffn_shared_w_down",
    "L0_ffn_anchor_w_gate", "L0_ffn_anchor_w_up", "L0_ffn_anchor_w_down",
]

ENGS = ("tensor", "vector", "scalar", "sync", "gpsimd")


class Sched:
    def __init__(self):
        self.q = {e: [] for e in ENGS}
        self.sems = {}
        self.ct = {}
        self.last_wait = {}
        self.trace = {e: [] for e in ENGS}

    def add_sem(self, name, handle):
        self.sems[name] = handle
        self.ct[name] = 0

    def wait(self, eng, sem, val):
        if not val or val <= 0:
            return
        key = (eng, sem)
        if self.last_wait.get(key, 0) >= val:
            return
        self.last_wait[key] = val
        h = self.sems[sem]
        self.trace[eng].append(("wait", sem, val))
        self.q[eng].append(lambda E, h=h, v=val: E.wait_ge(h, v))

    def waitp(self, eng, pair):
        if pair is not None:
            self.wait(eng, pair[0], pair[1])

    def op(self, eng, fn, inc=None, n=1):
        if inc is None:
            self.q[eng].append(fn)
            self.trace[eng].append(("op", None, 0))
            return None
        h = self.sems[inc]

        def g(E, fn=fn, h=h, n=n):
            fn(E).then_inc(h, n)

        self.q[eng].append(g)
        self.ct[inc] += n
        self.trace[eng].append(("inc", inc, n))
        return self.ct[inc]

    def vop(self, fn):
        v = self.op("vector", fn, inc="sv")
        self.wait("vector", "sv", v)
        return v

    def sop(self, fn):
        v = self.op("scalar", fn, inc="ss")
        self.wait("scalar", "ss", v)
        return v

    def full_barrier(self):
        for e in ENGS:
            for s in self.sems:
                self.wait(e, s, self.ct[s])


def build_program(debug_stop=None):
    nc = bass.Bass("TRN2", target_bir_lowering=False)
    k = Sched()

    d = {}
    d["x"] = nc.dram_tensor("L0_attn_x", [S, Dm], F32, kind="ExternalInput").ap()
    shapes = {
        "L0_attn_Wq": [Dm, Dm], "L0_attn_Wk": [Dm, Dm], "L0_attn_Wv": [Dm, Dm],
        "L0_attn_Wo": [Dm, Dm],
        "L0_ffn_gate_weight": [NEXP, Dm], "L0_ffn_gate_bias": [NEXP],
        "L0_ffn_w_gate": [NEXP, Dm, EH], "L0_ffn_w_up": [NEXP, Dm, EH],
        "L0_ffn_w_down": [NEXP, EH, Dm],
        "L0_ffn_shared_w_gate": [Dm, EH], "L0_ffn_shared_w_up": [Dm, EH],
        "L0_ffn_shared_w_down": [EH, Dm],
        "L0_ffn_anchor_w_gate": [NANC, Dm, AH], "L0_ffn_anchor_w_up": [NANC, Dm, AH],
        "L0_ffn_anchor_w_down": [NANC, AH, Dm],
    }
    for n, sh in shapes.items():
        d[n] = nc.dram_tensor(n, sh, F32, kind="ExternalInput").ap()
    dout = nc.dram_tensor("out", [S, Dm], F32, kind="ExternalOutput").ap()
    dr1 = nc.dram_tensor("r1_scr", [S, Dm], F32, kind="Internal").ap()
    dr2 = nc.dram_tensor("r2_scr", [S, Dm], F32, kind="Internal").ap()
    dat = nc.dram_tensor("attnT_scr", [32, 128, 8, 128], BF16, kind="Internal").ap()

    idn_c = nc.inline_tensor(np.eye(128, dtype=np.float32), name="idn_c").ap()
    ones_c = nc.inline_tensor(np.ones((1, 128), dtype=np.float32), name="ones_c").ap()
    onescol_c = nc.inline_tensor(np.ones((128, 1), dtype=np.float32), name="onescol_c").ap()
    consts_c = nc.inline_tensor(
        np.tile(np.array([[EPS, 0.125, 0.5, -1.0]], dtype=np.float32), (128, 1)),
        name="consts_c").ap()
    mnp = np.zeros((128, 4, 512), dtype=ml_dtypes.bfloat16)
    for m in range(4):
        for p in range(128):
            mnp[p, m, 128 * m + p:] = 1.0
    masks_c = nc.inline_tensor(mnp, name="masks_c").ap()

    with ExitStack() as ctx:
        def sbuf(name, shape, dt=F32):
            return ctx.enter_context(nc.sbuf_tensor(name, shape, dt))

        idn = sbuf("idn", [128, 128])
        onesr = sbuf("onesr", [1, 128])
        onescol = sbuf("onescol", [128, 1])
        cst = sbuf("cst", [128, 4])
        gb_sb = sbuf("gb_sb", [1, NEXP])
        gwT = sbuf("gwT", [128, 8, NEXP], BF16)
        biasbc = sbuf("biasbc", [128, NEXP])
        sum_sl = sbuf("sum_sl", [128, 32]); sumsq_sl = sbuf("sumsq_sl", [128, 32])
        mean_sl = sbuf("mean_sl", [128, 32]); var_sl = sbuf("var_sl", [128, 32])
        std_sl = sbuf("std_sl", [128, 32]); rstd_sl = sbuf("rstd_sl", [128, 32])
        nb_sl = sbuf("nb_sl", [128, 32]); st_tmp = sbuf("st_tmp", [128, 1])
        xt2 = [sbuf(f"xt{i}", [128, Dm]) for i in range(2)]
        yf2 = [sbuf(f"yf{i}", [128, Dm]) for i in range(2)]
        scrA = sbuf("scrA", [128, Dm])
        ro2 = [sbuf(f"ro{i}", [128, Dm]) for i in range(2)]
        scores2 = [sbuf(f"scores{i}", [128, NEXP]) for i in range(2)]
        sc_t = sbuf("sc_t", [128, NEXP])
        m8g = sbuf("m8g", [128, 4, 8]); gs8 = sbuf("gs8", [128, 8])
        pen = sbuf("pen", [128, 4]); m8 = sbuf("m8", [128, 8])
        w6 = sbuf("w6", [128, NEXP]); pm = sbuf("pm", [128, 16])
        ssum = sbuf("ssum", [128, 1]); rec1 = sbuf("rec1", [128, 1]); rec2 = sbuf("rec2", [128, 1])
        cw = sbuf("cw", [128, NTL, NEXP])
        sel = sbuf("sel", [128, NTL, NANC])
        ht2 = [sbuf(f"ht{i}", [128, 4, 512], BF16) for i in range(2)]
        sg2 = [sbuf(f"sg{i}", [128, 512], BF16) for i in range(2)]
        csq = sbuf("csq", [1, 1]); csc = sbuf("csc", [1, 1])
        csq2 = sbuf("csq2", [1, 1]); csc2 = sbuf("csc2", [1, 1])
        negc2 = [sbuf(f"negc{i}", [128, 1]) for i in range(2)]
        rrs = sbuf("rrs", [1, 512]); rscr = sbuf("rscr", [1, 512])
        rb_sb = sbuf("rb_sb", [64, 512])
        ao2 = [sbuf(f"ao{i}", [64, 512], BF16) for i in range(2)]

        arena_big = sbuf("arena_big", [128, 65536], U8)
        arena_w = sbuf("arena_w", [128, 49152], U8)
        arena_y = sbuf("arena_y", [128, 32768], U8)
        ab = nc.lookup_mloc(arena_big).addr
        aw = nc.lookup_mloc(arena_w).addr
        ay = nc.lookup_mloc(arena_y).addr

        def ov(base, name, shape, dt, off):
            return nc.alloc_sbuf_tensor_at(name, shape, dt, offset=base + off)

        acc = ov(ab, "acc", [128, NTL, Dm], F32, 0)
        yT_full = ov(ab, "yT_full", [128, 8, S], BF16, 0)
        graw = ov(ab, "graw", [NEXP, Dm], F32, 0)
        wg_b = [ov(aw, f"wg_b{i}", [128, 8, 512], BF16, 24576 * i) for i in range(2)]
        wu_b = [ov(aw, f"wu_b{i}", [128, 8, 512], BF16, 24576 * i + 8192) for i in range(2)]
        wd_b = [ov(aw, f"wd_b{i}", [128, 4, 1024], BF16, 24576 * i + 16384) for i in range(2)]
        qT2 = ov(aw, "qT2", [128, S], BF16, 0)
        kT2 = ov(aw, "kT2", [128, S], BF16, 8192)
        v2p = ov(aw, "v2p", [128, 32, 2, 65], BF16, 16384)
        wo_sb = ov(aw, "wo_sb", [128, 8, Dm], BF16, 24704)
        wq_sl = ov(aw, "wq_sl", [128, 8, 128], BF16, 41088)
        wk_sl = ov(aw, "wk_sl", [128, 8, 128], BF16, 43136)
        wv_sl = ov(aw, "wv_sl", [128, 8, 128], BF16, 45184)
        yT_blk = ov(ay, "yT_blk", [128, 8, BLKT], BF16, 0)
        qsq = ov(ay, "qsq", [128, 2048], F32, 0)
        at_sb = [ov(ay, f"at_sb{i}", [128, 8, 128], BF16, 8192 + 2048 * i)
                 for i in range(2)]
        masks = ov(ay, "masks", [128, 4, 512], BF16, 16384)
        expT = ov(ay, "expT", [128, 12, 512], BF16, 20480)

        ps = ctx.enter_context(nc.psum_tensor("ps", [128, 8, 512], F32))

        for nm in ("sx", "sw0", "sw1", "sxt0", "sxt1", "sro0", "sro1",
                   "sao0", "sao1", "sat0", "sat1", "st", "sv", "ss"):
            k.add_sem(nm, ctx.enter_context(nc.semaphore(nm)))

        T, V, Sc, Sy, G = "tensor", "vector", "scalar", "sync", "gpsimd"

        st8 = {
            "bank_free": [None] * 8,
            "ht_used_t": [0, 0],
            "sg_free_v": [0, 0],
            "wbuf_free_t": [0, 0],
            "wready": {},
            "yf_free": [None, None],
            "xt_free": [None, None],
            "ro_free": [None, None],
            "ao_free": [None, None],
            "at_free": [None, None],
            "scores_free_v": [0, 0],
            "eslot_t": [0] * 12,
            "rrs_free": None,
        }

        pro_dma = 0
        for dst, src in ((idn, idn_c), (onesr, ones_c), (onescol, onescol_c),
                         (cst, consts_c),
                         (graw, d["L0_ffn_gate_weight"]),
                         (gb_sb, d["L0_ffn_gate_bias"].rearrange("(a b) -> a b", a=1))):
            pro_dma = k.op(Sy, lambda E, dst=dst, src=src: E.dma_start(
                out=dst[:], in_=src), inc="sx", n=16)
        k.wait(T, "sx", pro_dma)
        k.wait(Sc, "sx", pro_dma)
        k.wait(V, "sx", pro_dma)
        for kk in range(8):
            bb = kk % 2
            k.waitp(T, st8["bank_free"][bb])
            t_tr = k.op(T, lambda E, kk=kk, bb=bb: E.transpose(
                out=ps[:, bb, 0:32],
                in_=graw[0:32, 128 * kk:128 * (kk + 1)],
                identity=idn[0:32, 0:32]), inc="st")
            k.wait(V, "st", t_tr)
            v_tr = k.op(V, lambda E, kk=kk, bb=bb: E.tensor_copy(
                out=gwT[:, kk, :], in_=ps[:, bb, 0:32]), inc="sv")
            st8["bank_free"][bb] = ("sv", v_tr)
        k.waitp(T, st8["bank_free"][1])
        t_bc = k.op(T, lambda E: E.matmul(out=ps[:, 1, 0:NEXP], lhsT=onesr[:],
                                          rhs=gb_sb[:], start=True, stop=True), inc="st")
        k.wait(V, "st", t_bc)
        k.op(V, lambda E: E.tensor_copy(out=biasbc[:], in_=ps[:, 1, 0:NEXP]), inc="sv")
        v_pro = k.op(V, lambda E: E.memset(gs8[:, 4:8], -BIG), inc="sv")
        st8["bank_free"][0] = ("sv", v_pro)
        st8["bank_free"][1] = ("sv", v_pro)

        def ln_pass(src, ntiles, yT, slot0):
            for i in range(ntiles):
                t = i % 2
                sl = slot0 + i
                rows = 128 * sl
                k.waitp(Sy, st8["xt_free"][t])
                xdma = k.op(Sy, lambda E, t=t, rows=rows, src=src: E.dma_start(
                    out=xt2[t][:], in_=src[rows:rows + 128, :]), inc=f"sxt{t}", n=16)
                k.wait(V, f"sxt{t}", xdma)
                k.op(V, lambda E, t=t, sl=sl: E.tensor_scalar(
                    out=scrA[:], in0=xt2[t][:], scalar1=1.0, scalar2=0.0,
                    op0=ALU.mult, op1=ALU.add, accum_out=sum_sl[:, sl:sl + 1]))
                k.op(V, lambda E, t=t, sl=sl: E.scalar_tensor_tensor(
                    out=scrA[:], in0=xt2[t][:], scalar=1.0, in1=xt2[t][:],
                    op0=ALU.mult, op1=ALU.mult, accum_out=sumsq_sl[:, sl:sl + 1]))
                k.vop(lambda E, sl=sl: E.tensor_scalar(
                    out=mean_sl[:, sl:sl + 1], in0=sum_sl[:, sl:sl + 1],
                    scalar1=INV_D, scalar2=0.0, op0=ALU.mult, op1=ALU.add))
                k.vop(lambda E, sl=sl: E.tensor_scalar(
                    out=var_sl[:, sl:sl + 1], in0=sumsq_sl[:, sl:sl + 1],
                    scalar1=INV_D, scalar2=0.0, op0=ALU.mult, op1=ALU.add))
                k.vop(lambda E, sl=sl: E.tensor_mul(
                    out=st_tmp[:], in0=mean_sl[:, sl:sl + 1], in1=mean_sl[:, sl:sl + 1]))
                v_st = k.vop(lambda E, sl=sl: E.tensor_sub(
                    out=var_sl[:, sl:sl + 1], in0=var_sl[:, sl:sl + 1],
                    in1=st_tmp[:]))
                k.wait(Sc, "sv", v_st)
                s_sq = k.op(Sc, lambda E, sl=sl: E.activation(
                    out=std_sl[:, sl:sl + 1], in_=var_sl[:, sl:sl + 1], func=AF.Sqrt,
                    bias=cst[:, 0:1], scale=1.0), inc="ss")
                k.wait(V, "ss", s_sq)
                k.vop(lambda E, sl=sl: E.reciprocal(
                    out=rstd_sl[:, sl:sl + 1], in_=std_sl[:, sl:sl + 1]))
                v_nb = k.vop(lambda E, sl=sl: E.scalar_tensor_tensor(
                    out=nb_sl[:, sl:sl + 1], in0=mean_sl[:, sl:sl + 1], scalar=-1.0,
                    in1=rstd_sl[:, sl:sl + 1], op0=ALU.mult, op1=ALU.mult))
                k.wait(Sc, "sv", v_nb)
                k.waitp(Sc, st8["yf_free"][t])
                y_s = k.op(Sc, lambda E, t=t, sl=sl: E.activation(
                    out=yf2[t][:], in_=xt2[t][:], func=AF.Identity,
                    bias=nb_sl[:, sl:sl + 1], scale=rstd_sl[:, sl:sl + 1]), inc="ss")
                st8["xt_free"][t] = ("ss", y_s)
                k.wait(T, "ss", y_s)
                last_tr = 0
                for kk in range(8):
                    bb = kk % 2
                    k.waitp(T, st8["bank_free"][bb])
                    t_tr = k.op(T, lambda E, t=t, kk=kk, bb=bb: E.transpose(
                        out=ps[:, bb, 0:128], in_=yf2[t][:, 128 * kk:128 * (kk + 1)],
                        identity=idn[:]), inc="st")
                    last_tr = t_tr
                    k.wait(V, "st", t_tr)
                    v_ev = k.op(V, lambda E, kk=kk, bb=bb, i=i, yT=yT: E.tensor_copy(
                        out=yT[:, kk, 128 * i:128 * (i + 1)], in_=ps[:, bb, 0:128]),
                        inc="sv")
                    st8["bank_free"][bb] = ("sv", v_ev)
                st8["yf_free"][t] = ("st", last_tr)
            return k.ct["sv"]

        def routing_pass(yT, yT_ready):
            sc3 = sc_t[:].rearrange("p (a b) -> p a b", b=2)
            pm3 = pm[:].rearrange("p (a b) -> p a b", b=1)
            g3 = gs8[:, 0:4].rearrange("p (a b) -> p a b", b=1)
            for i in range(NTL):
                t = i % 2
                bb = 2 + i % 2
                k.wait(T, "sv", yT_ready)
                k.waitp(T, st8["bank_free"][bb])
                t_lg = None
                for kk in range(8):
                    t_lg = k.op(T, lambda E, kk=kk, i=i, bb=bb, yT=yT: E.matmul(
                        out=ps[:, bb, 0:NEXP], lhsT=yT[:, kk, 128 * i:128 * (i + 1)],
                        rhs=gwT[:, kk, :], start=(kk == 0), stop=(kk == 7)),
                        inc="st" if kk == 7 else None)
                k.wait(Sc, "st", t_lg)
                k.wait(Sc, "sv", st8["scores_free_v"][t])
                s_sg = k.op(Sc, lambda E, t=t, bb=bb: E.activation(
                    out=scores2[t][:], in_=ps[:, bb, 0:NEXP], func=AF.Sigmoid),
                    inc="ss")
                st8["bank_free"][bb] = ("ss", s_sg)
                k.wait(V, "ss", s_sg)
                k.vop(lambda E, t=t: E.tensor_add(
                    out=sc_t[:], in0=scores2[t][:], in1=biasbc[:]))
                for g in range(4):
                    k.vop(lambda E, g=g: E.max(
                        out=m8g[:, g, :], in_=sc_t[:, 8 * g:8 * (g + 1)]))
                k.vop(lambda E: E.tensor_add(
                    out=g3, in0=m8g[:, :, 0:1], in1=m8g[:, :, 1:2]))
                k.vop(lambda E: E.max(out=m8[:], in_=gs8[:]))
                k.vop(lambda E: E.tensor_scalar(
                    out=pen[:], in0=gs8[:, 0:4], scalar1=m8[:, 1:2], scalar2=-BIG,
                    op0=ALU.is_lt, op1=ALU.mult))
                for g in range(4):
                    k.vop(lambda E, g=g: E.tensor_scalar_add(
                        out=sc_t[:, 8 * g:8 * (g + 1)], in0=sc_t[:, 8 * g:8 * (g + 1)],
                        scalar1=pen[:, g:g + 1]))
                k.vop(lambda E: E.max(out=m8[:], in_=sc_t[:]))
                k.vop(lambda E: E.tensor_single_scalar(
                    out=w6[:], in_=sc_t[:], scalar=m8[:, 5:6], op=ALU.is_ge))
                v_w = k.vop(lambda E, t=t: E.tensor_mul(
                    out=w6[:], in0=w6[:], in1=scores2[t][:]))
                st8["scores_free_v"][t] = v_w
                k.vop(lambda E: E.reduce_sum(out=ssum[:], in_=w6[:], axis=AX.X))
                k.vop(lambda E: E.tensor_scalar_add(
                    out=rec1[:], in0=ssum[:], scalar1=1e-20))
                k.vop(lambda E: E.reciprocal(out=rec2[:], in_=rec1[:]))
                k.vop(lambda E: E.tensor_scalar_mul(
                    out=rec1[:], in0=rec2[:], scalar1=0.2))
                k.vop(lambda E, i=i: E.tensor_scalar_mul(
                    out=cw[:, i, :], in0=w6[:], scalar1=rec1[:]))
                k.vop(lambda E: E.tensor_tensor(
                    out=pm3, in0=sc3[:, :, 0:1], in1=sc3[:, :, 1:2], op=ALU.max))
                k.vop(lambda E, i=i: E.tensor_scalar(
                    out=sel[:, i, :], in0=pm[:], scalar1=m8[:, 0:1], scalar2=0.1,
                    op0=ALU.is_ge, op1=ALU.mult))
            return k.ct["sv"]

        def unit_weight_dma(gu, after_t):
            par = gu % 2
            u = gu % NUNIT
            k.wait(G, "st", after_t)
            if u < NEXP:
                srcs = [(wg_b[par][:], d["L0_ffn_w_gate"][u]),
                        (wu_b[par][:], d["L0_ffn_w_up"][u]),
                        (wd_b[par][:], d["L0_ffn_w_down"][u])]
            elif u == NEXP:
                srcs = [(wg_b[par][:], d["L0_ffn_shared_w_gate"]),
                        (wu_b[par][:], d["L0_ffn_shared_w_up"]),
                        (wd_b[par][:], d["L0_ffn_shared_w_down"])]
            else:
                l = u - NEXP - 1
                srcs = [(wg_b[par][:, :, 0:AH], d["L0_ffn_anchor_w_gate"][l]),
                        (wu_b[par][:, :, 0:AH], d["L0_ffn_anchor_w_up"][l]),
                        (wd_b[par][:, 0:2, :], d["L0_ffn_anchor_w_down"][l])]
            last = 0
            for dst, src in srcs:
                src = src.rearrange("(a p) h -> p a h", p=128)
                last = k.op(G, lambda E, dst=dst, src=src: E.dma_start(
                    out=dst, in_=src), inc=f"sw{par}", n=16)
            st8["wready"][gu] = last

        def gu_quarters(yT, w, par_step, par_u, nq, yT_ready, wrdy):
            htr = 0
            for q in range(nq):
                k.waitp(T, st8["bank_free"][2 * q])
                k.waitp(T, st8["bank_free"][2 * q + 1])
                k.wait(T, f"sw{par_u}", wrdy)
                k.wait(T, "sv", yT_ready)

                def t_gu(E, q=q, w=w, par_u=par_u, yT=yT):
                    for kk in range(8):
                        E.matmul(ps[:, 2 * q, :],
                                 lhsT=wg_b[par_u][:, kk, 128 * q:128 * (q + 1)],
                                 rhs=yT[:, kk, WT * w:WT * (w + 1)],
                                 start=(kk == 0), stop=(kk == 7))
                    ins = None
                    for kk in range(8):
                        ins = E.matmul(ps[:, 2 * q + 1, :],
                                       lhsT=wu_b[par_u][:, kk, 128 * q:128 * (q + 1)],
                                       rhs=yT[:, kk, WT * w:WT * (w + 1)],
                                       start=(kk == 0), stop=(kk == 7))
                    return ins
                t_q = k.op(T, t_gu, inc="st")
                k.wait(Sc, "st", t_q)
                k.wait(Sc, "sv", st8["sg_free_v"][q % 2])
                s_q = k.op(Sc, lambda E, q=q: E.activation(
                    out=sg2[q % 2][:], in_=ps[:, 2 * q, :], func=AF.Silu), inc="ss")
                st8["bank_free"][2 * q] = ("ss", s_q)
                k.wait(V, "ss", s_q)
                k.wait(V, "st", max(t_q, st8["ht_used_t"][par_step]))
                v_q = k.op(V, lambda E, q=q, par_step=par_step: E.tensor_mul(
                    out=ht2[par_step][:, q, :], in0=sg2[q % 2][:],
                    in1=ps[:, 2 * q + 1, :]), inc="sv")
                htr = v_q
                st8["bank_free"][2 * q + 1] = ("sv", v_q)
                st8["sg_free_v"][q % 2] = v_q
            return htr

        def down_unit(w, par_step, par_u, nk, ht_ready, cw_src):
            last_t = 0
            for r in range(8):
                tt, j = r // 2, r % 2
                k.waitp(T, st8["bank_free"][r])
                k.wait(T, "sv", ht_ready)

                def t_dn(E, r=r, tt=tt, j=j, par_step=par_step, par_u=par_u, nk=nk):
                    ins = None
                    for kk in range(nk):
                        ins = E.matmul(ps[:, r, :],
                                       lhsT=ht2[par_step][:, kk, 128 * tt:128 * (tt + 1)],
                                       rhs=wd_b[par_u][:, kk, WT * j:WT * (j + 1)],
                                       start=(kk == 0), stop=(kk == nk - 1))
                    return ins
                t_r = k.op(T, t_dn, inc="st")
                last_t = t_r
                k.wait(V, "st", t_r)
                tile = 4 * w + tt
                sc_ap = cw_src(tile)
                v_r = k.op(V, lambda E, r=r, tile=tile, j=j, sc_ap=sc_ap:
                           E.scalar_tensor_tensor(
                               out=acc[:, tile, WT * j:WT * (j + 1)], in0=ps[:, r, :],
                               scalar=sc_ap, in1=acc[:, tile, WT * j:WT * (j + 1)],
                               op0=ALU.mult, op1=ALU.add), inc="sv")
                st8["bank_free"][r] = ("sv", v_r)
            return last_t

        def cw_source(u):
            if u < NEXP:
                return lambda tile, u=u: cw[:, tile, u:u + 1]
            if u == NEXP:
                return lambda tile: 1.0
            l = u - NEXP - 1
            return lambda tile, l=l: sel[:, tile, l:l + 1]

        def ln_stats_piece(src, sl, rows):
            t = sl % 2
            k.waitp(Sy, st8["xt_free"][t])
            xdma = k.op(Sy, lambda E, t=t, rows=rows, src=src: E.dma_start(
                out=xt2[t][:], in_=src[rows:rows + 128, :]), inc=f"sxt{t}", n=16)
            k.wait(V, f"sxt{t}", xdma)
            k.op(V, lambda E, t=t, sl=sl: E.tensor_scalar(
                out=scrA[:], in0=xt2[t][:], scalar1=1.0, scalar2=0.0,
                op0=ALU.mult, op1=ALU.add, accum_out=sum_sl[:, sl:sl + 1]))
            v_s = k.op(V, lambda E, t=t, sl=sl: E.scalar_tensor_tensor(
                out=scrA[:], in0=xt2[t][:], scalar=1.0, in1=xt2[t][:],
                op0=ALU.mult, op1=ALU.mult,
                accum_out=sumsq_sl[:, sl:sl + 1]), inc="sv")
            st8["xt_free"][t] = ("sv", v_s)
            k.wait(V, "sv", v_s)
            k.vop(lambda E, sl=sl: E.tensor_scalar(
                out=mean_sl[:, sl:sl + 1], in0=sum_sl[:, sl:sl + 1],
                scalar1=INV_D, scalar2=0.0, op0=ALU.mult, op1=ALU.add))
            k.vop(lambda E, sl=sl: E.tensor_scalar(
                out=var_sl[:, sl:sl + 1], in0=sumsq_sl[:, sl:sl + 1],
                scalar1=INV_D, scalar2=0.0, op0=ALU.mult, op1=ALU.add))
            k.vop(lambda E, sl=sl: E.tensor_mul(
                out=st_tmp[:], in0=mean_sl[:, sl:sl + 1],
                in1=mean_sl[:, sl:sl + 1]))
            k.vop(lambda E, sl=sl: E.tensor_sub(
                out=var_sl[:, sl:sl + 1], in0=var_sl[:, sl:sl + 1],
                in1=st_tmp[:]))

        def ln_finish(src, ntiles, yT, slot0):
            v_st = k.ct["sv"]
            k.wait(Sc, "sv", v_st)
            s_sq = k.op(Sc, lambda E, slot0=slot0, ntiles=ntiles: E.activation(
                out=std_sl[:, slot0:slot0 + ntiles],
                in_=var_sl[:, slot0:slot0 + ntiles],
                func=AF.Sqrt, bias=cst[:, 0:1], scale=1.0), inc="ss")
            k.wait(V, "ss", s_sq)
            k.vop(lambda E, slot0=slot0, ntiles=ntiles: E.reciprocal(
                out=rstd_sl[:, slot0:slot0 + ntiles],
                in_=std_sl[:, slot0:slot0 + ntiles]))
            k.vop(lambda E, slot0=slot0, ntiles=ntiles: E.scalar_tensor_tensor(
                out=nb_sl[:, slot0:slot0 + ntiles],
                in0=mean_sl[:, slot0:slot0 + ntiles], scalar=-1.0,
                in1=rstd_sl[:, slot0:slot0 + ntiles], op0=ALU.mult, op1=ALU.mult))
            for i in range(ntiles):
                t = i % 2
                sl = slot0 + i
                rows = 128 * sl
                k.waitp(Sy, st8["xt_free"][t])
                xdma = k.op(Sy, lambda E, t=t, rows=rows, src=src: E.dma_start(
                    out=xt2[t][:], in_=src[rows:rows + 128, :]), inc=f"sxt{t}", n=16)
                k.wait(Sc, f"sxt{t}", xdma)
                k.wait(Sc, "sv", k.ct["sv"])
                k.waitp(Sc, st8["yf_free"][t])
                y_s = k.op(Sc, lambda E, t=t, sl=sl: E.activation(
                    out=yf2[t][:], in_=xt2[t][:], func=AF.Identity,
                    bias=nb_sl[:, sl:sl + 1], scale=rstd_sl[:, sl:sl + 1]),
                    inc="ss")
                st8["xt_free"][t] = ("ss", y_s)
                k.wait(T, "ss", y_s)
                last_tr = 0
                for kk in range(8):
                    bb = kk % 2
                    k.waitp(T, st8["bank_free"][bb])
                    t_tr = k.op(T, lambda E, t=t, kk=kk, bb=bb: E.transpose(
                        out=ps[:, bb, 0:128], in_=yf2[t][:, 128 * kk:128 * (kk + 1)],
                        identity=idn[:]), inc="st")
                    last_tr = t_tr
                    k.wait(V, "st", t_tr)
                    v_ev = k.op(V, lambda E, kk=kk, bb=bb, i=i, yT=yT: E.tensor_copy(
                        out=yT[:, kk, 128 * i:128 * (i + 1)], in_=ps[:, bb, 0:128]),
                        inc="sv")
                    st8["bank_free"][bb] = ("sv", v_ev)
                st8["yf_free"][t] = ("st", last_tr)
            return k.ct["sv"]

        def ffn_phase(src, dst, gu0):
            gu = gu0
            for blk in range(NBLK):
                base = blk * BLKT
                if blk == 0:
                    yT_ready = ln_pass(src, NTL, yT_blk, slot0=blk * NTL)
                else:
                    yT_ready = ln_finish(src, NTL, yT_blk, slot0=blk * NTL)
                routing_pass(yT_blk, yT_ready)
                k.op(V, lambda E: E.memset(acc[:], 0.0), inc="sv")
                stats_left = list(range(NTL)) if blk + 1 < NBLK else []
                prev = None
                step = 0
                for u in range(NUNIT):
                    if stats_left and u % 3 == 2:
                        i2 = stats_left.pop(0)
                        ln_stats_piece(src, (blk + 1) * NTL + i2,
                                       (blk + 1) * BLKT + 128 * i2)
                    nq = 4 if u <= NEXP else 2
                    nk = 4 if u <= NEXP else 2
                    par_u = gu % 2
                    if gu == gu0 and blk == 0:
                        unit_weight_dma(gu, after_t=st8["wbuf_free_t"][gu % 2])
                        unit_weight_dma(gu + 1, after_t=st8["wbuf_free_t"][(gu + 1) % 2])
                    wrdy = st8["wready"][gu]
                    for w in range(NWIN):
                        par_step = step % 2
                        htr = gu_quarters(yT_blk, w, par_step, par_u, nq, yT_ready, wrdy)
                        if prev is not None:
                            lt = down_unit(*prev)
                            st8["ht_used_t"][prev[1]] = lt
                            if prev[0] == NWIN - 1:
                                st8["wbuf_free_t"][(gu - 1) % 2] = lt
                                nxt = gu + 1
                                if nxt < gu0 + NBLK * NUNIT and nxt not in st8["wready"]:
                                    unit_weight_dma(nxt, after_t=st8["wbuf_free_t"][nxt % 2])
                        prev = (w, par_step, par_u, nk, htr, cw_source(u))
                        step += 1
                    gu += 1
                lt = down_unit(*prev)
                st8["ht_used_t"][prev[1]] = lt
                st8["wbuf_free_t"][(gu - 1) % 2] = lt
                nxt = gu + 1
                if nxt < gu0 + NBLK * NUNIT and nxt not in st8["wready"]:
                    unit_weight_dma(nxt, after_t=st8["wbuf_free_t"][nxt % 2])
                prev = None
                while stats_left:
                    i2 = stats_left.pop(0)
                    ln_stats_piece(src, (blk + 1) * NTL + i2,
                                   (blk + 1) * BLKT + 128 * i2)
                k.wait(V, "sv", k.ct["sv"])
                for i in range(NTL):
                    t = i % 2
                    rows = base + 128 * i
                    k.waitp(Sy, st8["xt_free"][t])
                    xdma = k.op(Sy, lambda E, t=t, rows=rows, src=src: E.dma_start(
                        out=xt2[t][:], in_=src[rows:rows + 128, :]), inc=f"sxt{t}", n=16)
                    k.wait(V, f"sxt{t}", xdma)
                    k.waitp(V, st8["ro_free"][t])
                    v_e = k.op(V, lambda E, t=t, i=i: E.scalar_tensor_tensor(
                        out=ro2[t][:], in0=acc[:, i, :], scalar=0.5, in1=xt2[t][:],
                        op0=ALU.mult, op1=ALU.add), inc="sv")
                    st8["xt_free"][t] = ("sv", v_e)
                    k.wait(Sy, "sv", v_e)
                    odma = k.op(Sy, lambda E, t=t, rows=rows, dst=dst: E.dma_start(
                        out=dst[rows:rows + 128, :], in_=ro2[t][:]), inc=f"sro{t}", n=16)
                    st8["ro_free"][t] = (f"sro{t}", odma)
            return gu

        def headnorm_sq(srcT, pb, dst01):
            t_last_prev = st8.get("qsq_free_t", 0)
            for half in range(2):
                colsl = slice(2048 * half, 2048 * (half + 1))
                k.wait(V, "st", t_last_prev)
                v_sq = k.op(V, lambda E, colsl=colsl, pb=pb, srcT=srcT: E.tensor_mul(
                    out=qsq[pb:pb + 64, :], in0=srcT[pb:pb + 64, colsl],
                    in1=srcT[pb:pb + 64, colsl]), inc="sv")
                k.wait(T, "sv", v_sq)
                t_n = 0
                for c in range(4):
                    k.waitp(T, st8["bank_free"][4 + c])
                    t_n = k.op(T, lambda E, c=c, pb=pb: E.matmul(
                        out=ps[0:1, 4 + c, :], lhsT=onescol[pb:pb + 64, :],
                        rhs=qsq[pb:pb + 64, WT * c:WT * (c + 1)],
                        start=True, stop=True), inc="st")
                t_last_prev = t_n
                k.wait(V, "st", t_n)
                tgt = st_tmp[0:1, 0:1] if half == 0 else dst01[:]
                v_red = k.vop(lambda E, tgt=tgt: E.tensor_reduce(
                    out=tgt, in_=ps[0:1, 4:8, :], axis=AX.XY, op=ALU.max))
                for c in range(4):
                    st8["bank_free"][4 + c] = ("sv", v_red)
            st8["qsq_free_t"] = t_last_prev
            k.vop(lambda E, dst01=dst01: E.tensor_tensor(
                out=dst01[:], in0=dst01[:], in1=st_tmp[0:1, 0:1], op=ALU.max))

        def attention_phase():
            mask_dma = k.op(Sy, lambda E: E.dma_start(out=masks[:], in_=masks_c),
                            inc="sx", n=16)
            k.wait(V, "sx", mask_dma)
            yT_ready = ln_pass(dr1, 32, yT_full, slot0=0)
            v_ones = k.op(V, lambda E: E.memset(v2p[:, :, :, 64:65], 1.0), inc="sv")
            last_qkv_t = 0
            for hp in range(8):
                k.wait(G, "st", last_qkv_t)
                wsl_last = 0
                for dstw, nm in ((wq_sl, "L0_attn_Wq"), (wk_sl, "L0_attn_Wk"),
                                 (wv_sl, "L0_attn_Wv")):
                    srcw = d[nm][:, 128 * hp:128 * (hp + 1)].rearrange(
                        "(a p) c -> p a c", p=128)
                    wsl_last = k.op(G, lambda E, dstw=dstw, srcw=srcw: E.dma_start(
                        out=dstw[:], in_=srcw), inc="sw0", n=16)
                for wsl, dstt, scale in ((wq_sl, qT2, 0.125), (wk_sl, kT2, 1.0)):
                    for w8 in range(8):
                        bb = 6 + w8 % 2
                        k.waitp(T, st8["bank_free"][bb])
                        k.wait(T, "sw0", wsl_last)
                        k.wait(T, "sv", yT_ready)

                        def t_q(E, wsl=wsl, w8=w8, bb=bb):
                            ins = None
                            for kk in range(8):
                                ins = E.matmul(ps[:, bb, :], lhsT=wsl[:, kk, :],
                                               rhs=yT_full[:, kk, WT * w8:WT * (w8 + 1)],
                                               start=(kk == 0), stop=(kk == 7))
                            return ins
                        t_w = k.op(T, t_q, inc="st")
                        k.wait(Sc, "st", t_w)
                        scap = cst[:, 1:2] if scale == 0.125 else 1.0
                        s_w = k.op(Sc, lambda E, dstt=dstt, w8=w8, bb=bb, scap=scap:
                                   E.activation(
                                       out=dstt[:, WT * w8:WT * (w8 + 1)],
                                       in_=ps[:, bb, :], func=AF.Identity,
                                       scale=scap), inc="ss")
                        st8["bank_free"][bb] = ("ss", s_w)
                for tt in range(32):
                    bb = 6 + tt % 2
                    k.waitp(T, st8["bank_free"][bb])
                    k.wait(T, "sw0", wsl_last)
                    k.wait(T, "sv", v_ones)

                    def t_v(E, tt=tt, bb=bb):
                        ins = None
                        for kk in range(8):
                            ins = E.matmul(ps[:, bb, 0:128],
                                           lhsT=yT_full[:, kk, 128 * tt:128 * (tt + 1)],
                                           rhs=wv_sl[:, kk, :],
                                           start=(kk == 0), stop=(kk == 7))
                        return ins
                    t_w = k.op(T, t_v, inc="st")
                    k.wait(V, "st", t_w)
                    v_w = k.op(V, lambda E, tt=tt, bb=bb: E.tensor_copy(
                        out=v2p[:, tt, :, 0:64],
                        in_=ps[:, bb, 0:128].rearrange("p (h c) -> p h c", h=2)),
                        inc="sv")
                    st8["bank_free"][bb] = ("sv", v_w)
                qkv_done_v = k.ct["sv"]
                last_qkv_t = k.ct["st"]
                k.wait(V, "ss", k.ct["ss"])
                headnorm_sq(qT2, 0, csq)
                headnorm_sq(kT2, 0, csc)
                headnorm_sq(qT2, 64, csq2)
                headnorm_sq(kT2, 64, csc2)
                k.vop(lambda E: E.tensor_mul(out=csq[:], in0=csq[:], in1=csc[:]))
                k.vop(lambda E: E.tensor_mul(out=csq2[:], in0=csq2[:], in1=csc2[:]))
                v_c = k.vop(lambda E: E.tensor_tensor(
                    out=csq[:], in0=csq[:], in1=csq2[:], op=ALU.max))
                k.wait(Sc, "sv", v_c)
                s_ln = k.sop(lambda E: E.activation(
                    out=csc[:], in_=csq[:], func=AF.Ln))
                s_c = k.sop(lambda E: E.activation(
                    out=csq[:], in_=csc[:], func=AF.Exp, scale=cst[0:1, 2:3]))
                k.wait(V, "ss", s_c)
                v_c2 = k.vop(lambda E: E.tensor_scalar(
                    out=csc[:], in0=csq[:], scalar1=-1.0, scalar2=-0.1,
                    op0=ALU.mult, op1=ALU.add))
                k.wait(T, "sv", v_c2)
                k.waitp(T, st8["bank_free"][6])
                t_nc = k.op(T, lambda E: E.matmul(
                    out=ps[:, 6, 0:1], lhsT=onesr[:], rhs=csc[:],
                    start=True, stop=True), inc="st")
                k.wait(V, "st", t_nc)
                v_nc = k.vop(lambda E: E.tensor_copy(
                    out=negc2[0][:], in_=ps[:, 6, 0:1]))
                st8["bank_free"][6] = ("sv", v_nc)
                negc_v = k.ct["sv"]
                def normalize_store(w, t_fl, hp=hp):
                    for h01 in range(2):
                        head = 2 * hp + h01
                        ob = 4 + h01
                        k.wait(Sc, "st", t_fl)
                        k.waitp(Sc, st8["rrs_free"])
                        k.sop(lambda E, ob=ob: E.activation(
                            out=rscr[:], in_=ps[64:65, ob, :], func=AF.Ln))
                        s_rr = k.sop(lambda E: E.activation(
                            out=rrs[:], in_=rscr[:], func=AF.Exp, scale=cst[0:1, 3:4]))
                        k.wait(T, "ss", s_rr)
                        k.waitp(T, st8["bank_free"][6])
                        t_b2 = k.op(T, lambda E: E.matmul(
                            out=ps[0:64, 6, :], lhsT=onesr[0:1, 0:64], rhs=rrs[:],
                            start=True, stop=True), inc="st")
                        st8["rrs_free"] = ("st", t_b2)
                        k.wait(V, "st", t_b2)
                        v_rb = k.vop(lambda E: E.tensor_copy(
                            out=rb_sb[:], in_=ps[0:64, 6, :]))
                        st8["bank_free"][6] = ("sv", v_rb)
                        tb = h01
                        k.waitp(V, st8["ao_free"][tb])
                        v_ao = k.op(V, lambda E, tb=tb, ob=ob: E.tensor_mul(
                            out=ao2[tb][:], in0=ps[0:64, ob, :], in1=rb_sb[:]),
                            inc="sv")
                        st8["bank_free"][ob] = ("sv", v_ao)
                        k.wait(Sy, "sv", v_ao)
                        pr0 = 64 * (head % 2)
                        dst_ap = dat[4 * w:4 * (w + 1), pr0:pr0 + 64, head // 2, :]\
                            .rearrange("t r c -> r t c")
                        src_ap = ao2[tb][:].rearrange("r (t c) -> r t c", c=128)
                        o_dma = k.op(Sy, lambda E, dst_ap=dst_ap, src_ap=src_ap:
                                     E.dma_start(out=dst_ap, in_=src_ap),
                                     inc=f"sao{tb}", n=16)
                        st8["ao_free"][tb] = (f"sao{tb}", o_dma)
                for w in range(8):
                    nkt = 4 * w + 4
                    nb2 = nkt // 2
                    exp_s = [[0] * nb2, [0] * nb2]
                    msk_v = [[0] * nkt, [0] * nkt]

                    def slot(h01, b2, kt):
                        return 6 * h01 + 2 * (b2 % 3) + kt % 2

                    def emit_scores(h01, b2, w=w):
                        pb = 64 * h01
                        for kt in (2 * b2, 2 * b2 + 1):
                            bb = 2 * h01 + kt % 2
                            if b2 >= 1:
                                k.wait(T, "ss", exp_s[h01][b2 - 1])
                            else:
                                k.waitp(T, st8["bank_free"][bb])
                            k.wait(T, "sv", qkv_done_v)
                            k.op(T, lambda E, kt=kt, bb=bb, pb=pb, w=w: E.matmul(
                                ps[:, bb, :],
                                lhsT=kT2[pb:pb + 64, 128 * kt:128 * (kt + 1)],
                                rhs=qT2[pb:pb + 64, WT * w:WT * (w + 1)],
                                start=True, stop=True), inc="st")
                        return k.ct["st"]

                    def emit_exp(h01, b2, sc_done, w=w):
                        bb0 = 2 * h01
                        sl0 = slot(h01, b2, 0)
                        k.wait(Sc, "st", sc_done)
                        k.wait(Sc, "sv", negc_v)
                        k.wait(Sc, "st", max(st8["eslot_t"][sl0],
                                             st8["eslot_t"][sl0 + 1]))
                        s_e = k.op(Sc, lambda E, bb0=bb0, sl0=sl0: E.activation(
                            out=expT[:, sl0:sl0 + 2, :], in_=ps[:, bb0:bb0 + 2, :],
                            func=AF.Exp, bias=negc2[0][:], scale=1.0), inc="ss")
                        exp_s[h01][b2] = s_e
                        st8["bank_free"][bb0] = ("ss", s_e)
                        st8["bank_free"][bb0 + 1] = ("ss", s_e)
                        for kt in (2 * b2, 2 * b2 + 1):
                            o = 128 * kt - 512 * w
                            if o >= 0:
                                mi = o // 128
                                sl = slot(h01, b2, kt)
                                k.wait(V, "ss", s_e)
                                msk_v[h01][kt] = k.op(V, lambda E, sl=sl, mi=mi:
                                                      E.tensor_mul(
                                                          out=expT[:, sl, :],
                                                          in0=expT[:, sl, :],
                                                          in1=masks[:, mi, :]), inc="sv")

                    def emit_attnv(b2, w=w, nkt=nkt):
                        for h01 in range(2):
                            ob = 4 + h01
                            for kt in (2 * b2, 2 * b2 + 1):
                                sl = slot(h01, b2, kt)
                                k.wait(T, "ss", exp_s[h01][b2])
                                if msk_v[h01][kt]:
                                    k.wait(T, "sv", msk_v[h01][kt])
                                if kt == 0:
                                    k.waitp(T, st8["bank_free"][ob])
                                t_a = k.op(T, lambda E, kt=kt, h01=h01, nkt=nkt, ob=ob, sl=sl:
                                           E.matmul(
                                               ps[0:65, ob, :], lhsT=v2p[:, kt, h01, :],
                                               rhs=expT[:, sl, :],
                                               start=(kt == 0), stop=(kt == nkt - 1)),
                                           inc="st")
                                st8["eslot_t"][sl] = t_a

                    for b2 in range(nb2):
                        scd0 = emit_scores(0, b2)
                        scd1 = emit_scores(1, b2)
                        if b2 >= 2:
                            emit_attnv(b2 - 2)
                        emit_exp(0, b2, scd0)
                        emit_exp(1, b2, scd1)
                        if b2 == 1 and st8.get("pend_norm"):
                            st8["pend_norm"]()
                            st8["pend_norm"] = None
                    if st8.get("pend_norm"):
                        st8["pend_norm"]()
                        st8["pend_norm"] = None
                    for b2 in range(max(nb2 - 2, 0), nb2):
                        emit_attnv(b2)
                    t_fl = k.ct["st"]

                    def norm_emit(w=w, t_fl=t_fl):
                        normalize_store(w, t_fl)
                    if w == 7:
                        norm_emit()
                    else:
                        st8["pend_norm"] = norm_emit

            k.wait(G, "st", k.ct["st"])
            wo_last = k.op(G, lambda E: E.dma_start(
                out=wo_sb[:],
                in_=d["L0_attn_Wo"].rearrange("(a p) h -> p a h", p=128)),
                inc="sw1", n=16)
            r2dst = dout if debug_stop == "B" else dr2
            for i in range(32):
                t = i % 2
                rows = 128 * i
                k.wait(Sy, "sao0", k.ct["sao0"])
                k.wait(Sy, "sao1", k.ct["sao1"])
                k.waitp(Sy, st8["at_free"][t])
                a_dma = k.op(Sy, lambda E, t=t, i=i: E.dma_start(
                    out=at_sb[t][:], in_=dat[i]), inc=f"sat{t}", n=16)
                k.waitp(Sy, st8["xt_free"][t])
                x_dma = k.op(Sy, lambda E, t=t, rows=rows: E.dma_start(
                    out=xt2[t][:], in_=dr1[rows:rows + 128, :]), inc=f"sxt{t}", n=16)
                t_half = [0, 0]
                for j in range(2):
                    bb = 6 + j
                    k.waitp(T, st8["bank_free"][bb])
                    k.wait(T, f"sat{t}", a_dma)
                    k.wait(T, "sw1", wo_last)

                    def t_wo(E, t=t, j=j, bb=bb):
                        ins = None
                        for kk in range(8):
                            ins = E.matmul(ps[:, bb, :], lhsT=at_sb[t][:, kk, :],
                                           rhs=wo_sb[:, kk, WT * j:WT * (j + 1)],
                                           start=(kk == 0), stop=(kk == 7))
                        return ins
                    t_half[j] = k.op(T, t_wo, inc="st")
                st8["at_free"][t] = ("st", t_half[1])
                k.wait(Sc, f"sxt{t}", x_dma)
                k.waitp(Sc, st8["yf_free"][t])
                s_y = k.op(Sc, lambda E, t=t, i=i: E.activation(
                    out=yf2[t][:], in_=xt2[t][:], func=AF.Identity,
                    bias=nb_sl[:, i:i + 1], scale=rstd_sl[:, i:i + 1]), inc="ss")
                st8["xt_free"][t] = ("ss", s_y)
                k.wait(V, "ss", s_y)
                v_t1 = k.op(V, lambda E, t=t: E.tensor_add(
                    out=scrA[:], in0=yf2[t][:], in1=xt2[t][:]), inc="sv")
                st8["yf_free"][t] = ("sv", v_t1)
                k.wait(V, "sv", v_t1)
                k.waitp(V, st8["ro_free"][t])
                v_r = 0
                for j in range(2):
                    k.wait(V, "st", t_half[j])
                    v_r = k.op(V, lambda E, t=t, j=j: E.tensor_add(
                        out=ro2[t][:, WT * j:WT * (j + 1)],
                        in0=scrA[:, WT * j:WT * (j + 1)], in1=ps[:, 6 + j, :]),
                        inc="sv")
                    st8["bank_free"][6 + j] = ("sv", v_r)
                k.wait(Sy, "sv", v_r)
                o_dma = k.op(Sy, lambda E, t=t, rows=rows, r2dst=r2dst: E.dma_start(
                    out=r2dst[rows:rows + 128, :], in_=ro2[t][:]), inc=f"sro{t}", n=16)
                st8["ro_free"][t] = (f"sro{t}", o_dma)

        if debug_stop == "R":
            yT_ready = ln_pass(d["x"], NTL, yT_blk, slot0=0)
            rv = routing_pass(yT_blk, yT_ready)
            k.wait(V, "sv", rv)
            k.op(V, lambda E: E.tensor_copy(
                out=scrA[:, 0:512], in_=cw[:].rearrange("p a b -> p (a b)")))
            k.op(V, lambda E: E.tensor_copy(
                out=scrA[:, 512:768], in_=sel[:].rearrange("p a b -> p (a b)")))
            k.op(V, lambda E: E.tensor_copy(out=scrA[:, 768:784], in_=rstd_sl[:, 0:16]))
            k.op(V, lambda E: E.tensor_copy(out=scrA[:, 800:816], in_=nb_sl[:, 0:16]))
            k.op(V, lambda E: E.tensor_copy(out=scrA[:, 832:848], in_=sum_sl[:, 0:16]))
            k.op(V, lambda E: E.tensor_copy(out=scrA[:, 896:912], in_=mean_sl[:, 0:16]))
            k.op(V, lambda E: E.tensor_copy(out=scrA[:, 912:928], in_=var_sl[:, 0:16]))
            k.op(V, lambda E: E.tensor_copy(out=scrA[:, 928:944], in_=std_sl[:, 0:16]))
            v_d = k.op(V, lambda E: E.tensor_copy(out=scrA[:, 864:880],
                                                  in_=sumsq_sl[:, 0:16]), inc="sv")
            k.wait(Sy, "sv", v_d)
            k.op(Sy, lambda E: E.dma_start(out=dout[0:128, :], in_=scrA[:]),
                 inc="sx", n=16)
            for kk in range(8):
                k.op(V, lambda E, kk=kk: E.tensor_copy(
                    out=ro2[0][:, 128 * kk:128 * (kk + 1)], in_=yT_blk[:, kk, 0:128]))
            v_y = k.op(V, lambda E: E.engine_nop(), inc="sv")
            k.wait(Sy, "sv", v_y)
            k.op(Sy, lambda E: E.dma_start(out=dout[128:256, :], in_=ro2[0][:]),
                 inc="sx", n=16)
            for s in k.sems:
                k.wait(Sy, s, k.ct[s])
            with nc.Block() as block:
                for eng, name in ((block.sync, "sync"), (block.gpsimd, "gpsimd"),
                                  (block.vector, "vector"), (block.scalar, "scalar"),
                                  (block.tensor, "tensor")):
                    def mk(name):
                        def f(E):
                            for fn in k.q[name]:
                                fn(E)
                        return f
                    eng(mk(name))
            return nc
        gu = ffn_phase(d["x"], dout if debug_stop == "A" else dr1, 0)
        if debug_stop != "A":
            k.full_barrier()
            attention_phase()
            if debug_stop != "B":
                k.full_barrier()
                ffn_phase(dr2, dout, gu)
        for s in k.sems:
            k.wait(Sy, s, k.ct[s])

        with nc.Block() as block:
            @block.sync
            def _(E):
                for fn in k.q["sync"]:
                    fn(E)

            @block.gpsimd
            def _(E):
                for fn in k.q["gpsimd"]:
                    fn(E)

            @block.vector
            def _(E):
                for fn in k.q["vector"]:
                    fn(E)

            @block.scalar
            def _(E):
                for fn in k.q["scalar"]:
                    fn(E)

            @block.tensor
            def _(E):
                for fn in k.q["tensor"]:
                    fn(E)

    return nc


def kernel(**inputs):
    x = np.ascontiguousarray(np.asarray(inputs["L0_attn_x"], dtype=np.float32))
    base = {n: np.ascontiguousarray(np.asarray(inputs[n], dtype=np.float32))
            for n in WNAMES}
    nc = build_program()
    in_maps = []
    for c in range(8):
        m = dict(base)
        m["L0_attn_x"] = np.ascontiguousarray(x[c])
        in_maps.append(m)
    res = run_bass_kernel_spmd(nc, in_maps, core_ids=list(range(8)))
    return np.stack([r["out"] for r in res.results], axis=0)


# version 11
# speedup vs baseline: 1.1215x; 1.0181x over previous
import numpy as np
import ml_dtypes
from contextlib import ExitStack

import concourse.bass as bass
import concourse.mybir as mybir
from concourse.bass_utils import run_bass_kernel_spmd

F32 = mybir.dt.float32
BF16 = mybir.dt.bfloat16
U8 = mybir.dt.uint8
AF = mybir.ActivationFunctionType
ALU = mybir.AluOpType
AX = mybir.AxisListType

S, Dm = 4096, 1024
NBLK, BLKT, NWIN, WT, NTL = 2, 2048, 4, 512, 16
NEXP, EH = 32, 512
NANC, AH = 16, 256
NHD, DK = 16, 64
NUNIT = NEXP + 1 + NANC
EPS = 1e-5
BIG = 1e30
INV_D = 1.0 / Dm

WNAMES = [
    "L0_attn_Wq", "L0_attn_Wk", "L0_attn_Wv", "L0_attn_Wo",
    "L0_ffn_gate_weight", "L0_ffn_gate_bias",
    "L0_ffn_w_gate", "L0_ffn_w_up", "L0_ffn_w_down",
    "L0_ffn_shared_w_gate", "L0_ffn_shared_w_up", "L0_ffn_shared_w_down",
    "L0_ffn_anchor_w_gate", "L0_ffn_anchor_w_up", "L0_ffn_anchor_w_down",
]

ENGS = ("tensor", "vector", "scalar", "sync", "gpsimd")


class Sched:
    def __init__(self):
        self.q = {e: [] for e in ENGS}
        self.sems = {}
        self.ct = {}
        self.last_wait = {}
        self.trace = {e: [] for e in ENGS}

    def add_sem(self, name, handle):
        self.sems[name] = handle
        self.ct[name] = 0

    def wait(self, eng, sem, val):
        if not val or val <= 0:
            return
        key = (eng, sem)
        if self.last_wait.get(key, 0) >= val:
            return
        self.last_wait[key] = val
        h = self.sems[sem]
        self.trace[eng].append(("wait", sem, val))
        self.q[eng].append(lambda E, h=h, v=val: E.wait_ge(h, v))

    def waitp(self, eng, pair):
        if pair is not None:
            self.wait(eng, pair[0], pair[1])

    def op(self, eng, fn, inc=None, n=1):
        if inc is None:
            self.q[eng].append(fn)
            self.trace[eng].append(("op", None, 0))
            return None
        h = self.sems[inc]

        def g(E, fn=fn, h=h, n=n):
            fn(E).then_inc(h, n)

        self.q[eng].append(g)
        self.ct[inc] += n
        self.trace[eng].append(("inc", inc, n))
        return self.ct[inc]

    def vop(self, fn):
        v = self.op("vector", fn, inc="sv")
        self.wait("vector", "sv", v)
        return v

    def sop(self, fn):
        v = self.op("scalar", fn, inc="ss")
        self.wait("scalar", "ss", v)
        return v

    def full_barrier(self):
        for e in ENGS:
            for s in self.sems:
                self.wait(e, s, self.ct[s])


def build_program(debug_stop=None):
    nc = bass.Bass("TRN2", target_bir_lowering=False)
    k = Sched()

    d = {}
    d["x"] = nc.dram_tensor("L0_attn_x", [S, Dm], F32, kind="ExternalInput").ap()
    shapes = {
        "L0_attn_Wq": [Dm, Dm], "L0_attn_Wk": [Dm, Dm], "L0_attn_Wv": [Dm, Dm],
        "L0_attn_Wo": [Dm, Dm],
        "L0_ffn_gate_weight": [NEXP, Dm], "L0_ffn_gate_bias": [NEXP],
        "L0_ffn_w_gate": [NEXP, Dm, EH], "L0_ffn_w_up": [NEXP, Dm, EH],
        "L0_ffn_w_down": [NEXP, EH, Dm],
        "L0_ffn_shared_w_gate": [Dm, EH], "L0_ffn_shared_w_up": [Dm, EH],
        "L0_ffn_shared_w_down": [EH, Dm],
        "L0_ffn_anchor_w_gate": [NANC, Dm, AH], "L0_ffn_anchor_w_up": [NANC, Dm, AH],
        "L0_ffn_anchor_w_down": [NANC, AH, Dm],
    }
    for n, sh in shapes.items():
        d[n] = nc.dram_tensor(n, sh, F32, kind="ExternalInput").ap()
    dout = nc.dram_tensor("out", [S, Dm], F32, kind="ExternalOutput").ap()
    dr1 = nc.dram_tensor("r1_scr", [S, Dm], F32, kind="Internal").ap()
    dr2 = nc.dram_tensor("r2_scr", [S, Dm], F32, kind="Internal").ap()
    dat = nc.dram_tensor("attnT_scr", [32, 128, 8, 128], BF16, kind="Internal").ap()

    idn_c = nc.inline_tensor(np.eye(128, dtype=np.float32), name="idn_c").ap()
    ones_c = nc.inline_tensor(np.ones((1, 128), dtype=np.float32), name="ones_c").ap()
    onescol_c = nc.inline_tensor(np.ones((128, 1), dtype=np.float32), name="onescol_c").ap()
    consts_c = nc.inline_tensor(
        np.tile(np.array([[EPS, 0.125, 0.5, -1.0]], dtype=np.float32), (128, 1)),
        name="consts_c").ap()
    mnp = np.zeros((128, 4, 512), dtype=ml_dtypes.bfloat16)
    for m in range(4):
        for p in range(128):
            mnp[p, m, 128 * m + p:] = 1.0
    masks_c = nc.inline_tensor(mnp, name="masks_c").ap()

    with ExitStack() as ctx:
        def sbuf(name, shape, dt=F32):
            return ctx.enter_context(nc.sbuf_tensor(name, shape, dt))

        idn = sbuf("idn", [128, 128])
        onesr = sbuf("onesr", [1, 128])
        onescol = sbuf("onescol", [128, 1])
        cst = sbuf("cst", [128, 4])
        gb_sb = sbuf("gb_sb", [1, NEXP])
        gwT = sbuf("gwT", [128, 8, NEXP], BF16)
        biasbc = sbuf("biasbc", [128, NEXP])
        sum_sl = sbuf("sum_sl", [128, 32]); sumsq_sl = sbuf("sumsq_sl", [128, 32])
        mean_sl = sbuf("mean_sl", [128, 32]); var_sl = sbuf("var_sl", [128, 32])
        std_sl = sbuf("std_sl", [128, 32]); rstd_sl = sbuf("rstd_sl", [128, 32])
        nb_sl = sbuf("nb_sl", [128, 32]); st_tmp = sbuf("st_tmp", [128, 1])
        xt2 = [sbuf(f"xt{i}", [128, Dm]) for i in range(2)]
        yf2 = [sbuf(f"yf{i}", [128, Dm]) for i in range(2)]
        scrA = sbuf("scrA", [128, Dm])
        ro2 = [sbuf(f"ro{i}", [128, Dm]) for i in range(2)]
        scores2 = [sbuf(f"scores{i}", [128, NEXP]) for i in range(2)]
        sc_t = sbuf("sc_t", [128, NEXP])
        m8g = sbuf("m8g", [128, 4, 8]); gs8 = sbuf("gs8", [128, 8])
        pen = sbuf("pen", [128, 4]); m8 = sbuf("m8", [128, 8])
        w6 = sbuf("w6", [128, NEXP]); pm = sbuf("pm", [128, 16])
        ssum = sbuf("ssum", [128, 1]); rec1 = sbuf("rec1", [128, 1]); rec2 = sbuf("rec2", [128, 1])
        cw = sbuf("cw", [128, NTL, NEXP])
        sel = sbuf("sel", [128, NTL, NANC])
        ht2 = [sbuf(f"ht{i}", [128, 4, 512], BF16) for i in range(2)]
        sg2 = [sbuf(f"sg{i}", [128, 512], BF16) for i in range(2)]
        csq = sbuf("csq", [1, 1]); csc = sbuf("csc", [1, 1])
        csq2 = sbuf("csq2", [1, 1]); csc2 = sbuf("csc2", [1, 1])
        negc2 = [sbuf(f"negc{i}", [128, 1]) for i in range(2)]
        rrs = sbuf("rrs", [1, 512]); rscr = sbuf("rscr", [1, 512])
        rb_sb = sbuf("rb_sb", [64, 512])
        ao2 = [sbuf(f"ao{i}", [64, 512], BF16) for i in range(2)]

        arena_big = sbuf("arena_big", [128, 65536], U8)
        arena_w = sbuf("arena_w", [128, 49152], U8)
        arena_y = sbuf("arena_y", [128, 32768], U8)
        ab = nc.lookup_mloc(arena_big).addr
        aw = nc.lookup_mloc(arena_w).addr
        ay = nc.lookup_mloc(arena_y).addr

        def ov(base, name, shape, dt, off):
            return nc.alloc_sbuf_tensor_at(name, shape, dt, offset=base + off)

        acc = ov(ab, "acc", [128, NTL, Dm], F32, 0)
        yT_full = ov(ab, "yT_full", [128, 8, S], BF16, 0)
        graw = ov(ab, "graw", [NEXP, Dm], F32, 0)
        wg_b = [ov(aw, f"wg_b{i}", [128, 8, 512], BF16, 24576 * i) for i in range(2)]
        wu_b = [ov(aw, f"wu_b{i}", [128, 8, 512], BF16, 24576 * i + 8192) for i in range(2)]
        wd_b = [ov(aw, f"wd_b{i}", [128, 4, 1024], BF16, 24576 * i + 16384) for i in range(2)]
        qT2 = ov(aw, "qT2", [128, S], BF16, 0)
        kT2 = ov(aw, "kT2", [128, S], BF16, 8192)
        v2p = ov(aw, "v2p", [128, 32, 2, 65], BF16, 16384)
        wo_sb = ov(aw, "wo_sb", [128, 8, Dm], BF16, 24704)
        wq_sl = ov(aw, "wq_sl", [128, 8, 128], BF16, 41088)
        wk_sl = ov(aw, "wk_sl", [128, 8, 128], BF16, 43136)
        wv_sl = ov(aw, "wv_sl", [128, 8, 128], BF16, 45184)
        yT_blk = ov(ay, "yT_blk", [128, 8, BLKT], BF16, 0)
        qsq = ov(ay, "qsq", [128, 2048], F32, 0)
        at_sb = [ov(ay, f"at_sb{i}", [128, 8, 128], BF16, 8192 + 2048 * i)
                 for i in range(2)]
        masks = ov(ay, "masks", [128, 4, 512], BF16, 16384)
        expT = ov(ay, "expT", [128, 12, 512], BF16, 20480)

        ps = ctx.enter_context(nc.psum_tensor("ps", [128, 8, 512], F32))

        for nm in ("sx", "sw0", "sw1", "sxt0", "sxt1", "sro0", "sro1",
                   "sao0", "sao1", "sat0", "sat1", "st", "sv", "ss"):
            k.add_sem(nm, ctx.enter_context(nc.semaphore(nm)))

        T, V, Sc, Sy, G = "tensor", "vector", "scalar", "sync", "gpsimd"

        st8 = {
            "bank_free": [None] * 8,
            "ht_used_t": [0, 0],
            "sg_free_v": [0, 0],
            "wbuf_free_t": [0, 0],
            "wready": {},
            "yf_free": [None, None],
            "epi_dma": [[None] * NTL, [None] * NTL],
            "xt_free": [None, None],
            "ro_free": [None, None],
            "ao_free": [None, None],
            "at_free": [None, None],
            "scores_free_v": [0, 0],
            "eslot_t": [0] * 12,
            "rrs_free": None,
        }

        pro_dma = 0
        for dst, src in ((idn, idn_c), (onesr, ones_c), (onescol, onescol_c),
                         (cst, consts_c),
                         (graw, d["L0_ffn_gate_weight"]),
                         (gb_sb, d["L0_ffn_gate_bias"].rearrange("(a b) -> a b", a=1))):
            pro_dma = k.op(Sy, lambda E, dst=dst, src=src: E.dma_start(
                out=dst[:], in_=src), inc="sx", n=16)
        k.wait(T, "sx", pro_dma)
        k.wait(Sc, "sx", pro_dma)
        k.wait(V, "sx", pro_dma)
        for kk in range(8):
            bb = kk % 2
            k.waitp(T, st8["bank_free"][bb])
            t_tr = k.op(T, lambda E, kk=kk, bb=bb: E.transpose(
                out=ps[:, bb, 0:32],
                in_=graw[0:32, 128 * kk:128 * (kk + 1)],
                identity=idn[0:32, 0:32]), inc="st")
            k.wait(V, "st", t_tr)
            v_tr = k.op(V, lambda E, kk=kk, bb=bb: E.tensor_copy(
                out=gwT[:, kk, :], in_=ps[:, bb, 0:32]), inc="sv")
            st8["bank_free"][bb] = ("sv", v_tr)
        k.waitp(T, st8["bank_free"][1])
        t_bc = k.op(T, lambda E: E.matmul(out=ps[:, 1, 0:NEXP], lhsT=onesr[:],
                                          rhs=gb_sb[:], start=True, stop=True), inc="st")
        k.wait(V, "st", t_bc)
        k.op(V, lambda E: E.tensor_copy(out=biasbc[:], in_=ps[:, 1, 0:NEXP]), inc="sv")
        v_pro = k.op(V, lambda E: E.memset(gs8[:, 4:8], -BIG), inc="sv")
        st8["bank_free"][0] = ("sv", v_pro)
        st8["bank_free"][1] = ("sv", v_pro)

        def ln_pass(src, ntiles, yT, slot0):
            for i in range(ntiles):
                t = i % 2
                sl = slot0 + i
                rows = 128 * sl
                k.waitp(Sy, st8["xt_free"][t])
                xdma = k.op(Sy, lambda E, t=t, rows=rows, src=src: E.dma_start(
                    out=xt2[t][:], in_=src[rows:rows + 128, :]), inc=f"sxt{t}", n=16)
                k.wait(V, f"sxt{t}", xdma)
                k.op(V, lambda E, t=t, sl=sl: E.tensor_scalar(
                    out=scrA[:], in0=xt2[t][:], scalar1=1.0, scalar2=0.0,
                    op0=ALU.mult, op1=ALU.add, accum_out=sum_sl[:, sl:sl + 1]))
                k.op(V, lambda E, t=t, sl=sl: E.scalar_tensor_tensor(
                    out=scrA[:], in0=xt2[t][:], scalar=1.0, in1=xt2[t][:],
                    op0=ALU.mult, op1=ALU.mult, accum_out=sumsq_sl[:, sl:sl + 1]))
                k.vop(lambda E, sl=sl: E.tensor_scalar(
                    out=mean_sl[:, sl:sl + 1], in0=sum_sl[:, sl:sl + 1],
                    scalar1=INV_D, scalar2=0.0, op0=ALU.mult, op1=ALU.add))
                k.vop(lambda E, sl=sl: E.tensor_scalar(
                    out=var_sl[:, sl:sl + 1], in0=sumsq_sl[:, sl:sl + 1],
                    scalar1=INV_D, scalar2=0.0, op0=ALU.mult, op1=ALU.add))
                k.vop(lambda E, sl=sl: E.tensor_mul(
                    out=st_tmp[:], in0=mean_sl[:, sl:sl + 1], in1=mean_sl[:, sl:sl + 1]))
                v_st = k.vop(lambda E, sl=sl: E.tensor_sub(
                    out=var_sl[:, sl:sl + 1], in0=var_sl[:, sl:sl + 1],
                    in1=st_tmp[:]))
                k.wait(Sc, "sv", v_st)
                s_sq = k.op(Sc, lambda E, sl=sl: E.activation(
                    out=std_sl[:, sl:sl + 1], in_=var_sl[:, sl:sl + 1], func=AF.Sqrt,
                    bias=cst[:, 0:1], scale=1.0), inc="ss")
                k.wait(V, "ss", s_sq)
                k.vop(lambda E, sl=sl: E.reciprocal(
                    out=rstd_sl[:, sl:sl + 1], in_=std_sl[:, sl:sl + 1]))
                v_nb = k.vop(lambda E, sl=sl: E.scalar_tensor_tensor(
                    out=nb_sl[:, sl:sl + 1], in0=mean_sl[:, sl:sl + 1], scalar=-1.0,
                    in1=rstd_sl[:, sl:sl + 1], op0=ALU.mult, op1=ALU.mult))
                k.wait(Sc, "sv", v_nb)
                k.waitp(Sc, st8["yf_free"][t])
                y_s = k.op(Sc, lambda E, t=t, sl=sl: E.activation(
                    out=yf2[t][:], in_=xt2[t][:], func=AF.Identity,
                    bias=nb_sl[:, sl:sl + 1], scale=rstd_sl[:, sl:sl + 1]), inc="ss")
                st8["xt_free"][t] = ("ss", y_s)
                k.wait(T, "ss", y_s)
                last_tr = 0
                for kk in range(8):
                    bb = kk % 2
                    k.waitp(T, st8["bank_free"][bb])
                    t_tr = k.op(T, lambda E, t=t, kk=kk, bb=bb: E.transpose(
                        out=ps[:, bb, 0:128], in_=yf2[t][:, 128 * kk:128 * (kk + 1)],
                        identity=idn[:]), inc="st")
                    last_tr = t_tr
                    k.wait(V, "st", t_tr)
                    v_ev = k.op(V, lambda E, kk=kk, bb=bb, i=i, yT=yT: E.tensor_copy(
                        out=yT[:, kk, 128 * i:128 * (i + 1)], in_=ps[:, bb, 0:128]),
                        inc="sv")
                    st8["bank_free"][bb] = ("sv", v_ev)
                st8["yf_free"][t] = ("st", last_tr)
            return k.ct["sv"]

        def routing_pass(yT, yT_ready):
            sc3 = sc_t[:].rearrange("p (a b) -> p a b", b=2)
            pm3 = pm[:].rearrange("p (a b) -> p a b", b=1)
            g3 = gs8[:, 0:4].rearrange("p (a b) -> p a b", b=1)
            for i in range(NTL):
                t = i % 2
                bb = 2 + i % 2
                k.wait(T, "sv", yT_ready)
                k.waitp(T, st8["bank_free"][bb])
                t_lg = None
                for kk in range(8):
                    t_lg = k.op(T, lambda E, kk=kk, i=i, bb=bb, yT=yT: E.matmul(
                        out=ps[:, bb, 0:NEXP], lhsT=yT[:, kk, 128 * i:128 * (i + 1)],
                        rhs=gwT[:, kk, :], start=(kk == 0), stop=(kk == 7)),
                        inc="st" if kk == 7 else None)
                k.wait(Sc, "st", t_lg)
                k.wait(Sc, "sv", st8["scores_free_v"][t])
                s_sg = k.op(Sc, lambda E, t=t, bb=bb: E.activation(
                    out=scores2[t][:], in_=ps[:, bb, 0:NEXP], func=AF.Sigmoid),
                    inc="ss")
                st8["bank_free"][bb] = ("ss", s_sg)
                k.wait(V, "ss", s_sg)
                k.vop(lambda E, t=t: E.tensor_add(
                    out=sc_t[:], in0=scores2[t][:], in1=biasbc[:]))
                for g in range(4):
                    k.vop(lambda E, g=g: E.max(
                        out=m8g[:, g, :], in_=sc_t[:, 8 * g:8 * (g + 1)]))
                k.vop(lambda E: E.tensor_add(
                    out=g3, in0=m8g[:, :, 0:1], in1=m8g[:, :, 1:2]))
                k.vop(lambda E: E.max(out=m8[:], in_=gs8[:]))
                k.vop(lambda E: E.tensor_scalar(
                    out=pen[:], in0=gs8[:, 0:4], scalar1=m8[:, 1:2], scalar2=-BIG,
                    op0=ALU.is_lt, op1=ALU.mult))
                for g in range(4):
                    k.vop(lambda E, g=g: E.tensor_scalar_add(
                        out=sc_t[:, 8 * g:8 * (g + 1)], in0=sc_t[:, 8 * g:8 * (g + 1)],
                        scalar1=pen[:, g:g + 1]))
                k.vop(lambda E: E.max(out=m8[:], in_=sc_t[:]))
                k.vop(lambda E: E.tensor_single_scalar(
                    out=w6[:], in_=sc_t[:], scalar=m8[:, 5:6], op=ALU.is_ge))
                v_w = k.vop(lambda E, t=t: E.tensor_mul(
                    out=w6[:], in0=w6[:], in1=scores2[t][:]))
                st8["scores_free_v"][t] = v_w
                k.vop(lambda E: E.reduce_sum(out=ssum[:], in_=w6[:], axis=AX.X))
                k.vop(lambda E: E.tensor_scalar_add(
                    out=rec1[:], in0=ssum[:], scalar1=1e-20))
                k.vop(lambda E: E.reciprocal(out=rec2[:], in_=rec1[:]))
                k.vop(lambda E: E.tensor_scalar_mul(
                    out=rec1[:], in0=rec2[:], scalar1=0.2))
                k.vop(lambda E, i=i: E.tensor_scalar_mul(
                    out=cw[:, i, :], in0=w6[:], scalar1=rec1[:]))
                k.vop(lambda E: E.tensor_tensor(
                    out=pm3, in0=sc3[:, :, 0:1], in1=sc3[:, :, 1:2], op=ALU.max))
                k.vop(lambda E, i=i: E.tensor_scalar(
                    out=sel[:, i, :], in0=pm[:], scalar1=m8[:, 0:1], scalar2=0.1,
                    op0=ALU.is_ge, op1=ALU.mult))
            return k.ct["sv"]

        def unit_weight_dma(gu, after_t):
            par = gu % 2
            u = gu % NUNIT
            k.wait(G, "st", after_t)
            if u < NEXP:
                srcs = [(wg_b[par][:], d["L0_ffn_w_gate"][u]),
                        (wu_b[par][:], d["L0_ffn_w_up"][u]),
                        (wd_b[par][:], d["L0_ffn_w_down"][u])]
            elif u == NEXP:
                srcs = [(wg_b[par][:], d["L0_ffn_shared_w_gate"]),
                        (wu_b[par][:], d["L0_ffn_shared_w_up"]),
                        (wd_b[par][:], d["L0_ffn_shared_w_down"])]
            else:
                l = u - NEXP - 1
                srcs = [(wg_b[par][:, :, 0:AH], d["L0_ffn_anchor_w_gate"][l]),
                        (wu_b[par][:, :, 0:AH], d["L0_ffn_anchor_w_up"][l]),
                        (wd_b[par][:, 0:2, :], d["L0_ffn_anchor_w_down"][l])]
            last = 0
            for dst, src in srcs:
                src = src.rearrange("(a p) h -> p a h", p=128)
                last = k.op(G, lambda E, dst=dst, src=src: E.dma_start(
                    out=dst, in_=src), inc=f"sw{par}", n=16)
            st8["wready"][gu] = last

        def gu_quarters(yT, w, par_step, par_u, nq, yT_ready, wrdy):
            htr = 0
            for q in range(nq):
                k.waitp(T, st8["bank_free"][2 * q])
                k.waitp(T, st8["bank_free"][2 * q + 1])
                k.wait(T, f"sw{par_u}", wrdy)
                k.wait(T, "sv", yT_ready)

                def t_gu(E, q=q, w=w, par_u=par_u, yT=yT):
                    for kk in range(8):
                        E.matmul(ps[:, 2 * q, :],
                                 lhsT=wg_b[par_u][:, kk, 128 * q:128 * (q + 1)],
                                 rhs=yT[:, kk, WT * w:WT * (w + 1)],
                                 start=(kk == 0), stop=(kk == 7))
                    ins = None
                    for kk in range(8):
                        ins = E.matmul(ps[:, 2 * q + 1, :],
                                       lhsT=wu_b[par_u][:, kk, 128 * q:128 * (q + 1)],
                                       rhs=yT[:, kk, WT * w:WT * (w + 1)],
                                       start=(kk == 0), stop=(kk == 7))
                    return ins
                t_q = k.op(T, t_gu, inc="st")
                k.wait(Sc, "st", t_q)
                k.wait(Sc, "sv", st8["sg_free_v"][q % 2])
                s_q = k.op(Sc, lambda E, q=q: E.activation(
                    out=sg2[q % 2][:], in_=ps[:, 2 * q, :], func=AF.Silu), inc="ss")
                st8["bank_free"][2 * q] = ("ss", s_q)
                k.wait(V, "ss", s_q)
                k.wait(V, "st", max(t_q, st8["ht_used_t"][par_step]))
                v_q = k.op(V, lambda E, q=q, par_step=par_step: E.tensor_mul(
                    out=ht2[par_step][:, q, :], in0=sg2[q % 2][:],
                    in1=ps[:, 2 * q + 1, :]), inc="sv")
                htr = v_q
                st8["bank_free"][2 * q + 1] = ("sv", v_q)
                st8["sg_free_v"][q % 2] = v_q
            return htr

        def down_unit(w, par_step, par_u, nk, ht_ready, cw_src):
            last_t = 0
            for r in range(8):
                tt, j = r // 2, r % 2
                k.waitp(T, st8["bank_free"][r])
                k.wait(T, "sv", ht_ready)

                def t_dn(E, r=r, tt=tt, j=j, par_step=par_step, par_u=par_u, nk=nk):
                    ins = None
                    for kk in range(nk):
                        ins = E.matmul(ps[:, r, :],
                                       lhsT=ht2[par_step][:, kk, 128 * tt:128 * (tt + 1)],
                                       rhs=wd_b[par_u][:, kk, WT * j:WT * (j + 1)],
                                       start=(kk == 0), stop=(kk == nk - 1))
                    return ins
                t_r = k.op(T, t_dn, inc="st")
                last_t = t_r
                k.wait(V, "st", t_r)
                tile = 4 * w + tt
                sc_ap = cw_src(tile)
                v_r = k.op(V, lambda E, r=r, tile=tile, j=j, sc_ap=sc_ap:
                           E.scalar_tensor_tensor(
                               out=acc[:, tile, WT * j:WT * (j + 1)], in0=ps[:, r, :],
                               scalar=sc_ap, in1=acc[:, tile, WT * j:WT * (j + 1)],
                               op0=ALU.mult, op1=ALU.add), inc="sv")
                st8["bank_free"][r] = ("sv", v_r)
            return last_t

        def cw_source(u):
            if u < NEXP:
                return lambda tile, u=u: cw[:, tile, u:u + 1]
            if u == NEXP:
                return lambda tile: 1.0
            l = u - NEXP - 1
            return lambda tile, l=l: sel[:, tile, l:l + 1]

        def ln_stats_piece(src, sl, rows, gate=None):
            t = sl % 2
            k.waitp(Sy, gate)
            k.waitp(Sy, st8["xt_free"][t])
            xdma = k.op(Sy, lambda E, t=t, rows=rows, src=src: E.dma_start(
                out=xt2[t][:], in_=src[rows:rows + 128, :]), inc=f"sxt{t}", n=16)
            k.wait(V, f"sxt{t}", xdma)
            k.op(V, lambda E, t=t, sl=sl: E.tensor_scalar(
                out=scrA[:], in0=xt2[t][:], scalar1=1.0, scalar2=0.0,
                op0=ALU.mult, op1=ALU.add, accum_out=sum_sl[:, sl:sl + 1]))
            v_s = k.op(V, lambda E, t=t, sl=sl: E.scalar_tensor_tensor(
                out=scrA[:], in0=xt2[t][:], scalar=1.0, in1=xt2[t][:],
                op0=ALU.mult, op1=ALU.mult,
                accum_out=sumsq_sl[:, sl:sl + 1]), inc="sv")
            st8["xt_free"][t] = ("sv", v_s)
            k.wait(V, "sv", v_s)
            k.vop(lambda E, sl=sl: E.tensor_scalar(
                out=mean_sl[:, sl:sl + 1], in0=sum_sl[:, sl:sl + 1],
                scalar1=INV_D, scalar2=0.0, op0=ALU.mult, op1=ALU.add))
            k.vop(lambda E, sl=sl: E.tensor_scalar(
                out=var_sl[:, sl:sl + 1], in0=sumsq_sl[:, sl:sl + 1],
                scalar1=INV_D, scalar2=0.0, op0=ALU.mult, op1=ALU.add))
            k.vop(lambda E, sl=sl: E.tensor_mul(
                out=st_tmp[:], in0=mean_sl[:, sl:sl + 1],
                in1=mean_sl[:, sl:sl + 1]))
            k.vop(lambda E, sl=sl: E.tensor_sub(
                out=var_sl[:, sl:sl + 1], in0=var_sl[:, sl:sl + 1],
                in1=st_tmp[:]))

        def ln_finish(src, ntiles, yT, slot0):
            v_st = k.ct["sv"]
            k.wait(Sc, "sv", v_st)
            s_sq = k.op(Sc, lambda E, slot0=slot0, ntiles=ntiles: E.activation(
                out=std_sl[:, slot0:slot0 + ntiles],
                in_=var_sl[:, slot0:slot0 + ntiles],
                func=AF.Sqrt, bias=cst[:, 0:1], scale=1.0), inc="ss")
            k.wait(V, "ss", s_sq)
            k.vop(lambda E, slot0=slot0, ntiles=ntiles: E.reciprocal(
                out=rstd_sl[:, slot0:slot0 + ntiles],
                in_=std_sl[:, slot0:slot0 + ntiles]))
            k.vop(lambda E, slot0=slot0, ntiles=ntiles: E.scalar_tensor_tensor(
                out=nb_sl[:, slot0:slot0 + ntiles],
                in0=mean_sl[:, slot0:slot0 + ntiles], scalar=-1.0,
                in1=rstd_sl[:, slot0:slot0 + ntiles], op0=ALU.mult, op1=ALU.mult))
            for i in range(ntiles):
                t = i % 2
                sl = slot0 + i
                rows = 128 * sl
                k.waitp(Sy, st8["xt_free"][t])
                xdma = k.op(Sy, lambda E, t=t, rows=rows, src=src: E.dma_start(
                    out=xt2[t][:], in_=src[rows:rows + 128, :]), inc=f"sxt{t}", n=16)
                k.wait(Sc, f"sxt{t}", xdma)
                k.wait(Sc, "sv", k.ct["sv"])
                k.waitp(Sc, st8["yf_free"][t])
                y_s = k.op(Sc, lambda E, t=t, sl=sl: E.activation(
                    out=yf2[t][:], in_=xt2[t][:], func=AF.Identity,
                    bias=nb_sl[:, sl:sl + 1], scale=rstd_sl[:, sl:sl + 1]),
                    inc="ss")
                st8["xt_free"][t] = ("ss", y_s)
                k.wait(T, "ss", y_s)
                last_tr = 0
                for kk in range(8):
                    bb = kk % 2
                    k.waitp(T, st8["bank_free"][bb])
                    t_tr = k.op(T, lambda E, t=t, kk=kk, bb=bb: E.transpose(
                        out=ps[:, bb, 0:128], in_=yf2[t][:, 128 * kk:128 * (kk + 1)],
                        identity=idn[:]), inc="st")
                    last_tr = t_tr
                    k.wait(V, "st", t_tr)
                    v_ev = k.op(V, lambda E, kk=kk, bb=bb, i=i, yT=yT: E.tensor_copy(
                        out=yT[:, kk, 128 * i:128 * (i + 1)], in_=ps[:, bb, 0:128]),
                        inc="sv")
                    st8["bank_free"][bb] = ("sv", v_ev)
                st8["yf_free"][t] = ("st", last_tr)
            return k.ct["sv"]

        def ffn_phase(src, dst, gu0, extra_pieces=None, first_prepped=False):
            gu = gu0
            extra = list(extra_pieces or [])
            for blk in range(NBLK):
                base = blk * BLKT
                if blk == 0 and not first_prepped:
                    yT_ready = ln_pass(src, NTL, yT_blk, slot0=blk * NTL)
                elif blk == 0:
                    yT_ready = ln_finish(src, NTL, yT_blk, slot0=0)
                else:
                    yT_ready = ln_finish(src, NTL, yT_blk, slot0=blk * NTL)
                routing_pass(yT_blk, yT_ready)
                k.op(V, lambda E: E.memset(acc[:], 0.0), inc="sv")
                stats_left = list(range(NTL)) if blk + 1 < NBLK else []
                prev = None
                step = 0
                for u in range(NUNIT):
                    if stats_left and u % 3 == 2:
                        i2 = stats_left.pop(0)
                        ln_stats_piece(src, (blk + 1) * NTL + i2,
                                       (blk + 1) * BLKT + 128 * i2)
                    elif blk + 1 == NBLK and extra and u % 3 == 2:
                        extra.pop(0)()
                    nq = 4 if u <= NEXP else 2
                    nk = 4 if u <= NEXP else 2
                    par_u = gu % 2
                    if gu == gu0 and blk == 0:
                        unit_weight_dma(gu, after_t=st8["wbuf_free_t"][gu % 2])
                        unit_weight_dma(gu + 1, after_t=st8["wbuf_free_t"][(gu + 1) % 2])
                    wrdy = st8["wready"][gu]
                    for w in range(NWIN):
                        par_step = step % 2
                        htr = gu_quarters(yT_blk, w, par_step, par_u, nq, yT_ready, wrdy)
                        if prev is not None:
                            lt = down_unit(*prev)
                            st8["ht_used_t"][prev[1]] = lt
                            if prev[0] == NWIN - 1:
                                st8["wbuf_free_t"][(gu - 1) % 2] = lt
                                nxt = gu + 1
                                if nxt < gu0 + NBLK * NUNIT and nxt not in st8["wready"]:
                                    unit_weight_dma(nxt, after_t=st8["wbuf_free_t"][nxt % 2])
                        prev = (w, par_step, par_u, nk, htr, cw_source(u))
                        step += 1
                    gu += 1
                lt = down_unit(*prev)
                st8["ht_used_t"][prev[1]] = lt
                st8["wbuf_free_t"][(gu - 1) % 2] = lt
                nxt = gu + 1
                if nxt < gu0 + NBLK * NUNIT and nxt not in st8["wready"]:
                    unit_weight_dma(nxt, after_t=st8["wbuf_free_t"][nxt % 2])
                prev = None
                while stats_left:
                    i2 = stats_left.pop(0)
                    ln_stats_piece(src, (blk + 1) * NTL + i2,
                                   (blk + 1) * BLKT + 128 * i2)
                if blk + 1 == NBLK:
                    while extra:
                        extra.pop(0)()
                k.wait(V, "sv", k.ct["sv"])
                for i in range(NTL):
                    t = i % 2
                    rows = base + 128 * i
                    k.waitp(Sy, st8["xt_free"][t])
                    xdma = k.op(Sy, lambda E, t=t, rows=rows, src=src: E.dma_start(
                        out=xt2[t][:], in_=src[rows:rows + 128, :]), inc=f"sxt{t}", n=16)
                    k.wait(V, f"sxt{t}", xdma)
                    k.waitp(V, st8["ro_free"][t])
                    v_e = k.op(V, lambda E, t=t, i=i: E.scalar_tensor_tensor(
                        out=ro2[t][:], in0=acc[:, i, :], scalar=0.5, in1=xt2[t][:],
                        op0=ALU.mult, op1=ALU.add), inc="sv")
                    st8["xt_free"][t] = ("sv", v_e)
                    k.wait(Sc, "sv", v_e)
                    odma = k.op(Sc, lambda E, t=t, rows=rows, dst=dst: E.dma_start(
                        out=dst[rows:rows + 128, :], in_=ro2[t][:]), inc=f"sro{t}", n=16)
                    st8["ro_free"][t] = (f"sro{t}", odma)
                    st8["epi_dma"][blk % 2][i] = (f"sro{t}", odma)
            return gu

        def headnorm_sq(srcT, pb, dst01):
            t_last_prev = st8.get("qsq_free_t", 0)
            for half in range(2):
                colsl = slice(2048 * half, 2048 * (half + 1))
                k.wait(V, "st", t_last_prev)
                v_sq = k.op(V, lambda E, colsl=colsl, pb=pb, srcT=srcT: E.tensor_mul(
                    out=qsq[pb:pb + 64, :], in0=srcT[pb:pb + 64, colsl],
                    in1=srcT[pb:pb + 64, colsl]), inc="sv")
                k.wait(T, "sv", v_sq)
                t_n = 0
                for c in range(4):
                    k.waitp(T, st8["bank_free"][4 + c])
                    t_n = k.op(T, lambda E, c=c, pb=pb: E.matmul(
                        out=ps[0:1, 4 + c, :], lhsT=onescol[pb:pb + 64, :],
                        rhs=qsq[pb:pb + 64, WT * c:WT * (c + 1)],
                        start=True, stop=True), inc="st")
                t_last_prev = t_n
                k.wait(V, "st", t_n)
                tgt = st_tmp[0:1, 0:1] if half == 0 else dst01[:]
                v_red = k.vop(lambda E, tgt=tgt: E.tensor_reduce(
                    out=tgt, in_=ps[0:1, 4:8, :], axis=AX.XY, op=ALU.max))
                for c in range(4):
                    st8["bank_free"][4 + c] = ("sv", v_red)
            st8["qsq_free_t"] = t_last_prev
            k.vop(lambda E, dst01=dst01: E.tensor_tensor(
                out=dst01[:], in0=dst01[:], in1=st_tmp[0:1, 0:1], op=ALU.max))

        def attention_phase():
            mask_dma = k.op(Sy, lambda E: E.dma_start(out=masks[:], in_=masks_c),
                            inc="sx", n=16)
            k.wait(V, "sx", mask_dma)
            for i in range(16, 32):
                ln_stats_piece(dr1, i, 128 * i)
            yT_ready = ln_finish(dr1, 32, yT_full, slot0=0)
            v_ones = k.op(V, lambda E: E.memset(v2p[:, :, :, 64:65], 1.0), inc="sv")
            last_qkv_t = 0
            for hp in range(8):
                k.wait(G, "st", last_qkv_t)
                wsl_last = 0
                for dstw, nm in ((wq_sl, "L0_attn_Wq"), (wk_sl, "L0_attn_Wk"),
                                 (wv_sl, "L0_attn_Wv")):
                    srcw = d[nm][:, 128 * hp:128 * (hp + 1)].rearrange(
                        "(a p) c -> p a c", p=128)
                    wsl_last = k.op(G, lambda E, dstw=dstw, srcw=srcw: E.dma_start(
                        out=dstw[:], in_=srcw), inc="sw0", n=16)
                for wsl, dstt, scale in ((wq_sl, qT2, 0.125), (wk_sl, kT2, 1.0)):
                    for w8 in range(8):
                        bb = 6 + w8 % 2
                        k.waitp(T, st8["bank_free"][bb])
                        k.wait(T, "sw0", wsl_last)
                        k.wait(T, "sv", yT_ready)

                        def t_q(E, wsl=wsl, w8=w8, bb=bb):
                            ins = None
                            for kk in range(8):
                                ins = E.matmul(ps[:, bb, :], lhsT=wsl[:, kk, :],
                                               rhs=yT_full[:, kk, WT * w8:WT * (w8 + 1)],
                                               start=(kk == 0), stop=(kk == 7))
                            return ins
                        t_w = k.op(T, t_q, inc="st")
                        k.wait(Sc, "st", t_w)
                        scap = cst[:, 1:2] if scale == 0.125 else 1.0
                        s_w = k.op(Sc, lambda E, dstt=dstt, w8=w8, bb=bb, scap=scap:
                                   E.activation(
                                       out=dstt[:, WT * w8:WT * (w8 + 1)],
                                       in_=ps[:, bb, :], func=AF.Identity,
                                       scale=scap), inc="ss")
                        st8["bank_free"][bb] = ("ss", s_w)
                for tt in range(32):
                    bb = 6 + tt % 2
                    k.waitp(T, st8["bank_free"][bb])
                    k.wait(T, "sw0", wsl_last)
                    k.wait(T, "sv", v_ones)

                    def t_v(E, tt=tt, bb=bb):
                        ins = None
                        for kk in range(8):
                            ins = E.matmul(ps[:, bb, 0:128],
                                           lhsT=yT_full[:, kk, 128 * tt:128 * (tt + 1)],
                                           rhs=wv_sl[:, kk, :],
                                           start=(kk == 0), stop=(kk == 7))
                        return ins
                    t_w = k.op(T, t_v, inc="st")
                    k.wait(V, "st", t_w)
                    v_w = k.op(V, lambda E, tt=tt, bb=bb: E.tensor_copy(
                        out=v2p[:, tt, :, 0:64],
                        in_=ps[:, bb, 0:128].rearrange("p (h c) -> p h c", h=2)),
                        inc="sv")
                    st8["bank_free"][bb] = ("sv", v_w)
                qkv_done_v = k.ct["sv"]
                last_qkv_t = k.ct["st"]
                k.wait(V, "ss", k.ct["ss"])
                headnorm_sq(qT2, 0, csq)
                headnorm_sq(kT2, 0, csc)
                headnorm_sq(qT2, 64, csq2)
                headnorm_sq(kT2, 64, csc2)
                k.vop(lambda E: E.tensor_mul(out=csq[:], in0=csq[:], in1=csc[:]))
                k.vop(lambda E: E.tensor_mul(out=csq2[:], in0=csq2[:], in1=csc2[:]))
                v_c = k.vop(lambda E: E.tensor_tensor(
                    out=csq[:], in0=csq[:], in1=csq2[:], op=ALU.max))
                k.wait(Sc, "sv", v_c)
                s_ln = k.sop(lambda E: E.activation(
                    out=csc[:], in_=csq[:], func=AF.Ln))
                s_c = k.sop(lambda E: E.activation(
                    out=csq[:], in_=csc[:], func=AF.Exp, scale=cst[0:1, 2:3]))
                k.wait(V, "ss", s_c)
                v_c2 = k.vop(lambda E: E.tensor_scalar(
                    out=csc[:], in0=csq[:], scalar1=-1.0, scalar2=-0.1,
                    op0=ALU.mult, op1=ALU.add))
                k.wait(T, "sv", v_c2)
                k.waitp(T, st8["bank_free"][6])
                t_nc = k.op(T, lambda E: E.matmul(
                    out=ps[:, 6, 0:1], lhsT=onesr[:], rhs=csc[:],
                    start=True, stop=True), inc="st")
                k.wait(V, "st", t_nc)
                v_nc = k.vop(lambda E: E.tensor_copy(
                    out=negc2[0][:], in_=ps[:, 6, 0:1]))
                st8["bank_free"][6] = ("sv", v_nc)
                negc_v = k.ct["sv"]
                def normalize_store(w, t_fl, hp=hp):
                    for h01 in range(2):
                        head = 2 * hp + h01
                        ob = 4 + h01
                        k.wait(Sc, "st", t_fl)
                        k.waitp(Sc, st8["rrs_free"])
                        k.sop(lambda E, ob=ob: E.activation(
                            out=rscr[:], in_=ps[64:65, ob, :], func=AF.Ln))
                        s_rr = k.sop(lambda E: E.activation(
                            out=rrs[:], in_=rscr[:], func=AF.Exp, scale=cst[0:1, 3:4]))
                        k.wait(T, "ss", s_rr)
                        k.waitp(T, st8["bank_free"][6])
                        t_b2 = k.op(T, lambda E: E.matmul(
                            out=ps[0:64, 6, :], lhsT=onesr[0:1, 0:64], rhs=rrs[:],
                            start=True, stop=True), inc="st")
                        st8["rrs_free"] = ("st", t_b2)
                        k.wait(V, "st", t_b2)
                        v_rb = k.vop(lambda E: E.tensor_copy(
                            out=rb_sb[:], in_=ps[0:64, 6, :]))
                        st8["bank_free"][6] = ("sv", v_rb)
                        tb = h01
                        k.waitp(V, st8["ao_free"][tb])
                        v_ao = k.op(V, lambda E, tb=tb, ob=ob: E.tensor_mul(
                            out=ao2[tb][:], in0=ps[0:64, ob, :], in1=rb_sb[:]),
                            inc="sv")
                        st8["bank_free"][ob] = ("sv", v_ao)
                        k.wait(Sy, "sv", v_ao)
                        pr0 = 64 * (head % 2)
                        dst_ap = dat[4 * w:4 * (w + 1), pr0:pr0 + 64, head // 2, :]\
                            .rearrange("t r c -> r t c")
                        src_ap = ao2[tb][:].rearrange("r (t c) -> r t c", c=128)
                        o_dma = k.op(Sy, lambda E, dst_ap=dst_ap, src_ap=src_ap:
                                     E.dma_start(out=dst_ap, in_=src_ap),
                                     inc=f"sao{tb}", n=16)
                        st8["ao_free"][tb] = (f"sao{tb}", o_dma)
                for w in range(8):
                    nkt = 4 * w + 4
                    nb2 = nkt // 2
                    exp_s = [[0] * nb2, [0] * nb2]
                    msk_v = [[0] * nkt, [0] * nkt]

                    def slot(h01, b2, kt):
                        return 6 * h01 + 2 * (b2 % 3) + kt % 2

                    def emit_scores(h01, b2, w=w):
                        pb = 64 * h01
                        for kt in (2 * b2, 2 * b2 + 1):
                            bb = 2 * h01 + kt % 2
                            if b2 >= 1:
                                k.wait(T, "ss", exp_s[h01][b2 - 1])
                            else:
                                k.waitp(T, st8["bank_free"][bb])
                            k.wait(T, "sv", qkv_done_v)
                            k.op(T, lambda E, kt=kt, bb=bb, pb=pb, w=w: E.matmul(
                                ps[:, bb, :],
                                lhsT=kT2[pb:pb + 64, 128 * kt:128 * (kt + 1)],
                                rhs=qT2[pb:pb + 64, WT * w:WT * (w + 1)],
                                start=True, stop=True), inc="st")
                        return k.ct["st"]

                    def emit_exp(h01, b2, sc_done, w=w):
                        bb0 = 2 * h01
                        sl0 = slot(h01, b2, 0)
                        k.wait(Sc, "st", sc_done)
                        k.wait(Sc, "sv", negc_v)
                        k.wait(Sc, "st", max(st8["eslot_t"][sl0],
                                             st8["eslot_t"][sl0 + 1]))
                        s_e = k.op(Sc, lambda E, bb0=bb0, sl0=sl0: E.activation(
                            out=expT[:, sl0:sl0 + 2, :], in_=ps[:, bb0:bb0 + 2, :],
                            func=AF.Exp, bias=negc2[0][:], scale=1.0), inc="ss")
                        exp_s[h01][b2] = s_e
                        st8["bank_free"][bb0] = ("ss", s_e)
                        st8["bank_free"][bb0 + 1] = ("ss", s_e)
                        for kt in (2 * b2, 2 * b2 + 1):
                            o = 128 * kt - 512 * w
                            if o >= 0:
                                mi = o // 128
                                sl = slot(h01, b2, kt)
                                k.wait(V, "ss", s_e)
                                msk_v[h01][kt] = k.op(V, lambda E, sl=sl, mi=mi:
                                                      E.tensor_mul(
                                                          out=expT[:, sl, :],
                                                          in0=expT[:, sl, :],
                                                          in1=masks[:, mi, :]), inc="sv")

                    def emit_attnv(b2, w=w, nkt=nkt):
                        for h01 in range(2):
                            ob = 4 + h01
                            for kt in (2 * b2, 2 * b2 + 1):
                                sl = slot(h01, b2, kt)
                                k.wait(T, "ss", exp_s[h01][b2])
                                if msk_v[h01][kt]:
                                    k.wait(T, "sv", msk_v[h01][kt])
                                if kt == 0:
                                    k.waitp(T, st8["bank_free"][ob])
                                t_a = k.op(T, lambda E, kt=kt, h01=h01, nkt=nkt, ob=ob, sl=sl:
                                           E.matmul(
                                               ps[0:65, ob, :], lhsT=v2p[:, kt, h01, :],
                                               rhs=expT[:, sl, :],
                                               start=(kt == 0), stop=(kt == nkt - 1)),
                                           inc="st")
                                st8["eslot_t"][sl] = t_a

                    for b2 in range(nb2):
                        scd0 = emit_scores(0, b2)
                        scd1 = emit_scores(1, b2)
                        if b2 >= 2:
                            emit_attnv(b2 - 2)
                        emit_exp(0, b2, scd0)
                        emit_exp(1, b2, scd1)
                        if b2 == 1 and st8.get("pend_norm"):
                            st8["pend_norm"]()
                            st8["pend_norm"] = None
                    if st8.get("pend_norm"):
                        st8["pend_norm"]()
                        st8["pend_norm"] = None
                    for b2 in range(max(nb2 - 2, 0), nb2):
                        emit_attnv(b2)
                    t_fl = k.ct["st"]

                    def norm_emit(w=w, t_fl=t_fl):
                        normalize_store(w, t_fl)
                    if w == 7:
                        norm_emit()
                    else:
                        st8["pend_norm"] = norm_emit

            k.wait(G, "st", k.ct["st"])
            wo_last = k.op(G, lambda E: E.dma_start(
                out=wo_sb[:],
                in_=d["L0_attn_Wo"].rearrange("(a p) h -> p a h", p=128)),
                inc="sw1", n=16)
            r2dst = dout if debug_stop == "B" else dr2
            for i in range(32):
                t = i % 2
                rows = 128 * i
                k.wait(Sy, "sao0", k.ct["sao0"])
                k.wait(Sy, "sao1", k.ct["sao1"])
                k.waitp(Sy, st8["at_free"][t])
                a_dma = k.op(Sy, lambda E, t=t, i=i: E.dma_start(
                    out=at_sb[t][:], in_=dat[i]), inc=f"sat{t}", n=16)
                k.waitp(Sy, st8["xt_free"][t])
                x_dma = k.op(Sy, lambda E, t=t, rows=rows: E.dma_start(
                    out=xt2[t][:], in_=dr1[rows:rows + 128, :]), inc=f"sxt{t}", n=16)
                t_half = [0, 0]
                for j in range(2):
                    bb = 6 + j
                    k.waitp(T, st8["bank_free"][bb])
                    k.wait(T, f"sat{t}", a_dma)
                    k.wait(T, "sw1", wo_last)

                    def t_wo(E, t=t, j=j, bb=bb):
                        ins = None
                        for kk in range(8):
                            ins = E.matmul(ps[:, bb, :], lhsT=at_sb[t][:, kk, :],
                                           rhs=wo_sb[:, kk, WT * j:WT * (j + 1)],
                                           start=(kk == 0), stop=(kk == 7))
                        return ins
                    t_half[j] = k.op(T, t_wo, inc="st")
                st8["at_free"][t] = ("st", t_half[1])
                k.wait(Sc, f"sxt{t}", x_dma)
                k.waitp(Sc, st8["yf_free"][t])
                s_y = k.op(Sc, lambda E, t=t, i=i: E.activation(
                    out=yf2[t][:], in_=xt2[t][:], func=AF.Identity,
                    bias=nb_sl[:, i:i + 1], scale=rstd_sl[:, i:i + 1]), inc="ss")
                st8["xt_free"][t] = ("ss", s_y)
                k.wait(V, "ss", s_y)
                v_t1 = k.op(V, lambda E, t=t: E.tensor_add(
                    out=scrA[:], in0=yf2[t][:], in1=xt2[t][:]), inc="sv")
                st8["yf_free"][t] = ("sv", v_t1)
                k.wait(V, "sv", v_t1)
                k.waitp(V, st8["ro_free"][t])
                v_r = 0
                for j in range(2):
                    k.wait(V, "st", t_half[j])
                    v_r = k.op(V, lambda E, t=t, j=j: E.tensor_add(
                        out=ro2[t][:, WT * j:WT * (j + 1)],
                        in0=scrA[:, WT * j:WT * (j + 1)], in1=ps[:, 6 + j, :]),
                        inc="sv")
                    st8["bank_free"][6 + j] = ("sv", v_r)
                k.wait(Sc, "sv", v_r)
                o_dma = k.op(Sc, lambda E, t=t, rows=rows, r2dst=r2dst: E.dma_start(
                    out=r2dst[rows:rows + 128, :], in_=ro2[t][:]), inc=f"sro{t}", n=16)
                st8["ro_free"][t] = (f"sro{t}", o_dma)
                if debug_stop is None and i < 16:
                    ln_stats_piece(dr2, i, 128 * i, gate=(f"sro{t}", o_dma))

        if debug_stop == "R":
            yT_ready = ln_pass(d["x"], NTL, yT_blk, slot0=0)
            rv = routing_pass(yT_blk, yT_ready)
            k.wait(V, "sv", rv)
            k.op(V, lambda E: E.tensor_copy(
                out=scrA[:, 0:512], in_=cw[:].rearrange("p a b -> p (a b)")))
            k.op(V, lambda E: E.tensor_copy(
                out=scrA[:, 512:768], in_=sel[:].rearrange("p a b -> p (a b)")))
            k.op(V, lambda E: E.tensor_copy(out=scrA[:, 768:784], in_=rstd_sl[:, 0:16]))
            k.op(V, lambda E: E.tensor_copy(out=scrA[:, 800:816], in_=nb_sl[:, 0:16]))
            k.op(V, lambda E: E.tensor_copy(out=scrA[:, 832:848], in_=sum_sl[:, 0:16]))
            k.op(V, lambda E: E.tensor_copy(out=scrA[:, 896:912], in_=mean_sl[:, 0:16]))
            k.op(V, lambda E: E.tensor_copy(out=scrA[:, 912:928], in_=var_sl[:, 0:16]))
            k.op(V, lambda E: E.tensor_copy(out=scrA[:, 928:944], in_=std_sl[:, 0:16]))
            v_d = k.op(V, lambda E: E.tensor_copy(out=scrA[:, 864:880],
                                                  in_=sumsq_sl[:, 0:16]), inc="sv")
            k.wait(Sy, "sv", v_d)
            k.op(Sy, lambda E: E.dma_start(out=dout[0:128, :], in_=scrA[:]),
                 inc="sx", n=16)
            for kk in range(8):
                k.op(V, lambda E, kk=kk: E.tensor_copy(
                    out=ro2[0][:, 128 * kk:128 * (kk + 1)], in_=yT_blk[:, kk, 0:128]))
            v_y = k.op(V, lambda E: E.engine_nop(), inc="sv")
            k.wait(Sy, "sv", v_y)
            k.op(Sy, lambda E: E.dma_start(out=dout[128:256, :], in_=ro2[0][:]),
                 inc="sx", n=16)
            for s in k.sems:
                k.wait(Sy, s, k.ct[s])
            with nc.Block() as block:
                for eng, name in ((block.sync, "sync"), (block.gpsimd, "gpsimd"),
                                  (block.vector, "vector"), (block.scalar, "scalar"),
                                  (block.tensor, "tensor")):
                    def mk(name):
                        def f(E):
                            for fn in k.q[name]:
                                fn(E)
                        return f
                    eng(mk(name))
            return nc
        def bstat_piece(i):
            def go(i=i):
                t = i % 2
                ln_stats_piece(dr1, i, 128 * i, gate=st8["epi_dma"][0][i])
            return go
        bpieces = [bstat_piece(i) for i in range(16)] if debug_stop != "A" else []
        gu = ffn_phase(d["x"], dout if debug_stop == "A" else dr1, 0,
                       extra_pieces=bpieces)
        if debug_stop != "A":
            k.full_barrier()
            attention_phase()
            if debug_stop != "B":
                k.full_barrier()
                ffn_phase(dr2, dout, gu, first_prepped=True)
        for s in k.sems:
            k.wait(Sy, s, k.ct[s])

        with nc.Block() as block:
            @block.sync
            def _(E):
                for fn in k.q["sync"]:
                    fn(E)

            @block.gpsimd
            def _(E):
                for fn in k.q["gpsimd"]:
                    fn(E)

            @block.vector
            def _(E):
                for fn in k.q["vector"]:
                    fn(E)

            @block.scalar
            def _(E):
                for fn in k.q["scalar"]:
                    fn(E)

            @block.tensor
            def _(E):
                for fn in k.q["tensor"]:
                    fn(E)

    return nc


def kernel(**inputs):
    x = np.ascontiguousarray(np.asarray(inputs["L0_attn_x"], dtype=np.float32))
    base = {n: np.ascontiguousarray(np.asarray(inputs[n], dtype=np.float32))
            for n in WNAMES}
    nc = build_program()
    in_maps = []
    for c in range(8):
        m = dict(base)
        m["L0_attn_x"] = np.ascontiguousarray(x[c])
        in_maps.append(m)
    res = run_bass_kernel_spmd(nc, in_maps, core_ids=list(range(8)))
    return np.stack([r["out"] for r in res.results], axis=0)


# version 12
# speedup vs baseline: 1.2017x; 1.0716x over previous
import numpy as np
import ml_dtypes
from contextlib import ExitStack

import concourse.bass as bass
import concourse.mybir as mybir
from concourse.bass_utils import run_bass_kernel_spmd

F32 = mybir.dt.float32
BF16 = mybir.dt.bfloat16
U8 = mybir.dt.uint8
AF = mybir.ActivationFunctionType
ALU = mybir.AluOpType
AX = mybir.AxisListType

S, Dm = 4096, 1024
NBLK, BLKT, NWIN, WT, NTL = 2, 2048, 4, 512, 16
NEXP, EH = 32, 512
NANC, AH = 16, 256
NHD, DK = 16, 64
NUNIT = NEXP + 1 + NANC
EPS = 1e-5
BIG = 1e30
INV_D = 1.0 / Dm

WNAMES = [
    "L0_attn_Wq", "L0_attn_Wk", "L0_attn_Wv", "L0_attn_Wo",
    "L0_ffn_gate_weight", "L0_ffn_gate_bias",
    "L0_ffn_w_gate", "L0_ffn_w_up", "L0_ffn_w_down",
    "L0_ffn_shared_w_gate", "L0_ffn_shared_w_up", "L0_ffn_shared_w_down",
    "L0_ffn_anchor_w_gate", "L0_ffn_anchor_w_up", "L0_ffn_anchor_w_down",
]

ENGS = ("tensor", "vector", "scalar", "sync", "gpsimd")


class Sched:
    def __init__(self):
        self.q = {e: [] for e in ENGS}
        self.sems = {}
        self.ct = {}
        self.last_wait = {}
        self.trace = {e: [] for e in ENGS}

    def add_sem(self, name, handle):
        self.sems[name] = handle
        self.ct[name] = 0

    def wait(self, eng, sem, val):
        if not val or val <= 0:
            return
        key = (eng, sem)
        if self.last_wait.get(key, 0) >= val:
            return
        self.last_wait[key] = val
        h = self.sems[sem]
        self.trace[eng].append(("wait", sem, val))
        self.q[eng].append(lambda E, h=h, v=val: E.wait_ge(h, v))

    def waitp(self, eng, pair):
        if pair is not None:
            self.wait(eng, pair[0], pair[1])

    def op(self, eng, fn, inc=None, n=1):
        if inc is None:
            self.q[eng].append(fn)
            self.trace[eng].append(("op", None, 0))
            return None
        h = self.sems[inc]

        def g(E, fn=fn, h=h, n=n):
            fn(E).then_inc(h, n)

        self.q[eng].append(g)
        self.ct[inc] += n
        self.trace[eng].append(("inc", inc, n))
        return self.ct[inc]

    def vop(self, fn):
        v = self.op("vector", fn, inc="sv")
        self.wait("vector", "sv", v)
        return v

    def sop(self, fn):
        v = self.op("scalar", fn, inc="ss")
        self.wait("scalar", "ss", v)
        return v

    def full_barrier(self):
        for e in ENGS:
            for s in self.sems:
                self.wait(e, s, self.ct[s])


def build_program(debug_stop=None):
    nc = bass.Bass("TRN2", target_bir_lowering=False)
    k = Sched()

    d = {}
    d["x"] = nc.dram_tensor("L0_attn_x", [S, Dm], F32, kind="ExternalInput").ap()
    shapes = {
        "L0_attn_Wq": [Dm, Dm], "L0_attn_Wk": [Dm, Dm], "L0_attn_Wv": [Dm, Dm],
        "L0_attn_Wo": [Dm, Dm],
        "L0_ffn_gate_weight": [NEXP, Dm], "L0_ffn_gate_bias": [NEXP],
        "L0_ffn_w_gate": [NEXP, Dm, EH], "L0_ffn_w_up": [NEXP, Dm, EH],
        "L0_ffn_w_down": [NEXP, EH, Dm],
        "L0_ffn_shared_w_gate": [Dm, EH], "L0_ffn_shared_w_up": [Dm, EH],
        "L0_ffn_shared_w_down": [EH, Dm],
        "L0_ffn_anchor_w_gate": [NANC, Dm, AH], "L0_ffn_anchor_w_up": [NANC, Dm, AH],
        "L0_ffn_anchor_w_down": [NANC, AH, Dm],
    }
    for n, sh in shapes.items():
        d[n] = nc.dram_tensor(n, sh, F32, kind="ExternalInput").ap()
    dout = nc.dram_tensor("out", [S, Dm], F32, kind="ExternalOutput").ap()
    dr1 = nc.dram_tensor("r1_scr", [S, Dm], F32, kind="Internal").ap()
    dr2 = nc.dram_tensor("r2_scr", [S, Dm], F32, kind="Internal").ap()
    dat = nc.dram_tensor("attnT_scr", [32, 128, 8, 128], BF16, kind="Internal").ap()

    idn_c = nc.inline_tensor(np.eye(128, dtype=np.float32), name="idn_c").ap()
    ones_c = nc.inline_tensor(np.ones((1, 128), dtype=np.float32), name="ones_c").ap()
    onescol_c = nc.inline_tensor(np.ones((128, 1), dtype=np.float32), name="onescol_c").ap()
    onescol_b_c = nc.inline_tensor(np.ones((128, 1), dtype=ml_dtypes.bfloat16),
                                   name="onescol_b_c").ap()
    consts_c = nc.inline_tensor(
        np.tile(np.array([[EPS, 0.125, 0.5, -1.0]], dtype=np.float32), (128, 1)),
        name="consts_c").ap()
    mnp = np.zeros((128, 4, 512), dtype=ml_dtypes.bfloat16)
    for m in range(4):
        for p in range(128):
            mnp[p, m, 128 * m + p:] = 1.0
    masks_c = nc.inline_tensor(mnp, name="masks_c").ap()

    with ExitStack() as ctx:
        def sbuf(name, shape, dt=F32):
            return ctx.enter_context(nc.sbuf_tensor(name, shape, dt))

        idn = sbuf("idn", [128, 128])
        onesr = sbuf("onesr", [1, 128])
        onescol = sbuf("onescol", [128, 1])
        onescol_b = sbuf("onescol_b", [128, 1], BF16)
        cst = sbuf("cst", [128, 4])
        gb_sb = sbuf("gb_sb", [1, NEXP])
        gb16 = sbuf("gb16", [1, NTL * NEXP])
        gwT = sbuf("gwT", [128, 8, NEXP], BF16)
        biasbc16 = sbuf("biasbc16", [128, NTL * NEXP])
        sum_sl = sbuf("sum_sl", [128, 32]); sumsq_sl = sbuf("sumsq_sl", [128, 32])
        mean_sl = sbuf("mean_sl", [128, 32]); var_sl = sbuf("var_sl", [128, 32])
        std_sl = sbuf("std_sl", [128, 32]); rstd_sl = sbuf("rstd_sl", [128, 32])
        nb_sl = sbuf("nb_sl", [128, 32]); st_tmp = sbuf("st_tmp", [128, 1])
        xt2 = [sbuf(f"xt{i}", [128, Dm]) for i in range(2)]
        yf2 = [sbuf(f"yf{i}", [128, Dm]) for i in range(2)]
        scrA = sbuf("scrA", [128, Dm])
        ro2 = [sbuf(f"ro{i}", [128, Dm]) for i in range(2)]
        scores_all = sbuf("scores_all", [128, NTL * NEXP])
        sc_all = sbuf("sc_all", [128, NTL * NEXP])
        w6_all = sbuf("w6_all", [128, NTL * NEXP])
        m8g_all = nc.alloc_sbuf_tensor_at(
            "m8g_all", [128, 4 * NTL, 8], F32,
            offset=nc.lookup_mloc(w6_all).addr)
        gs8_all = sbuf("gs8_all", [128, NTL, 8])
        m8a_all = sbuf("m8a_all", [128, NTL, 8])
        m8s_all = sbuf("m8s_all", [128, NTL, 8])
        pen_all = sbuf("pen_all", [128, NTL, 4])
        pm_all = sbuf("pm_all", [128, NTL, 16])
        ssum_all = sbuf("ssum_all", [128, NTL, 1])
        rec_all = sbuf("rec_all", [128, NTL, 1])
        cw = sbuf("cw", [128, NTL, NEXP])
        sel = sbuf("sel", [128, NTL, NANC])
        ht2 = [sbuf(f"ht{i}", [128, 4, 512], BF16) for i in range(2)]
        sg2 = [sbuf(f"sg{i}", [128, 512], BF16) for i in range(2)]
        csq = sbuf("csq", [1, 1]); csc = sbuf("csc", [1, 1])
        csq2 = sbuf("csq2", [1, 1]); csc2 = sbuf("csc2", [1, 1])
        negc2 = [sbuf(f"negc{i}", [128, 1]) for i in range(2)]
        rrs = sbuf("rrs", [1, 512])
        rb_sb = sbuf("rb_sb", [64, 512])
        ao2 = [sbuf(f"ao{i}", [64, 512], BF16) for i in range(2)]

        arena_big = sbuf("arena_big", [128, 65536], U8)
        arena_w = sbuf("arena_w", [128, 49152], U8)
        arena_y = sbuf("arena_y", [128, 32768], U8)
        ab = nc.lookup_mloc(arena_big).addr
        aw = nc.lookup_mloc(arena_w).addr
        ay = nc.lookup_mloc(arena_y).addr

        def ov(base, name, shape, dt, off):
            return nc.alloc_sbuf_tensor_at(name, shape, dt, offset=base + off)

        acc = ov(ab, "acc", [128, NTL, Dm], F32, 0)
        yT_full = ov(ab, "yT_full", [128, 8, S], BF16, 0)
        graw = ov(ab, "graw", [NEXP, Dm], F32, 0)
        wg_b = [ov(aw, f"wg_b{i}", [128, 8, 512], BF16, 24576 * i) for i in range(2)]
        wu_b = [ov(aw, f"wu_b{i}", [128, 8, 512], BF16, 24576 * i + 8192) for i in range(2)]
        wd_b = [ov(aw, f"wd_b{i}", [128, 4, 1024], BF16, 24576 * i + 16384) for i in range(2)]
        qT2 = ov(aw, "qT2", [128, S], BF16, 0)
        kT2 = ov(aw, "kT2", [128, S], BF16, 8192)
        v2p = ov(aw, "v2p", [128, 32, 2, 65], BF16, 16384)
        wo_sb = ov(aw, "wo_sb", [128, 8, Dm], BF16, 24704)
        wq_sl = ov(aw, "wq_sl", [128, 8, 128], BF16, 41088)
        wk_sl = ov(aw, "wk_sl", [128, 8, 128], BF16, 43136)
        wv_sl = ov(aw, "wv_sl", [128, 8, 128], BF16, 45184)
        yT_blk = ov(ay, "yT_blk", [128, 8, BLKT], BF16, 0)
        qsq = ov(ay, "qsq", [128, 2048], BF16, 0)
        at_sb = [ov(ay, f"at_sb{i}", [128, 8, 128], BF16, 8192 + 2048 * i)
                 for i in range(2)]
        masks = ov(ay, "masks", [128, 4, 512], BF16, 16384)
        expT = ov(ay, "expT", [128, 12, 512], BF16, 20480)

        ps = ctx.enter_context(nc.psum_tensor("ps", [128, 8, 512], F32))

        for nm in ("sx", "sw0", "sw1", "sxt0", "sxt1", "sro0", "sro1",
                   "sao0", "sao1", "sat0", "sat1", "st", "sv", "ss"):
            k.add_sem(nm, ctx.enter_context(nc.semaphore(nm)))

        T, V, Sc, Sy, G = "tensor", "vector", "scalar", "sync", "gpsimd"

        st8 = {
            "bank_free": [None] * 8,
            "ht_used_t": [0, 0],
            "sg_free_v": [0, 0],
            "wbuf_free_t": [0, 0],
            "wready": {},
            "yf_free": [None, None],
            "epi_dma": [[None] * NTL, [None] * NTL],
            "xt_free": [None, None],
            "ro_free": [None, None],
            "ao_free": [None, None],
            "at_free": [None, None],
            "scores_free_v": [0, 0],
            "eslot_t": [0] * 12,
            "rrs_free": None,
            "wgate": {},
        }

        pro_dma = 0
        for dst, src in ((idn, idn_c), (onesr, ones_c), (onescol, onescol_c),
                         (onescol_b, onescol_b_c), (cst, consts_c),
                         (graw, d["L0_ffn_gate_weight"]),
                         (gb_sb, d["L0_ffn_gate_bias"].rearrange("(a b) -> a b", a=1))):
            pro_dma = k.op(Sy, lambda E, dst=dst, src=src: E.dma_start(
                out=dst[:], in_=src), inc="sx", n=16)
        k.wait(T, "sx", pro_dma)
        k.wait(Sc, "sx", pro_dma)
        k.wait(V, "sx", pro_dma)
        for i in range(NTL):
            k.op(V, lambda E, i=i: E.tensor_copy(
                out=gb16[:, NEXP * i:NEXP * (i + 1)], in_=gb_sb[:]))
        k.op(V, lambda E: E.memset(gs8_all[:], -BIG), inc="sv")
        for kk in range(8):
            bb = kk % 2
            k.waitp(T, st8["bank_free"][bb])
            t_tr = k.op(T, lambda E, kk=kk, bb=bb: E.transpose(
                out=ps[:, bb, 0:32],
                in_=graw[0:32, 128 * kk:128 * (kk + 1)],
                identity=idn[0:32, 0:32]), inc="st")
            k.wait(V, "st", t_tr)
            v_tr = k.op(V, lambda E, kk=kk, bb=bb: E.tensor_copy(
                out=gwT[:, kk, :], in_=ps[:, bb, 0:32]), inc="sv")
            st8["bank_free"][bb] = ("sv", v_tr)
        k.waitp(T, st8["bank_free"][1])
        k.wait(T, "sv", k.ct["sv"])
        t_bc = k.op(T, lambda E: E.matmul(out=ps[:, 1, :], lhsT=onesr[:],
                                          rhs=gb16[:], start=True, stop=True), inc="st")
        k.wait(V, "st", t_bc)
        v_pro = k.op(V, lambda E: E.tensor_copy(out=biasbc16[:], in_=ps[:, 1, :]),
                     inc="sv")
        st8["bank_free"][0] = ("sv", v_pro)
        st8["bank_free"][1] = ("sv", v_pro)

        def ln_pass(src, ntiles, yT, slot0):
            for i in range(ntiles):
                t = i % 2
                sl = slot0 + i
                rows = 128 * sl
                k.waitp(Sy, st8["xt_free"][t])
                xdma = k.op(Sy, lambda E, t=t, rows=rows, src=src: E.dma_start(
                    out=xt2[t][:], in_=src[rows:rows + 128, :]), inc=f"sxt{t}", n=16)
                k.wait(V, f"sxt{t}", xdma)
                k.op(V, lambda E, t=t, sl=sl: E.tensor_scalar(
                    out=scrA[:], in0=xt2[t][:], scalar1=1.0, scalar2=0.0,
                    op0=ALU.mult, op1=ALU.add, accum_out=sum_sl[:, sl:sl + 1]))
                k.op(V, lambda E, t=t, sl=sl: E.scalar_tensor_tensor(
                    out=scrA[:], in0=xt2[t][:], scalar=1.0, in1=xt2[t][:],
                    op0=ALU.mult, op1=ALU.mult, accum_out=sumsq_sl[:, sl:sl + 1]))
                k.vop(lambda E, sl=sl: E.tensor_scalar(
                    out=mean_sl[:, sl:sl + 1], in0=sum_sl[:, sl:sl + 1],
                    scalar1=INV_D, scalar2=0.0, op0=ALU.mult, op1=ALU.add))
                k.vop(lambda E, sl=sl: E.tensor_scalar(
                    out=var_sl[:, sl:sl + 1], in0=sumsq_sl[:, sl:sl + 1],
                    scalar1=INV_D, scalar2=0.0, op0=ALU.mult, op1=ALU.add))
                k.vop(lambda E, sl=sl: E.tensor_mul(
                    out=st_tmp[:], in0=mean_sl[:, sl:sl + 1], in1=mean_sl[:, sl:sl + 1]))
                v_st = k.vop(lambda E, sl=sl: E.tensor_sub(
                    out=var_sl[:, sl:sl + 1], in0=var_sl[:, sl:sl + 1],
                    in1=st_tmp[:]))
                k.wait(Sc, "sv", v_st)
                s_sq = k.op(Sc, lambda E, sl=sl: E.activation(
                    out=std_sl[:, sl:sl + 1], in_=var_sl[:, sl:sl + 1], func=AF.Sqrt,
                    bias=cst[:, 0:1], scale=1.0), inc="ss")
                k.wait(V, "ss", s_sq)
                k.vop(lambda E, sl=sl: E.reciprocal(
                    out=rstd_sl[:, sl:sl + 1], in_=std_sl[:, sl:sl + 1]))
                v_nb = k.vop(lambda E, sl=sl: E.scalar_tensor_tensor(
                    out=nb_sl[:, sl:sl + 1], in0=mean_sl[:, sl:sl + 1], scalar=-1.0,
                    in1=rstd_sl[:, sl:sl + 1], op0=ALU.mult, op1=ALU.mult))
                k.wait(Sc, "sv", v_nb)
                k.waitp(Sc, st8["yf_free"][t])
                y_s = k.op(Sc, lambda E, t=t, sl=sl: E.activation(
                    out=yf2[t][:], in_=xt2[t][:], func=AF.Identity,
                    bias=nb_sl[:, sl:sl + 1], scale=rstd_sl[:, sl:sl + 1]), inc="ss")
                st8["xt_free"][t] = ("ss", y_s)
                k.wait(T, "ss", y_s)
                last_tr = 0
                for kk in range(8):
                    bb = kk % 2
                    k.waitp(T, st8["bank_free"][bb])
                    t_tr = k.op(T, lambda E, t=t, kk=kk, bb=bb: E.transpose(
                        out=ps[:, bb, 0:128], in_=yf2[t][:, 128 * kk:128 * (kk + 1)],
                        identity=idn[:]), inc="st")
                    last_tr = t_tr
                    k.wait(V, "st", t_tr)
                    v_ev = k.op(V, lambda E, kk=kk, bb=bb, i=i, yT=yT: E.tensor_copy(
                        out=yT[:, kk, 128 * i:128 * (i + 1)], in_=ps[:, bb, 0:128]),
                        inc="sv")
                    st8["bank_free"][bb] = ("sv", v_ev)
                st8["yf_free"][t] = ("st", last_tr)
            return k.ct["sv"]

        def routing_pass(yT, yT_ready):
            k.op(V, lambda E: E.memset(gs8_all[:, :, 4:8], -BIG))
            sig = [0, 0]
            for half in range(2):
                bb = 2 + half
                k.wait(T, "sv", yT_ready)
                k.waitp(T, st8["bank_free"][bb])
                t_lg = None
                for i2 in range(8):
                    i = 8 * half + i2
                    for kk in range(8):
                        t_lg = k.op(T, lambda E, kk=kk, i=i, i2=i2, bb=bb, yT=yT:
                                    E.matmul(
                                        out=ps[:, bb, NEXP * i2:NEXP * (i2 + 1)],
                                        lhsT=yT[:, kk, 128 * i:128 * (i + 1)],
                                        rhs=gwT[:, kk, :],
                                        start=(kk == 0), stop=(kk == 7)),
                                    inc="st" if (kk == 7 and i2 == 7) else None)
                k.wait(Sc, "st", t_lg)
                sig[half] = k.op(Sc, lambda E, half=half, bb=bb: E.activation(
                    out=scores_all[:, 256 * half:256 * (half + 1)],
                    in_=ps[:, bb, 0:256], func=AF.Sigmoid), inc="ss")
                st8["bank_free"][bb] = ("ss", sig[half])
            k.wait(V, "ss", sig[1])
            k.vop(lambda E: E.tensor_add(out=sc_all[:], in0=scores_all[:],
                                         in1=biasbc16[:]))
            lvl = None
            for g in range(4 * NTL):
                lvl = k.op(V, lambda E, g=g: E.max(
                    out=m8g_all[:, g, :], in_=sc_all[:, 8 * g:8 * (g + 1)]),
                    inc="sv" if g == 4 * NTL - 1 else None)
            k.wait(V, "sv", lvl)
            for i in range(NTL):
                g3 = gs8_all[:, i, 0:4].rearrange("p (a b) -> p a b", b=1)
                lvl = k.op(V, lambda E, g3=g3, i=i: E.tensor_add(
                    out=g3, in0=m8g_all[:, 4 * i:4 * i + 4, 0:1],
                    in1=m8g_all[:, 4 * i:4 * i + 4, 1:2]),
                    inc="sv" if i == NTL - 1 else None)
            k.wait(V, "sv", lvl)
            for i in range(NTL):
                lvl = k.op(V, lambda E, i=i: E.max(
                    out=m8a_all[:, i, :], in_=gs8_all[:, i, :]),
                    inc="sv" if i == NTL - 1 else None)
            k.wait(V, "sv", lvl)
            for i in range(NTL):
                lvl = k.op(V, lambda E, i=i: E.tensor_scalar(
                    out=pen_all[:, i, :], in0=gs8_all[:, i, 0:4],
                    scalar1=m8a_all[:, i, 1:2], scalar2=-BIG,
                    op0=ALU.is_lt, op1=ALU.mult),
                    inc="sv" if i == NTL - 1 else None)
            k.wait(V, "sv", lvl)
            for i in range(NTL):
                for g in range(4):
                    c0 = NEXP * i + 8 * g
                    lvl = k.op(V, lambda E, c0=c0, i=i, g=g: E.tensor_scalar_add(
                        out=sc_all[:, c0:c0 + 8], in0=sc_all[:, c0:c0 + 8],
                        scalar1=pen_all[:, i, g:g + 1]),
                        inc="sv" if (i == NTL - 1 and g == 3) else None)
            k.wait(V, "sv", lvl)
            for i in range(NTL):
                lvl = k.op(V, lambda E, i=i: E.max(
                    out=m8s_all[:, i, :], in_=sc_all[:, NEXP * i:NEXP * (i + 1)]),
                    inc="sv" if i == NTL - 1 else None)
            k.wait(V, "sv", lvl)
            for i in range(NTL):
                lvl = k.op(V, lambda E, i=i: E.tensor_single_scalar(
                    out=w6_all[:, NEXP * i:NEXP * (i + 1)],
                    in_=sc_all[:, NEXP * i:NEXP * (i + 1)],
                    scalar=m8s_all[:, i, 5:6], op=ALU.is_ge),
                    inc="sv" if i == NTL - 1 else None)
            k.wait(V, "sv", lvl)
            k.vop(lambda E: E.tensor_mul(out=w6_all[:], in0=w6_all[:],
                                         in1=scores_all[:]))
            w6v3 = w6_all[:].rearrange("p (a b) -> p a b", b=NEXP)
            k.vop(lambda E: E.reduce_sum(out=ssum_all[:], in_=w6v3, axis=AX.X))
            k.vop(lambda E: E.tensor_scalar_add(out=rec_all[:], in0=ssum_all[:],
                                                scalar1=1e-20))
            k.vop(lambda E: E.reciprocal(out=ssum_all[:], in_=rec_all[:]))
            k.vop(lambda E: E.tensor_scalar_mul(out=rec_all[:], in0=ssum_all[:],
                                                scalar1=0.2))
            for i in range(NTL):
                lvl = k.op(V, lambda E, i=i: E.tensor_scalar_mul(
                    out=cw[:, i, :], in0=w6_all[:, NEXP * i:NEXP * (i + 1)],
                    scalar1=rec_all[:, i, :]),
                    inc="sv" if i == NTL - 1 else None)
            sc4p = sc_all[:].rearrange("p (a b c) -> p a b c", b=16, c=2)
            for i in range(NTL):
                pm3 = pm_all[:, i, :].rearrange("p (a b) -> p a b", b=1)
                lvl2 = k.op(V, lambda E, pm3=pm3, i=i: E.tensor_tensor(
                    out=pm3, in0=sc4p[:, i, :, 0:1], in1=sc4p[:, i, :, 1:2],
                    op=ALU.max), inc="sv" if i == NTL - 1 else None)
            k.wait(V, "sv", max(lvl, lvl2))
            for i in range(NTL):
                lvl = k.op(V, lambda E, i=i: E.tensor_scalar(
                    out=sel[:, i, :], in0=pm_all[:, i, :],
                    scalar1=m8s_all[:, i, 0:1], scalar2=0.1,
                    op0=ALU.is_ge, op1=ALU.mult),
                    inc="sv" if i == NTL - 1 else None)
            k.wait(V, "sv", lvl)
            return k.ct["sv"]

        def unit_weight_dma(gu, after_t):
            par = gu % 2
            u = gu % NUNIT
            k.wait(G, "st", after_t)
            if u < NEXP:
                srcs = [(wg_b[par][:], d["L0_ffn_w_gate"][u]),
                        (wu_b[par][:], d["L0_ffn_w_up"][u]),
                        (wd_b[par][:], d["L0_ffn_w_down"][u])]
            elif u == NEXP:
                srcs = [(wg_b[par][:], d["L0_ffn_shared_w_gate"]),
                        (wu_b[par][:], d["L0_ffn_shared_w_up"]),
                        (wd_b[par][:], d["L0_ffn_shared_w_down"])]
            else:
                l = u - NEXP - 1
                srcs = [(wg_b[par][:, :, 0:AH], d["L0_ffn_anchor_w_gate"][l]),
                        (wu_b[par][:, :, 0:AH], d["L0_ffn_anchor_w_up"][l]),
                        (wd_b[par][:, 0:2, :], d["L0_ffn_anchor_w_down"][l])]
            last = 0
            for dst, src in srcs:
                src = src.rearrange("(a p) h -> p a h", p=128)
                last = k.op(G, lambda E, dst=dst, src=src: E.dma_start(
                    out=dst, in_=src), inc=f"sw{par}", n=16)
            st8["wready"][gu] = last

        def gu_quarters(yT, w, par_step, par_u, nq, yT_ready, wrdy):
            htr = 0
            for q in range(nq):
                k.waitp(T, st8["bank_free"][2 * q])
                k.waitp(T, st8["bank_free"][2 * q + 1])
                k.wait(T, f"sw{par_u}", wrdy)
                k.wait(T, "sv", yT_ready)

                def t_gu(E, q=q, w=w, par_u=par_u, yT=yT):
                    for kk in range(8):
                        E.matmul(ps[:, 2 * q, :],
                                 lhsT=wg_b[par_u][:, kk, 128 * q:128 * (q + 1)],
                                 rhs=yT[:, kk, WT * w:WT * (w + 1)],
                                 start=(kk == 0), stop=(kk == 7))
                    ins = None
                    for kk in range(8):
                        ins = E.matmul(ps[:, 2 * q + 1, :],
                                       lhsT=wu_b[par_u][:, kk, 128 * q:128 * (q + 1)],
                                       rhs=yT[:, kk, WT * w:WT * (w + 1)],
                                       start=(kk == 0), stop=(kk == 7))
                    return ins
                t_q = k.op(T, t_gu, inc="st")
                k.wait(Sc, "st", t_q)
                k.wait(Sc, "sv", st8["sg_free_v"][q % 2])
                s_q = k.op(Sc, lambda E, q=q: E.activation(
                    out=sg2[q % 2][:], in_=ps[:, 2 * q, :], func=AF.Silu), inc="ss")
                st8["bank_free"][2 * q] = ("ss", s_q)
                k.wait(V, "ss", s_q)
                k.wait(V, "st", max(t_q, st8["ht_used_t"][par_step]))
                v_q = k.op(V, lambda E, q=q, par_step=par_step: E.tensor_mul(
                    out=ht2[par_step][:, q, :], in0=sg2[q % 2][:],
                    in1=ps[:, 2 * q + 1, :]), inc="sv")
                htr = v_q
                st8["bank_free"][2 * q + 1] = ("sv", v_q)
                st8["sg_free_v"][q % 2] = v_q
            return htr

        def down_unit(w, par_step, par_u, nk, ht_ready, cw_src):
            last_t = 0
            for r in range(8):
                tt, j = r // 2, r % 2
                k.waitp(T, st8["bank_free"][r])
                k.wait(T, "sv", ht_ready)

                def t_dn(E, r=r, tt=tt, j=j, par_step=par_step, par_u=par_u, nk=nk):
                    ins = None
                    for kk in range(nk):
                        ins = E.matmul(ps[:, r, :],
                                       lhsT=ht2[par_step][:, kk, 128 * tt:128 * (tt + 1)],
                                       rhs=wd_b[par_u][:, kk, WT * j:WT * (j + 1)],
                                       start=(kk == 0), stop=(kk == nk - 1))
                    return ins
                t_r = k.op(T, t_dn, inc="st")
                last_t = t_r
                k.wait(V, "st", t_r)
                tile = 4 * w + tt
                sc_ap = cw_src(tile)
                v_r = k.op(V, lambda E, r=r, tile=tile, j=j, sc_ap=sc_ap:
                           E.scalar_tensor_tensor(
                               out=acc[:, tile, WT * j:WT * (j + 1)], in0=ps[:, r, :],
                               scalar=sc_ap, in1=acc[:, tile, WT * j:WT * (j + 1)],
                               op0=ALU.mult, op1=ALU.add), inc="sv")
                st8["bank_free"][r] = ("sv", v_r)
            return last_t

        def cw_source(u):
            if u < NEXP:
                return lambda tile, u=u: cw[:, tile, u:u + 1]
            if u == NEXP:
                return lambda tile: 1.0
            l = u - NEXP - 1
            return lambda tile, l=l: sel[:, tile, l:l + 1]

        def ln_stats_piece(src, sl, rows, gate=None):
            t = sl % 2
            k.waitp(Sy, gate)
            k.waitp(Sy, st8["xt_free"][t])
            xdma = k.op(Sy, lambda E, t=t, rows=rows, src=src: E.dma_start(
                out=xt2[t][:], in_=src[rows:rows + 128, :]), inc=f"sxt{t}", n=16)
            k.wait(V, f"sxt{t}", xdma)
            k.op(V, lambda E, t=t, sl=sl: E.tensor_scalar(
                out=scrA[:], in0=xt2[t][:], scalar1=1.0, scalar2=0.0,
                op0=ALU.mult, op1=ALU.add, accum_out=sum_sl[:, sl:sl + 1]))
            v_s = k.op(V, lambda E, t=t, sl=sl: E.scalar_tensor_tensor(
                out=scrA[:], in0=xt2[t][:], scalar=1.0, in1=xt2[t][:],
                op0=ALU.mult, op1=ALU.mult,
                accum_out=sumsq_sl[:, sl:sl + 1]), inc="sv")
            st8["xt_free"][t] = ("sv", v_s)

        def ln_stats_sbuf(src_sb, sl):
            k.op(V, lambda E, src_sb=src_sb, sl=sl: E.tensor_scalar(
                out=scrA[:], in0=src_sb[:], scalar1=1.0, scalar2=0.0,
                op0=ALU.mult, op1=ALU.add, accum_out=sum_sl[:, sl:sl + 1]))
            k.op(V, lambda E, src_sb=src_sb, sl=sl: E.scalar_tensor_tensor(
                out=scrA[:], in0=src_sb[:], scalar=1.0, in1=src_sb[:],
                op0=ALU.mult, op1=ALU.mult,
                accum_out=sumsq_sl[:, sl:sl + 1]), inc="sv")

        def ln_finish(src, ntiles, yT, slot0):
            sl_ = slice(slot0, slot0 + ntiles)
            k.vop(lambda E, sl_=sl_: E.tensor_scalar(
                out=mean_sl[:, sl_], in0=sum_sl[:, sl_],
                scalar1=INV_D, scalar2=0.0, op0=ALU.mult, op1=ALU.add))
            k.vop(lambda E, sl_=sl_: E.tensor_scalar(
                out=var_sl[:, sl_], in0=sumsq_sl[:, sl_],
                scalar1=INV_D, scalar2=0.0, op0=ALU.mult, op1=ALU.add))
            k.vop(lambda E, sl_=sl_: E.tensor_mul(
                out=std_sl[:, sl_], in0=mean_sl[:, sl_], in1=mean_sl[:, sl_]))
            v_st = k.vop(lambda E, sl_=sl_: E.tensor_sub(
                out=var_sl[:, sl_], in0=var_sl[:, sl_], in1=std_sl[:, sl_]))
            k.wait(Sc, "sv", v_st)
            s_sq = k.op(Sc, lambda E, slot0=slot0, ntiles=ntiles: E.activation(
                out=std_sl[:, slot0:slot0 + ntiles],
                in_=var_sl[:, slot0:slot0 + ntiles],
                func=AF.Sqrt, bias=cst[:, 0:1], scale=1.0), inc="ss")
            k.wait(V, "ss", s_sq)
            k.vop(lambda E, slot0=slot0, ntiles=ntiles: E.reciprocal(
                out=rstd_sl[:, slot0:slot0 + ntiles],
                in_=std_sl[:, slot0:slot0 + ntiles]))
            k.vop(lambda E, slot0=slot0, ntiles=ntiles: E.scalar_tensor_tensor(
                out=nb_sl[:, slot0:slot0 + ntiles],
                in0=mean_sl[:, slot0:slot0 + ntiles], scalar=-1.0,
                in1=rstd_sl[:, slot0:slot0 + ntiles], op0=ALU.mult, op1=ALU.mult))
            for i in range(ntiles):
                t = i % 2
                sl = slot0 + i
                rows = 128 * sl
                k.waitp(Sy, st8["xt_free"][t])
                xdma = k.op(Sy, lambda E, t=t, rows=rows, src=src: E.dma_start(
                    out=xt2[t][:], in_=src[rows:rows + 128, :]), inc=f"sxt{t}", n=16)
                k.wait(Sc, f"sxt{t}", xdma)
                k.wait(Sc, "sv", k.ct["sv"])
                k.waitp(Sc, st8["yf_free"][t])
                y_s = k.op(Sc, lambda E, t=t, sl=sl: E.activation(
                    out=yf2[t][:], in_=xt2[t][:], func=AF.Identity,
                    bias=nb_sl[:, sl:sl + 1], scale=rstd_sl[:, sl:sl + 1]),
                    inc="ss")
                st8["xt_free"][t] = ("ss", y_s)
                k.wait(T, "ss", y_s)
                last_tr = 0
                for kk in range(8):
                    bb = kk % 2
                    k.waitp(T, st8["bank_free"][bb])
                    t_tr = k.op(T, lambda E, t=t, kk=kk, bb=bb: E.transpose(
                        out=ps[:, bb, 0:128], in_=yf2[t][:, 128 * kk:128 * (kk + 1)],
                        identity=idn[:]), inc="st")
                    last_tr = t_tr
                    k.wait(V, "st", t_tr)
                    v_ev = k.op(V, lambda E, kk=kk, bb=bb, i=i, yT=yT: E.tensor_copy(
                        out=yT[:, kk, 128 * i:128 * (i + 1)], in_=ps[:, bb, 0:128]),
                        inc="sv")
                    st8["bank_free"][bb] = ("sv", v_ev)
                st8["yf_free"][t] = ("st", last_tr)
            return k.ct["sv"]

        def ffn_phase(src, dst, gu0, extra_pieces=None, first_prepped=False,
                      want_stats=False):
            gu = gu0
            extra = list(extra_pieces or [])
            for blk in range(NBLK):
                base = blk * BLKT
                if blk == 0 and not first_prepped:
                    yT_ready = ln_pass(src, NTL, yT_blk, slot0=blk * NTL)
                elif blk == 0:
                    yT_ready = ln_finish(src, NTL, yT_blk, slot0=0)
                else:
                    yT_ready = ln_finish(src, NTL, yT_blk, slot0=blk * NTL)
                routing_pass(yT_blk, yT_ready)
                k.op(V, lambda E: E.memset(acc[:], 0.0), inc="sv")
                stats_left = list(range(NTL)) if blk + 1 < NBLK else []
                prev = None
                step = 0
                for u in range(NUNIT):
                    if stats_left and u % 3 == 2:
                        i2 = stats_left.pop(0)
                        ln_stats_piece(src, (blk + 1) * NTL + i2,
                                       (blk + 1) * BLKT + 128 * i2)
                    elif blk + 1 == NBLK and extra and u % 3 == 2:
                        extra.pop(0)()
                    nq = 4 if u <= NEXP else 2
                    nk = 4 if u <= NEXP else 2
                    par_u = gu % 2
                    if gu == gu0 and blk == 0:
                        unit_weight_dma(gu, after_t=st8["wbuf_free_t"][gu % 2])
                        unit_weight_dma(gu + 1, after_t=st8["wbuf_free_t"][(gu + 1) % 2])
                    wrdy = st8["wready"][gu]
                    for w in range(NWIN):
                        par_step = step % 2
                        htr = gu_quarters(yT_blk, w, par_step, par_u, nq, yT_ready, wrdy)
                        if prev is not None:
                            lt = down_unit(*prev)
                            st8["ht_used_t"][prev[1]] = lt
                            if prev[0] == NWIN - 1:
                                st8["wbuf_free_t"][(gu - 1) % 2] = lt
                                nxt = gu + 1
                                if nxt < gu0 + NBLK * NUNIT and nxt not in st8["wready"]:
                                    unit_weight_dma(nxt, after_t=st8["wbuf_free_t"][nxt % 2])
                        prev = (w, par_step, par_u, nk, htr, cw_source(u))
                        step += 1
                    gu += 1
                lt = down_unit(*prev)
                st8["ht_used_t"][prev[1]] = lt
                st8["wbuf_free_t"][(gu - 1) % 2] = lt
                nxt = gu + 1
                if nxt < gu0 + NBLK * NUNIT and nxt not in st8["wready"]:
                    unit_weight_dma(nxt, after_t=st8["wbuf_free_t"][nxt % 2])
                prev = None
                while stats_left:
                    i2 = stats_left.pop(0)
                    ln_stats_piece(src, (blk + 1) * NTL + i2,
                                   (blk + 1) * BLKT + 128 * i2)
                if blk + 1 == NBLK:
                    while extra:
                        extra.pop(0)()
                k.wait(V, "sv", k.ct["sv"])
                for i in range(NTL):
                    t = i % 2
                    rows = base + 128 * i
                    k.waitp(Sy, st8["xt_free"][t])
                    xdma = k.op(Sy, lambda E, t=t, rows=rows, src=src: E.dma_start(
                        out=xt2[t][:], in_=src[rows:rows + 128, :]), inc=f"sxt{t}", n=16)
                    k.wait(V, f"sxt{t}", xdma)
                    k.waitp(V, st8["ro_free"][t])
                    v_e = k.op(V, lambda E, t=t, i=i: E.scalar_tensor_tensor(
                        out=ro2[t][:], in0=acc[:, i, :], scalar=0.5, in1=xt2[t][:],
                        op0=ALU.mult, op1=ALU.add), inc="sv")
                    st8["xt_free"][t] = ("sv", v_e)
                    k.wait(Sc, "sv", v_e)
                    odma = k.op(Sc, lambda E, t=t, rows=rows, dst=dst: E.dma_start(
                        out=dst[rows:rows + 128, :], in_=ro2[t][:]), inc=f"sro{t}", n=16)
                    st8["ro_free"][t] = (f"sro{t}", odma)
                    st8["epi_dma"][blk % 2][i] = (f"sro{t}", odma)
                    if want_stats:
                        ln_stats_sbuf(ro2[t], base // 128 + i)
            return gu

        def headnorm_sq(srcT, pb, dst01):
            t_last_prev = st8.get("qsq_free_t", 0)
            for half in range(2):
                colsl = slice(2048 * half, 2048 * (half + 1))
                k.wait(V, "st", t_last_prev)
                v_sq = k.op(V, lambda E, colsl=colsl, pb=pb, srcT=srcT: E.tensor_mul(
                    out=qsq[pb:pb + 64, :], in0=srcT[pb:pb + 64, colsl],
                    in1=srcT[pb:pb + 64, colsl]), inc="sv")
                k.wait(T, "sv", v_sq)
                t_n = 0
                for c in range(4):
                    k.waitp(T, st8["bank_free"][4 + c])
                    t_n = k.op(T, lambda E, c=c, pb=pb: E.matmul(
                        out=ps[0:1, 4 + c, :], lhsT=onescol_b[pb:pb + 64, :],
                        rhs=qsq[pb:pb + 64, WT * c:WT * (c + 1)],
                        start=True, stop=True), inc="st")
                t_last_prev = t_n
                k.wait(V, "st", t_n)
                tgt = st_tmp[0:1, 0:1] if half == 0 else dst01[:]
                v_red = k.vop(lambda E, tgt=tgt: E.tensor_reduce(
                    out=tgt, in_=ps[0:1, 4:8, :], axis=AX.XY, op=ALU.max))
                for c in range(4):
                    st8["bank_free"][4 + c] = ("sv", v_red)
            st8["qsq_free_t"] = t_last_prev
            k.vop(lambda E, dst01=dst01: E.tensor_tensor(
                out=dst01[:], in0=dst01[:], in1=st_tmp[0:1, 0:1], op=ALU.max))

        def attention_phase():
            mask_dma = k.op(Sy, lambda E: E.dma_start(out=masks[:], in_=masks_c),
                            inc="sx", n=16)
            k.wait(V, "sx", mask_dma)
            yT_ready = ln_finish(dr1, 32, yT_full, slot0=0)
            v_ones = k.op(V, lambda E: E.memset(v2p[:, :, :, 64:65], 1.0), inc="sv")
            last_qkv_t = 0
            for hp in range(8):
                k.wait(G, "st", last_qkv_t)
                if hp == 1:
                    st8["wo_dma"] = k.op(G, lambda E: E.dma_start(
                        out=wo_sb[:],
                        in_=d["L0_attn_Wo"].rearrange("(a p) h -> p a h", p=128)),
                        inc="sw1", n=16)
                wsl_last = 0
                for dstw, nm in ((wq_sl, "L0_attn_Wq"), (wk_sl, "L0_attn_Wk"),
                                 (wv_sl, "L0_attn_Wv")):
                    srcw = d[nm][:, 128 * hp:128 * (hp + 1)].rearrange(
                        "(a p) c -> p a c", p=128)
                    wsl_last = k.op(G, lambda E, dstw=dstw, srcw=srcw: E.dma_start(
                        out=dstw[:], in_=srcw), inc="sw0", n=16)
                for wsl, dstt, scale in ((wq_sl, qT2, 0.125), (wk_sl, kT2, 1.0)):
                    for w8 in range(8):
                        bb = 6 + w8 % 2
                        k.waitp(T, st8["bank_free"][bb])
                        k.wait(T, "sw0", wsl_last)
                        k.wait(T, "sv", yT_ready)

                        def t_q(E, wsl=wsl, w8=w8, bb=bb):
                            ins = None
                            for kk in range(8):
                                ins = E.matmul(ps[:, bb, :], lhsT=wsl[:, kk, :],
                                               rhs=yT_full[:, kk, WT * w8:WT * (w8 + 1)],
                                               start=(kk == 0), stop=(kk == 7))
                            return ins
                        t_w = k.op(T, t_q, inc="st")
                        k.wait(Sc, "st", t_w)
                        scap = cst[:, 1:2] if scale == 0.125 else 1.0
                        s_w = k.op(Sc, lambda E, dstt=dstt, w8=w8, bb=bb, scap=scap:
                                   E.activation(
                                       out=dstt[:, WT * w8:WT * (w8 + 1)],
                                       in_=ps[:, bb, :], func=AF.Identity,
                                       scale=scap), inc="ss")
                        st8["bank_free"][bb] = ("ss", s_w)
                for tt in range(32):
                    bb = 6 + tt % 2
                    k.waitp(T, st8["bank_free"][bb])
                    k.wait(T, "sw0", wsl_last)
                    k.wait(T, "sv", v_ones)

                    def t_v(E, tt=tt, bb=bb):
                        ins = None
                        for kk in range(8):
                            ins = E.matmul(ps[:, bb, 0:128],
                                           lhsT=yT_full[:, kk, 128 * tt:128 * (tt + 1)],
                                           rhs=wv_sl[:, kk, :],
                                           start=(kk == 0), stop=(kk == 7))
                        return ins
                    t_w = k.op(T, t_v, inc="st")
                    k.wait(V, "st", t_w)
                    v_w = k.op(V, lambda E, tt=tt, bb=bb: E.tensor_copy(
                        out=v2p[:, tt, :, 0:64],
                        in_=ps[:, bb, 0:128].rearrange("p (h c) -> p h c", h=2)),
                        inc="sv")
                    st8["bank_free"][bb] = ("sv", v_w)
                qkv_done_v = k.ct["sv"]
                last_qkv_t = k.ct["st"]
                k.wait(V, "ss", k.ct["ss"])
                headnorm_sq(qT2, 0, csq)
                headnorm_sq(kT2, 0, csc)
                headnorm_sq(qT2, 64, csq2)
                headnorm_sq(kT2, 64, csc2)
                k.vop(lambda E: E.tensor_mul(out=csq[:], in0=csq[:], in1=csc[:]))
                k.vop(lambda E: E.tensor_mul(out=csq2[:], in0=csq2[:], in1=csc2[:]))
                v_c = k.vop(lambda E: E.tensor_tensor(
                    out=csq[:], in0=csq[:], in1=csq2[:], op=ALU.max))
                k.wait(Sc, "sv", v_c)
                s_ln = k.sop(lambda E: E.activation(
                    out=csc[:], in_=csq[:], func=AF.Ln))
                s_c = k.sop(lambda E: E.activation(
                    out=csq[:], in_=csc[:], func=AF.Exp, scale=cst[0:1, 2:3]))
                k.wait(V, "ss", s_c)
                v_c2 = k.vop(lambda E: E.tensor_scalar(
                    out=csc[:], in0=csq[:], scalar1=-1.0, scalar2=-0.1,
                    op0=ALU.mult, op1=ALU.add))
                k.wait(T, "sv", v_c2)
                k.waitp(T, st8["bank_free"][6])
                t_nc = k.op(T, lambda E: E.matmul(
                    out=ps[:, 6, 0:1], lhsT=onesr[:], rhs=csc[:],
                    start=True, stop=True), inc="st")
                k.wait(V, "st", t_nc)
                v_nc = k.vop(lambda E: E.tensor_copy(
                    out=negc2[0][:], in_=ps[:, 6, 0:1]))
                st8["bank_free"][6] = ("sv", v_nc)
                negc_v = k.ct["sv"]
                def normalize_store(w, t_fl, hp=hp):
                    for h01 in range(2):
                        head = 2 * hp + h01
                        ob = 4 + h01
                        k.wait(V, "st", t_fl)
                        k.waitp(V, st8["rrs_free"])
                        v_rr = k.vop(lambda E, ob=ob: E.reciprocal(
                            out=rrs[:], in_=ps[64:65, ob, :]))
                        k.wait(T, "sv", v_rr)
                        k.waitp(T, st8["bank_free"][6])
                        t_b2 = k.op(T, lambda E: E.matmul(
                            out=ps[0:64, 6, :], lhsT=onesr[0:1, 0:64], rhs=rrs[:],
                            start=True, stop=True), inc="st")
                        st8["rrs_free"] = ("st", t_b2)
                        k.wait(V, "st", t_b2)
                        v_rb = k.vop(lambda E: E.tensor_copy(
                            out=rb_sb[:], in_=ps[0:64, 6, :]))
                        st8["bank_free"][6] = ("sv", v_rb)
                        tb = h01
                        k.waitp(V, st8["ao_free"][tb])
                        v_ao = k.op(V, lambda E, tb=tb, ob=ob: E.tensor_mul(
                            out=ao2[tb][:], in0=ps[0:64, ob, :], in1=rb_sb[:]),
                            inc="sv")
                        st8["bank_free"][ob] = ("sv", v_ao)
                        k.wait(Sy, "sv", v_ao)
                        pr0 = 64 * (head % 2)
                        dst_ap = dat[4 * w:4 * (w + 1), pr0:pr0 + 64, head // 2, :]\
                            .rearrange("t r c -> r t c")
                        src_ap = ao2[tb][:].rearrange("r (t c) -> r t c", c=128)
                        o_dma = k.op(Sy, lambda E, dst_ap=dst_ap, src_ap=src_ap:
                                     E.dma_start(out=dst_ap, in_=src_ap),
                                     inc=f"sao{tb}", n=16)
                        st8["ao_free"][tb] = (f"sao{tb}", o_dma)
                    if hp == 7:
                        st8["wgate"][w] = (k.ct["sao0"], k.ct["sao1"])
                for w in range(8):
                    nkt = 4 * w + 4
                    nb2 = nkt // 2
                    exp_k = [[0] * nkt, [0] * nkt]
                    msk_v = [[0] * nkt, [0] * nkt]

                    def slot(h01, kt):
                        return 6 * h01 + 2 * ((kt // 2) % 3) + kt % 2

                    def emit_scores(b2, w=w):
                        out_t = [[0, 0], [0, 0]]
                        for kt01 in range(2):
                            kt = 2 * b2 + kt01
                            for h01 in range(2):
                                bb = 2 * h01 + kt01
                                pb = 64 * h01
                                if b2 >= 1:
                                    k.wait(T, "ss", exp_k[h01][kt - 2])
                                else:
                                    k.waitp(T, st8["bank_free"][bb])
                                k.wait(T, "sv", qkv_done_v)
                                out_t[h01][kt01] = k.op(
                                    T, lambda E, kt=kt, bb=bb, pb=pb, w=w: E.matmul(
                                        ps[:, bb, :],
                                        lhsT=kT2[pb:pb + 64, 128 * kt:128 * (kt + 1)],
                                        rhs=qT2[pb:pb + 64, WT * w:WT * (w + 1)],
                                        start=True, stop=True), inc="st")
                        return out_t

                    def emit_exp(h01, b2, sc_t2, w=w):
                        for kt01 in range(2):
                            kt = 2 * b2 + kt01
                            bb = 2 * h01 + kt01
                            sl = slot(h01, kt)
                            k.wait(Sc, "st", sc_t2[kt01])
                            k.wait(Sc, "sv", negc_v)
                            k.wait(Sc, "st", st8["eslot_t"][sl])
                            s_e = k.op(Sc, lambda E, bb=bb, sl=sl: E.activation(
                                out=expT[:, sl, :], in_=ps[:, bb, :],
                                func=AF.Exp, bias=negc2[0][:], scale=1.0), inc="ss")
                            exp_k[h01][kt] = s_e
                            st8["bank_free"][bb] = ("ss", s_e)
                            o = 128 * kt - 512 * w
                            if o >= 0:
                                mi = o // 128
                                k.wait(V, "ss", s_e)
                                msk_v[h01][kt] = k.op(V, lambda E, sl=sl, mi=mi:
                                                      E.tensor_mul(
                                                          out=expT[:, sl, :],
                                                          in0=expT[:, sl, :],
                                                          in1=masks[:, mi, :]),
                                                      inc="sv")

                    def emit_attnv(b2, w=w, nkt=nkt):
                        for h01 in range(2):
                            ob = 4 + h01
                            for kt in (2 * b2, 2 * b2 + 1):
                                sl = slot(h01, kt)
                                k.wait(T, "ss", exp_k[h01][kt])
                                if msk_v[h01][kt]:
                                    k.wait(T, "sv", msk_v[h01][kt])
                                if kt == 0:
                                    k.waitp(T, st8["bank_free"][ob])
                                t_a = k.op(T, lambda E, kt=kt, h01=h01, nkt=nkt, ob=ob, sl=sl:
                                           E.matmul(
                                               ps[0:65, ob, :], lhsT=v2p[:, kt, h01, :],
                                               rhs=expT[:, sl, :],
                                               start=(kt == 0), stop=(kt == nkt - 1)),
                                           inc="st")
                                st8["eslot_t"][sl] = t_a

                    for b2 in range(nb2):
                        sc_t2 = emit_scores(b2)
                        if b2 >= 2:
                            emit_attnv(b2 - 2)
                        emit_exp(0, b2, sc_t2[0])
                        emit_exp(1, b2, sc_t2[1])
                        if b2 == 1 and st8.get("pend_norm"):
                            st8["pend_norm"]()
                            st8["pend_norm"] = None
                    if st8.get("pend_norm"):
                        st8["pend_norm"]()
                        st8["pend_norm"] = None
                    for b2 in range(max(nb2 - 2, 0), nb2):
                        emit_attnv(b2)
                    t_fl = k.ct["st"]

                    def norm_emit(w=w, t_fl=t_fl):
                        normalize_store(w, t_fl)
                    if w == 7:
                        norm_emit()
                    else:
                        st8["pend_norm"] = norm_emit

            wo_last = st8["wo_dma"]
            r2dst = dout if debug_stop == "B" else dr2
            for i in range(32):
                t = i % 2
                rows = 128 * i
                g0, g1 = st8["wgate"][i // 4]
                k.wait(Sy, "sao0", g0)
                k.wait(Sy, "sao1", g1)
                k.waitp(Sy, st8["at_free"][t])
                a_dma = k.op(Sy, lambda E, t=t, i=i: E.dma_start(
                    out=at_sb[t][:], in_=dat[i]), inc=f"sat{t}", n=16)
                k.waitp(Sy, st8["xt_free"][t])
                x_dma = k.op(Sy, lambda E, t=t, rows=rows: E.dma_start(
                    out=xt2[t][:], in_=dr1[rows:rows + 128, :]), inc=f"sxt{t}", n=16)
                t_half = [0, 0]
                for j in range(2):
                    bb = 6 + j
                    k.waitp(T, st8["bank_free"][bb])
                    k.wait(T, f"sat{t}", a_dma)
                    k.wait(T, "sw1", wo_last)

                    def t_wo(E, t=t, j=j, bb=bb):
                        ins = None
                        for kk in range(8):
                            ins = E.matmul(ps[:, bb, :], lhsT=at_sb[t][:, kk, :],
                                           rhs=wo_sb[:, kk, WT * j:WT * (j + 1)],
                                           start=(kk == 0), stop=(kk == 7))
                        return ins
                    t_half[j] = k.op(T, t_wo, inc="st")
                st8["at_free"][t] = ("st", t_half[1])
                k.wait(Sc, f"sxt{t}", x_dma)
                k.waitp(Sc, st8["yf_free"][t])
                s_y = k.op(Sc, lambda E, t=t, i=i: E.activation(
                    out=yf2[t][:], in_=xt2[t][:], func=AF.Identity,
                    bias=nb_sl[:, i:i + 1], scale=rstd_sl[:, i:i + 1]), inc="ss")
                st8["xt_free"][t] = ("ss", s_y)
                k.wait(V, "ss", s_y)
                v_t1 = k.op(V, lambda E, t=t: E.tensor_add(
                    out=scrA[:], in0=yf2[t][:], in1=xt2[t][:]), inc="sv")
                st8["yf_free"][t] = ("sv", v_t1)
                k.wait(V, "sv", v_t1)
                k.waitp(V, st8["ro_free"][t])
                v_r = 0
                for j in range(2):
                    k.wait(V, "st", t_half[j])
                    v_r = k.op(V, lambda E, t=t, j=j: E.tensor_add(
                        out=ro2[t][:, WT * j:WT * (j + 1)],
                        in0=scrA[:, WT * j:WT * (j + 1)], in1=ps[:, 6 + j, :]),
                        inc="sv")
                    st8["bank_free"][6 + j] = ("sv", v_r)
                k.wait(Sc, "sv", v_r)
                o_dma = k.op(Sc, lambda E, t=t, rows=rows, r2dst=r2dst: E.dma_start(
                    out=r2dst[rows:rows + 128, :], in_=ro2[t][:]), inc=f"sro{t}", n=16)
                st8["ro_free"][t] = (f"sro{t}", o_dma)
                if debug_stop is None and i < 16:
                    ln_stats_sbuf(ro2[t], i)

        if debug_stop == "R":
            yT_ready = ln_pass(d["x"], NTL, yT_blk, slot0=0)
            rv = routing_pass(yT_blk, yT_ready)
            k.wait(V, "sv", rv)
            k.op(V, lambda E: E.tensor_copy(
                out=scrA[:, 0:512], in_=cw[:].rearrange("p a b -> p (a b)")))
            k.op(V, lambda E: E.tensor_copy(
                out=scrA[:, 512:768], in_=sel[:].rearrange("p a b -> p (a b)")))
            k.op(V, lambda E: E.tensor_copy(out=scrA[:, 768:784], in_=rstd_sl[:, 0:16]))
            k.op(V, lambda E: E.tensor_copy(out=scrA[:, 800:816], in_=nb_sl[:, 0:16]))
            k.op(V, lambda E: E.tensor_copy(out=scrA[:, 832:848], in_=sum_sl[:, 0:16]))
            k.op(V, lambda E: E.tensor_copy(out=scrA[:, 896:912], in_=mean_sl[:, 0:16]))
            k.op(V, lambda E: E.tensor_copy(out=scrA[:, 912:928], in_=var_sl[:, 0:16]))
            k.op(V, lambda E: E.tensor_copy(out=scrA[:, 928:944], in_=std_sl[:, 0:16]))
            v_d = k.op(V, lambda E: E.tensor_copy(out=scrA[:, 864:880],
                                                  in_=sumsq_sl[:, 0:16]), inc="sv")
            k.wait(Sy, "sv", v_d)
            k.op(Sy, lambda E: E.dma_start(out=dout[0:128, :], in_=scrA[:]),
                 inc="sx", n=16)
            for kk in range(8):
                k.op(V, lambda E, kk=kk: E.tensor_copy(
                    out=ro2[0][:, 128 * kk:128 * (kk + 1)], in_=yT_blk[:, kk, 0:128]))
            v_y = k.op(V, lambda E: E.engine_nop(), inc="sv")
            k.wait(Sy, "sv", v_y)
            k.op(Sy, lambda E: E.dma_start(out=dout[128:256, :], in_=ro2[0][:]),
                 inc="sx", n=16)
            for s in k.sems:
                k.wait(Sy, s, k.ct[s])
            with nc.Block() as block:
                for eng, name in ((block.sync, "sync"), (block.gpsimd, "gpsimd"),
                                  (block.vector, "vector"), (block.scalar, "scalar"),
                                  (block.tensor, "tensor")):
                    def mk(name):
                        def f(E):
                            for fn in k.q[name]:
                                fn(E)
                        return f
                    eng(mk(name))
            return nc
        gu = ffn_phase(d["x"], dout if debug_stop == "A" else dr1, 0,
                       want_stats=(debug_stop != "A"))
        if debug_stop != "A":
            k.full_barrier()
            attention_phase()
            if debug_stop != "B":
                k.full_barrier()
                ffn_phase(dr2, dout, gu, first_prepped=True)
        for s in k.sems:
            k.wait(Sy, s, k.ct[s])

        with nc.Block() as block:
            @block.sync
            def _(E):
                for fn in k.q["sync"]:
                    fn(E)

            @block.gpsimd
            def _(E):
                for fn in k.q["gpsimd"]:
                    fn(E)

            @block.vector
            def _(E):
                for fn in k.q["vector"]:
                    fn(E)

            @block.scalar
            def _(E):
                for fn in k.q["scalar"]:
                    fn(E)

            @block.tensor
            def _(E):
                for fn in k.q["tensor"]:
                    fn(E)

    return nc


def kernel(**inputs):
    x = np.ascontiguousarray(np.asarray(inputs["L0_attn_x"], dtype=np.float32))
    base = {n: np.ascontiguousarray(np.asarray(inputs[n], dtype=np.float32))
            for n in WNAMES}
    nc = build_program()
    in_maps = []
    for c in range(8):
        m = dict(base)
        m["L0_attn_x"] = np.ascontiguousarray(x[c])
        in_maps.append(m)
    res = run_bass_kernel_spmd(nc, in_maps, core_ids=list(range(8)))
    return np.stack([r["out"] for r in res.results], axis=0)


# version 14 (final)
# speedup vs baseline: 1.2387x; 1.0307x over previous
import numpy as np
import ml_dtypes
from contextlib import ExitStack

import concourse.bass as bass
import concourse.mybir as mybir
from concourse.bass_utils import run_bass_kernel_spmd

F32 = mybir.dt.float32
BF16 = mybir.dt.bfloat16
U8 = mybir.dt.uint8
AF = mybir.ActivationFunctionType
ALU = mybir.AluOpType
AX = mybir.AxisListType

S, Dm = 4096, 1024
NBLK, BLKT, NWIN, WT, NTL = 2, 2048, 4, 512, 16
NEXP, EH = 32, 512
NANC, AH = 16, 256
NHD, DK = 16, 64
NUNIT = NEXP + 1 + NANC
EPS = 1e-5
BIG = 1e30
INV_D = 1.0 / Dm

WNAMES = [
    "L0_attn_Wq", "L0_attn_Wk", "L0_attn_Wv", "L0_attn_Wo",
    "L0_ffn_gate_weight", "L0_ffn_gate_bias",
    "L0_ffn_w_gate", "L0_ffn_w_up", "L0_ffn_w_down",
    "L0_ffn_shared_w_gate", "L0_ffn_shared_w_up", "L0_ffn_shared_w_down",
    "L0_ffn_anchor_w_gate", "L0_ffn_anchor_w_up", "L0_ffn_anchor_w_down",
]

ENGS = ("tensor", "vector", "scalar", "sync", "gpsimd")


class Sched:
    def __init__(self):
        self.q = {e: [] for e in ENGS}
        self.sems = {}
        self.ct = {}
        self.last_wait = {}
        self.trace = {e: [] for e in ENGS}

    def add_sem(self, name, handle):
        self.sems[name] = handle
        self.ct[name] = 0

    def wait(self, eng, sem, val):
        if not val or val <= 0:
            return
        key = (eng, sem)
        if self.last_wait.get(key, 0) >= val:
            return
        self.last_wait[key] = val
        h = self.sems[sem]
        self.trace[eng].append(("wait", sem, val))
        self.q[eng].append(lambda E, h=h, v=val: E.wait_ge(h, v))

    def waitp(self, eng, pair):
        if pair is not None:
            self.wait(eng, pair[0], pair[1])

    def op(self, eng, fn, inc=None, n=1):
        if inc is None:
            self.q[eng].append(fn)
            self.trace[eng].append(("op", None, 0))
            return None
        h = self.sems[inc]

        def g(E, fn=fn, h=h, n=n):
            fn(E).then_inc(h, n)

        self.q[eng].append(g)
        self.ct[inc] += n
        self.trace[eng].append(("inc", inc, n))
        return self.ct[inc]

    def vop(self, fn):
        v = self.op("vector", fn, inc="sv")
        self.wait("vector", "sv", v)
        return v

    def sop(self, fn):
        v = self.op("scalar", fn, inc="ss")
        self.wait("scalar", "ss", v)
        return v

    def full_barrier(self):
        for e in ENGS:
            for s in self.sems:
                self.wait(e, s, self.ct[s])


def build_program(debug_stop=None):
    nc = bass.Bass("TRN2", target_bir_lowering=False)
    k = Sched()

    d = {}
    d["x"] = nc.dram_tensor("L0_attn_x", [S, Dm], F32, kind="ExternalInput").ap()
    shapes = {
        "L0_attn_Wq": [Dm, Dm], "L0_attn_Wk": [Dm, Dm], "L0_attn_Wv": [Dm, Dm],
        "L0_attn_Wo": [Dm, Dm],
        "L0_ffn_gate_weight": [NEXP, Dm], "L0_ffn_gate_bias": [NEXP],
        "L0_ffn_w_gate": [NEXP, Dm, EH], "L0_ffn_w_up": [NEXP, Dm, EH],
        "L0_ffn_w_down": [NEXP, EH, Dm],
        "L0_ffn_shared_w_gate": [Dm, EH], "L0_ffn_shared_w_up": [Dm, EH],
        "L0_ffn_shared_w_down": [EH, Dm],
        "L0_ffn_anchor_w_gate": [NANC, Dm, AH], "L0_ffn_anchor_w_up": [NANC, Dm, AH],
        "L0_ffn_anchor_w_down": [NANC, AH, Dm],
    }
    for n, sh in shapes.items():
        d[n] = nc.dram_tensor(n, sh, F32, kind="ExternalInput").ap()
    dout = nc.dram_tensor("out", [S, Dm], F32, kind="ExternalOutput").ap()
    dr1 = nc.dram_tensor("r1_scr", [S, Dm], F32, kind="Internal").ap()
    dr2 = nc.dram_tensor("r2_scr", [S, Dm], F32, kind="Internal").ap()
    dat = nc.dram_tensor("attnT_scr", [32, 128, 8, 128], BF16, kind="Internal").ap()

    idn_c = nc.inline_tensor(np.eye(128, dtype=np.float32), name="idn_c").ap()
    ones_c = nc.inline_tensor(np.ones((1, 128), dtype=np.float32), name="ones_c").ap()
    onescol_c = nc.inline_tensor(np.ones((128, 1), dtype=np.float32), name="onescol_c").ap()
    onescol_b_c = nc.inline_tensor(np.ones((128, 1), dtype=ml_dtypes.bfloat16),
                                   name="onescol_b_c").ap()
    consts_c = nc.inline_tensor(
        np.tile(np.array([[EPS, 0.125, 0.5, -1.0]], dtype=np.float32), (128, 1)),
        name="consts_c").ap()
    mnp = np.zeros((128, 4, 512), dtype=ml_dtypes.bfloat16)
    for m in range(4):
        for p in range(128):
            mnp[p, m, 128 * m + p:] = 1.0
    masks_c = nc.inline_tensor(mnp, name="masks_c").ap()

    with ExitStack() as ctx:
        def sbuf(name, shape, dt=F32):
            return ctx.enter_context(nc.sbuf_tensor(name, shape, dt))

        idn = sbuf("idn", [128, 128])
        onesr = sbuf("onesr", [1, 128])
        onescol = sbuf("onescol", [128, 1])
        onescol_b = sbuf("onescol_b", [128, 1], BF16)
        cst = sbuf("cst", [128, 4])
        gb_sb = sbuf("gb_sb", [1, NEXP])
        gb16 = sbuf("gb16", [1, NTL * NEXP])
        gwT = sbuf("gwT", [128, 8, NEXP], BF16)
        biasbc16 = sbuf("biasbc16", [128, NTL * NEXP])
        sum_sl = sbuf("sum_sl", [128, 32]); sumsq_sl = sbuf("sumsq_sl", [128, 32])
        mean_sl = sbuf("mean_sl", [128, 32]); var_sl = sbuf("var_sl", [128, 32])
        std_sl = sbuf("std_sl", [128, 32]); rstd_sl = sbuf("rstd_sl", [128, 32])
        nb_sl = sbuf("nb_sl", [128, 32]); st_tmp = sbuf("st_tmp", [128, 1])
        xt2 = [sbuf(f"xt{i}", [128, Dm]) for i in range(2)]
        yf2 = [sbuf(f"yf{i}", [128, Dm]) for i in range(2)]
        scrA = sbuf("scrA", [128, Dm])
        ro2 = [sbuf(f"ro{i}", [128, Dm]) for i in range(2)]
        scores_all = sbuf("scores_all", [128, NTL * NEXP])
        sc_all = sbuf("sc_all", [128, NTL * NEXP])
        w6_all = sbuf("w6_all", [128, NTL * NEXP])
        m8g_all = nc.alloc_sbuf_tensor_at(
            "m8g_all", [128, 4 * NTL, 8], F32,
            offset=nc.lookup_mloc(w6_all).addr)
        gs8_all = sbuf("gs8_all", [128, NTL, 8])
        m8a_all = sbuf("m8a_all", [128, NTL, 8])
        m8s_all = sbuf("m8s_all", [128, NTL, 8])
        pen_all = sbuf("pen_all", [128, NTL, 4])
        pm_all = sbuf("pm_all", [128, NTL, 16])
        ssum_all = sbuf("ssum_all", [128, NTL, 1])
        rec_all = sbuf("rec_all", [128, NTL, 1])
        cw = sbuf("cw", [128, NTL, NEXP])
        sel = sbuf("sel", [128, NTL, NANC])
        ht2 = [sbuf(f"ht{i}", [128, 4, 512], BF16) for i in range(2)]
        sg2 = [sbuf(f"sg{i}", [128, 512], BF16) for i in range(2)]
        csq = sbuf("csq", [1, 1]); csc = sbuf("csc", [1, 1])
        csq2 = sbuf("csq2", [1, 1]); csc2 = sbuf("csc2", [1, 1])
        negc2 = [sbuf(f"negc{i}", [128, 1]) for i in range(2)]
        rrs = sbuf("rrs", [1, 512])
        rb_sb = sbuf("rb_sb", [64, 512])
        ao2 = [sbuf(f"ao{i}", [64, 512], BF16) for i in range(2)]

        arena_big = sbuf("arena_big", [128, 65536], U8)
        arena_w = sbuf("arena_w", [128, 49152], U8)
        arena_y = sbuf("arena_y", [128, 32768], U8)
        ab = nc.lookup_mloc(arena_big).addr
        aw = nc.lookup_mloc(arena_w).addr
        ay = nc.lookup_mloc(arena_y).addr

        def ov(base, name, shape, dt, off):
            return nc.alloc_sbuf_tensor_at(name, shape, dt, offset=base + off)

        acc = ov(ab, "acc", [128, NTL, Dm], F32, 0)
        yT_full = ov(ab, "yT_full", [128, 8, S], BF16, 0)
        graw = ov(ab, "graw", [NEXP, Dm], F32, 0)
        wg_b = [ov(aw, f"wg_b{i}", [128, 8, 512], BF16, 24576 * i) for i in range(2)]
        wu_b = [ov(aw, f"wu_b{i}", [128, 8, 512], BF16, 24576 * i + 8192) for i in range(2)]
        wd_b = [ov(aw, f"wd_b{i}", [128, 4, 1024], BF16, 24576 * i + 16384) for i in range(2)]
        qT2 = ov(aw, "qT2", [128, S], BF16, 0)
        kT2 = ov(aw, "kT2", [128, S], BF16, 8192)
        v2p = ov(aw, "v2p", [128, 32, 2, 65], BF16, 16384)
        wo_sb = ov(aw, "wo_sb", [128, 8, Dm], BF16, 24704)
        wq_sl = ov(aw, "wq_sl", [128, 8, 128], BF16, 41088)
        wk_sl = ov(aw, "wk_sl", [128, 8, 128], BF16, 43136)
        wv_sl = ov(aw, "wv_sl", [128, 8, 128], BF16, 45184)
        yT_blk = ov(ay, "yT_blk", [128, 8, BLKT], BF16, 0)
        qsq = ov(ay, "qsq", [128, 2048], BF16, 0)
        at_sb = [ov(ay, f"at_sb{i}", [128, 8, 128], BF16, 8192 + 2048 * i)
                 for i in range(2)]
        masks = ov(ay, "masks", [128, 4, 512], BF16, 16384)
        expT = ov(ay, "expT", [128, 12, 512], BF16, 20480)

        ps = ctx.enter_context(nc.psum_tensor("ps", [128, 8, 512], F32))

        for nm in ("sx", "sw0", "sw1", "sxt0", "sxt1", "sro0", "sro1",
                   "sao0", "sao1", "sat0", "sat1", "st", "sv", "ss"):
            k.add_sem(nm, ctx.enter_context(nc.semaphore(nm)))

        T, V, Sc, Sy, G = "tensor", "vector", "scalar", "sync", "gpsimd"

        st8 = {
            "bank_free": [None] * 8,
            "ht_used_t": [0, 0],
            "sg_free_v": [0, 0],
            "wbuf_free_t": [0, 0],
            "wready": {},
            "yf_free": [None, None],
            "epi_dma": [[None] * NTL, [None] * NTL],
            "xt_free": [None, None],
            "ro_free": [None, None],
            "ao_free": [None, None],
            "at_free": [None, None],
            "scores_free_v": [0, 0],
            "eslot_t": [0] * 12,
            "rrs_free": None,
            "wgate": {},
        }

        pro_dma = 0
        for dst, src in ((idn, idn_c), (onesr, ones_c), (onescol, onescol_c),
                         (onescol_b, onescol_b_c), (cst, consts_c),
                         (graw, d["L0_ffn_gate_weight"]),
                         (gb_sb, d["L0_ffn_gate_bias"].rearrange("(a b) -> a b", a=1))):
            pro_dma = k.op(Sy, lambda E, dst=dst, src=src: E.dma_start(
                out=dst[:], in_=src), inc="sx", n=16)
        k.wait(T, "sx", pro_dma)
        k.wait(Sc, "sx", pro_dma)
        k.wait(V, "sx", pro_dma)
        for i in range(NTL):
            k.op(V, lambda E, i=i: E.tensor_copy(
                out=gb16[:, NEXP * i:NEXP * (i + 1)], in_=gb_sb[:]))
        k.op(V, lambda E: E.memset(gs8_all[:], -BIG), inc="sv")
        for kk in range(8):
            bb = kk % 2
            k.waitp(T, st8["bank_free"][bb])
            t_tr = k.op(T, lambda E, kk=kk, bb=bb: E.transpose(
                out=ps[:, bb, 0:32],
                in_=graw[0:32, 128 * kk:128 * (kk + 1)],
                identity=idn[0:32, 0:32]), inc="st")
            k.wait(V, "st", t_tr)
            v_tr = k.op(V, lambda E, kk=kk, bb=bb: E.tensor_copy(
                out=gwT[:, kk, :], in_=ps[:, bb, 0:32]), inc="sv")
            st8["bank_free"][bb] = ("sv", v_tr)
        k.waitp(T, st8["bank_free"][1])
        k.wait(T, "sv", k.ct["sv"])
        t_bc = k.op(T, lambda E: E.matmul(out=ps[:, 1, :], lhsT=onesr[:],
                                          rhs=gb16[:], start=True, stop=True), inc="st")
        k.wait(V, "st", t_bc)
        v_pro = k.op(V, lambda E: E.tensor_copy(out=biasbc16[:], in_=ps[:, 1, :]),
                     inc="sv")
        st8["bank_free"][0] = ("sv", v_pro)
        st8["bank_free"][1] = ("sv", v_pro)

        def ln_pass(src, ntiles, yT, slot0):
            for i in range(ntiles):
                t = i % 2
                sl = slot0 + i
                rows = 128 * sl
                Q = Sy if t == 0 else Sc
                k.waitp(Q, st8["xt_free"][t])
                xdma = k.op(Q, lambda E, t=t, rows=rows, src=src: E.dma_start(
                    out=xt2[t][:], in_=src[rows:rows + 128, :]), inc=f"sxt{t}", n=16)
                k.wait(V, f"sxt{t}", xdma)
                k.op(V, lambda E, t=t, sl=sl: E.tensor_scalar(
                    out=scrA[:], in0=xt2[t][:], scalar1=1.0, scalar2=0.0,
                    op0=ALU.mult, op1=ALU.add, accum_out=sum_sl[:, sl:sl + 1]))
                k.op(V, lambda E, t=t, sl=sl: E.scalar_tensor_tensor(
                    out=scrA[:], in0=xt2[t][:], scalar=1.0, in1=xt2[t][:],
                    op0=ALU.mult, op1=ALU.mult, accum_out=sumsq_sl[:, sl:sl + 1]))
                k.vop(lambda E, sl=sl: E.tensor_scalar(
                    out=mean_sl[:, sl:sl + 1], in0=sum_sl[:, sl:sl + 1],
                    scalar1=INV_D, scalar2=0.0, op0=ALU.mult, op1=ALU.add))
                k.vop(lambda E, sl=sl: E.tensor_scalar(
                    out=var_sl[:, sl:sl + 1], in0=sumsq_sl[:, sl:sl + 1],
                    scalar1=INV_D, scalar2=0.0, op0=ALU.mult, op1=ALU.add))
                k.vop(lambda E, sl=sl: E.tensor_mul(
                    out=st_tmp[:], in0=mean_sl[:, sl:sl + 1], in1=mean_sl[:, sl:sl + 1]))
                v_st = k.vop(lambda E, sl=sl: E.tensor_sub(
                    out=var_sl[:, sl:sl + 1], in0=var_sl[:, sl:sl + 1],
                    in1=st_tmp[:]))
                k.wait(Sc, "sv", v_st)
                s_sq = k.op(Sc, lambda E, sl=sl: E.activation(
                    out=std_sl[:, sl:sl + 1], in_=var_sl[:, sl:sl + 1], func=AF.Sqrt,
                    bias=cst[:, 0:1], scale=1.0), inc="ss")
                k.wait(V, "ss", s_sq)
                k.vop(lambda E, sl=sl: E.reciprocal(
                    out=rstd_sl[:, sl:sl + 1], in_=std_sl[:, sl:sl + 1]))
                v_nb = k.vop(lambda E, sl=sl: E.scalar_tensor_tensor(
                    out=nb_sl[:, sl:sl + 1], in0=mean_sl[:, sl:sl + 1], scalar=-1.0,
                    in1=rstd_sl[:, sl:sl + 1], op0=ALU.mult, op1=ALU.mult))
                k.wait(Sc, "sv", v_nb)
                k.waitp(Sc, st8["yf_free"][t])
                y_s = k.op(Sc, lambda E, t=t, sl=sl: E.activation(
                    out=yf2[t][:], in_=xt2[t][:], func=AF.Identity,
                    bias=nb_sl[:, sl:sl + 1], scale=rstd_sl[:, sl:sl + 1]), inc="ss")
                st8["xt_free"][t] = ("ss", y_s)
                k.wait(T, "ss", y_s)
                last_tr = 0
                for kk in range(8):
                    bb = kk % 2
                    k.waitp(T, st8["bank_free"][bb])
                    t_tr = k.op(T, lambda E, t=t, kk=kk, bb=bb: E.transpose(
                        out=ps[:, bb, 0:128], in_=yf2[t][:, 128 * kk:128 * (kk + 1)],
                        identity=idn[:]), inc="st")
                    last_tr = t_tr
                    k.wait(V, "st", t_tr)
                    v_ev = k.op(V, lambda E, kk=kk, bb=bb, i=i, yT=yT: E.tensor_copy(
                        out=yT[:, kk, 128 * i:128 * (i + 1)], in_=ps[:, bb, 0:128]),
                        inc="sv")
                    st8["bank_free"][bb] = ("sv", v_ev)
                st8["yf_free"][t] = ("st", last_tr)
            return k.ct["sv"]

        def routing_pass(yT, yT_ready):
            k.op(V, lambda E: E.memset(gs8_all[:, :, 4:8], -BIG))
            sig = [0, 0]
            for half in range(2):
                bb = 2 + half
                k.wait(T, "sv", yT_ready)
                k.waitp(T, st8["bank_free"][bb])
                t_lg = None
                for i2 in range(8):
                    i = 8 * half + i2
                    for kk in range(8):
                        t_lg = k.op(T, lambda E, kk=kk, i=i, i2=i2, bb=bb, yT=yT:
                                    E.matmul(
                                        out=ps[:, bb, NEXP * i2:NEXP * (i2 + 1)],
                                        lhsT=yT[:, kk, 128 * i:128 * (i + 1)],
                                        rhs=gwT[:, kk, :],
                                        start=(kk == 0), stop=(kk == 7)),
                                    inc="st" if (kk == 7 and i2 == 7) else None)
                k.wait(Sc, "st", t_lg)
                sig[half] = k.op(Sc, lambda E, half=half, bb=bb: E.activation(
                    out=scores_all[:, 256 * half:256 * (half + 1)],
                    in_=ps[:, bb, 0:256], func=AF.Sigmoid), inc="ss")
                st8["bank_free"][bb] = ("ss", sig[half])
            k.wait(V, "ss", sig[1])
            k.vop(lambda E: E.tensor_add(out=sc_all[:], in0=scores_all[:],
                                         in1=biasbc16[:]))
            lvl = None
            for g in range(4 * NTL):
                lvl = k.op(V, lambda E, g=g: E.max(
                    out=m8g_all[:, g, :], in_=sc_all[:, 8 * g:8 * (g + 1)]),
                    inc="sv" if g == 4 * NTL - 1 else None)
            k.wait(V, "sv", lvl)
            for i in range(NTL):
                g3 = gs8_all[:, i, 0:4].rearrange("p (a b) -> p a b", b=1)
                lvl = k.op(V, lambda E, g3=g3, i=i: E.tensor_add(
                    out=g3, in0=m8g_all[:, 4 * i:4 * i + 4, 0:1],
                    in1=m8g_all[:, 4 * i:4 * i + 4, 1:2]),
                    inc="sv" if i == NTL - 1 else None)
            k.wait(V, "sv", lvl)
            for i in range(NTL):
                lvl = k.op(V, lambda E, i=i: E.max(
                    out=m8a_all[:, i, :], in_=gs8_all[:, i, :]),
                    inc="sv" if i == NTL - 1 else None)
            k.wait(V, "sv", lvl)
            for i in range(NTL):
                lvl = k.op(V, lambda E, i=i: E.tensor_scalar(
                    out=pen_all[:, i, :], in0=gs8_all[:, i, 0:4],
                    scalar1=m8a_all[:, i, 1:2], scalar2=-BIG,
                    op0=ALU.is_lt, op1=ALU.mult),
                    inc="sv" if i == NTL - 1 else None)
            k.wait(V, "sv", lvl)
            for i in range(NTL):
                for g in range(4):
                    c0 = NEXP * i + 8 * g
                    lvl = k.op(V, lambda E, c0=c0, i=i, g=g: E.tensor_scalar_add(
                        out=sc_all[:, c0:c0 + 8], in0=sc_all[:, c0:c0 + 8],
                        scalar1=pen_all[:, i, g:g + 1]),
                        inc="sv" if (i == NTL - 1 and g == 3) else None)
            k.wait(V, "sv", lvl)
            for i in range(NTL):
                lvl = k.op(V, lambda E, i=i: E.max(
                    out=m8s_all[:, i, :], in_=sc_all[:, NEXP * i:NEXP * (i + 1)]),
                    inc="sv" if i == NTL - 1 else None)
            k.wait(V, "sv", lvl)
            for i in range(NTL):
                lvl = k.op(V, lambda E, i=i: E.tensor_single_scalar(
                    out=w6_all[:, NEXP * i:NEXP * (i + 1)],
                    in_=sc_all[:, NEXP * i:NEXP * (i + 1)],
                    scalar=m8s_all[:, i, 5:6], op=ALU.is_ge),
                    inc="sv" if i == NTL - 1 else None)
            k.wait(V, "sv", lvl)
            k.vop(lambda E: E.tensor_mul(out=w6_all[:], in0=w6_all[:],
                                         in1=scores_all[:]))
            w6v3 = w6_all[:].rearrange("p (a b) -> p a b", b=NEXP)
            k.vop(lambda E: E.reduce_sum(out=ssum_all[:], in_=w6v3, axis=AX.X))
            k.vop(lambda E: E.tensor_scalar_add(out=rec_all[:], in0=ssum_all[:],
                                                scalar1=1e-20))
            k.vop(lambda E: E.reciprocal(out=ssum_all[:], in_=rec_all[:]))
            k.vop(lambda E: E.tensor_scalar_mul(out=rec_all[:], in0=ssum_all[:],
                                                scalar1=0.2))
            for i in range(NTL):
                lvl = k.op(V, lambda E, i=i: E.tensor_scalar_mul(
                    out=cw[:, i, :], in0=w6_all[:, NEXP * i:NEXP * (i + 1)],
                    scalar1=rec_all[:, i, :]),
                    inc="sv" if i == NTL - 1 else None)
            sc4p = sc_all[:].rearrange("p (a b c) -> p a b c", b=16, c=2)
            for i in range(NTL):
                pm3 = pm_all[:, i, :].rearrange("p (a b) -> p a b", b=1)
                lvl2 = k.op(V, lambda E, pm3=pm3, i=i: E.tensor_tensor(
                    out=pm3, in0=sc4p[:, i, :, 0:1], in1=sc4p[:, i, :, 1:2],
                    op=ALU.max), inc="sv" if i == NTL - 1 else None)
            k.wait(V, "sv", max(lvl, lvl2))
            for i in range(NTL):
                lvl = k.op(V, lambda E, i=i: E.tensor_scalar(
                    out=sel[:, i, :], in0=pm_all[:, i, :],
                    scalar1=m8s_all[:, i, 0:1], scalar2=0.1,
                    op0=ALU.is_ge, op1=ALU.mult),
                    inc="sv" if i == NTL - 1 else None)
            k.wait(V, "sv", lvl)
            return k.ct["sv"]

        def unit_weight_dma(gu, after_t):
            par = gu % 2
            u = gu % NUNIT
            k.wait(G, "st", after_t)
            if u < NEXP:
                srcs = [(wg_b[par][:], d["L0_ffn_w_gate"][u]),
                        (wu_b[par][:], d["L0_ffn_w_up"][u]),
                        (wd_b[par][:], d["L0_ffn_w_down"][u])]
            elif u == NEXP:
                srcs = [(wg_b[par][:], d["L0_ffn_shared_w_gate"]),
                        (wu_b[par][:], d["L0_ffn_shared_w_up"]),
                        (wd_b[par][:], d["L0_ffn_shared_w_down"])]
            else:
                l = u - NEXP - 1
                srcs = [(wg_b[par][:, :, 0:AH], d["L0_ffn_anchor_w_gate"][l]),
                        (wu_b[par][:, :, 0:AH], d["L0_ffn_anchor_w_up"][l]),
                        (wd_b[par][:, 0:2, :], d["L0_ffn_anchor_w_down"][l])]
            last = 0
            for dst, src in srcs:
                src = src.rearrange("(a p) h -> p a h", p=128)
                last = k.op(G, lambda E, dst=dst, src=src: E.dma_start(
                    out=dst, in_=src), inc=f"sw{par}", n=16)
            st8["wready"][gu] = last

        def gu_quarters(yT, w, par_step, par_u, nq, yT_ready, wrdy):
            htr = 0
            for q in range(nq):
                k.waitp(T, st8["bank_free"][2 * q])
                k.waitp(T, st8["bank_free"][2 * q + 1])
                k.wait(T, f"sw{par_u}", wrdy)
                k.wait(T, "sv", yT_ready)

                def t_gu(E, q=q, w=w, par_u=par_u, yT=yT):
                    for kk in range(8):
                        E.matmul(ps[:, 2 * q, :],
                                 lhsT=wg_b[par_u][:, kk, 128 * q:128 * (q + 1)],
                                 rhs=yT[:, kk, WT * w:WT * (w + 1)],
                                 start=(kk == 0), stop=(kk == 7))
                    ins = None
                    for kk in range(8):
                        ins = E.matmul(ps[:, 2 * q + 1, :],
                                       lhsT=wu_b[par_u][:, kk, 128 * q:128 * (q + 1)],
                                       rhs=yT[:, kk, WT * w:WT * (w + 1)],
                                       start=(kk == 0), stop=(kk == 7))
                    return ins
                t_q = k.op(T, t_gu, inc="st")
                k.wait(Sc, "st", t_q)
                k.wait(Sc, "sv", st8["sg_free_v"][q % 2])
                s_q = k.op(Sc, lambda E, q=q: E.activation(
                    out=sg2[q % 2][:], in_=ps[:, 2 * q, :], func=AF.Silu), inc="ss")
                st8["bank_free"][2 * q] = ("ss", s_q)
                k.wait(V, "ss", s_q)
                k.wait(V, "st", max(t_q, st8["ht_used_t"][par_step]))
                v_q = k.op(V, lambda E, q=q, par_step=par_step: E.tensor_mul(
                    out=ht2[par_step][:, q, :], in0=sg2[q % 2][:],
                    in1=ps[:, 2 * q + 1, :]), inc="sv")
                htr = v_q
                st8["bank_free"][2 * q + 1] = ("sv", v_q)
                st8["sg_free_v"][q % 2] = v_q
            return htr

        def down_unit(w, par_step, par_u, nk, ht_ready, cw_src):
            last_t = 0
            for r in range(8):
                tt, j = r // 2, r % 2
                k.waitp(T, st8["bank_free"][r])
                k.wait(T, "sv", ht_ready)

                def t_dn(E, r=r, tt=tt, j=j, par_step=par_step, par_u=par_u, nk=nk):
                    ins = None
                    for kk in range(nk):
                        ins = E.matmul(ps[:, r, :],
                                       lhsT=ht2[par_step][:, kk, 128 * tt:128 * (tt + 1)],
                                       rhs=wd_b[par_u][:, kk, WT * j:WT * (j + 1)],
                                       start=(kk == 0), stop=(kk == nk - 1))
                    return ins
                t_r = k.op(T, t_dn, inc="st")
                last_t = t_r
                k.wait(V, "st", t_r)
                tile = 4 * w + tt
                sc_ap = cw_src(tile)
                v_r = k.op(V, lambda E, r=r, tile=tile, j=j, sc_ap=sc_ap:
                           E.scalar_tensor_tensor(
                               out=acc[:, tile, WT * j:WT * (j + 1)], in0=ps[:, r, :],
                               scalar=sc_ap, in1=acc[:, tile, WT * j:WT * (j + 1)],
                               op0=ALU.mult, op1=ALU.add), inc="sv")
                st8["bank_free"][r] = ("sv", v_r)
            return last_t

        def cw_source(u):
            if u < NEXP:
                return lambda tile, u=u: cw[:, tile, u:u + 1]
            if u == NEXP:
                return lambda tile: 1.0
            l = u - NEXP - 1
            return lambda tile, l=l: sel[:, tile, l:l + 1]

        def ln_stats_piece(src, sl, rows, gate=None):
            t = sl % 2
            k.waitp(Sy, gate)
            k.waitp(Sy, st8["xt_free"][t])
            xdma = k.op(Sy, lambda E, t=t, rows=rows, src=src: E.dma_start(
                out=xt2[t][:], in_=src[rows:rows + 128, :]), inc=f"sxt{t}", n=16)
            k.wait(V, f"sxt{t}", xdma)
            k.op(V, lambda E, t=t, sl=sl: E.tensor_scalar(
                out=scrA[:], in0=xt2[t][:], scalar1=1.0, scalar2=0.0,
                op0=ALU.mult, op1=ALU.add, accum_out=sum_sl[:, sl:sl + 1]))
            v_s = k.op(V, lambda E, t=t, sl=sl: E.scalar_tensor_tensor(
                out=scrA[:], in0=xt2[t][:], scalar=1.0, in1=xt2[t][:],
                op0=ALU.mult, op1=ALU.mult,
                accum_out=sumsq_sl[:, sl:sl + 1]), inc="sv")
            st8["xt_free"][t] = ("sv", v_s)

        def ln_stats_sbuf(src_sb, sl):
            k.op(V, lambda E, src_sb=src_sb, sl=sl: E.tensor_scalar(
                out=scrA[:], in0=src_sb[:], scalar1=1.0, scalar2=0.0,
                op0=ALU.mult, op1=ALU.add, accum_out=sum_sl[:, sl:sl + 1]))
            k.op(V, lambda E, src_sb=src_sb, sl=sl: E.scalar_tensor_tensor(
                out=scrA[:], in0=src_sb[:], scalar=1.0, in1=src_sb[:],
                op0=ALU.mult, op1=ALU.mult,
                accum_out=sumsq_sl[:, sl:sl + 1]), inc="sv")

        def ln_finish(src, ntiles, yT, slot0):
            sl_ = slice(slot0, slot0 + ntiles)
            k.vop(lambda E, sl_=sl_: E.tensor_scalar(
                out=mean_sl[:, sl_], in0=sum_sl[:, sl_],
                scalar1=INV_D, scalar2=0.0, op0=ALU.mult, op1=ALU.add))
            k.vop(lambda E, sl_=sl_: E.tensor_scalar(
                out=var_sl[:, sl_], in0=sumsq_sl[:, sl_],
                scalar1=INV_D, scalar2=0.0, op0=ALU.mult, op1=ALU.add))
            k.vop(lambda E, sl_=sl_: E.tensor_mul(
                out=std_sl[:, sl_], in0=mean_sl[:, sl_], in1=mean_sl[:, sl_]))
            v_st = k.vop(lambda E, sl_=sl_: E.tensor_sub(
                out=var_sl[:, sl_], in0=var_sl[:, sl_], in1=std_sl[:, sl_]))
            k.wait(Sc, "sv", v_st)
            s_sq = k.op(Sc, lambda E, slot0=slot0, ntiles=ntiles: E.activation(
                out=std_sl[:, slot0:slot0 + ntiles],
                in_=var_sl[:, slot0:slot0 + ntiles],
                func=AF.Sqrt, bias=cst[:, 0:1], scale=1.0), inc="ss")
            k.wait(V, "ss", s_sq)
            k.vop(lambda E, slot0=slot0, ntiles=ntiles: E.reciprocal(
                out=rstd_sl[:, slot0:slot0 + ntiles],
                in_=std_sl[:, slot0:slot0 + ntiles]))
            k.vop(lambda E, slot0=slot0, ntiles=ntiles: E.scalar_tensor_tensor(
                out=nb_sl[:, slot0:slot0 + ntiles],
                in0=mean_sl[:, slot0:slot0 + ntiles], scalar=-1.0,
                in1=rstd_sl[:, slot0:slot0 + ntiles], op0=ALU.mult, op1=ALU.mult))
            for i in range(ntiles):
                t = i % 2
                sl = slot0 + i
                rows = 128 * sl
                Q = Sy if t == 0 else Sc
                k.waitp(Q, st8["xt_free"][t])
                xdma = k.op(Q, lambda E, t=t, rows=rows, src=src: E.dma_start(
                    out=xt2[t][:], in_=src[rows:rows + 128, :]), inc=f"sxt{t}", n=16)
                k.wait(Sc, f"sxt{t}", xdma)
                k.wait(Sc, "sv", k.ct["sv"])
                k.waitp(Sc, st8["yf_free"][t])
                y_s = k.op(Sc, lambda E, t=t, sl=sl: E.activation(
                    out=yf2[t][:], in_=xt2[t][:], func=AF.Identity,
                    bias=nb_sl[:, sl:sl + 1], scale=rstd_sl[:, sl:sl + 1]),
                    inc="ss")
                st8["xt_free"][t] = ("ss", y_s)
                k.wait(T, "ss", y_s)
                last_tr = 0
                for kk in range(8):
                    bb = kk % 2
                    k.waitp(T, st8["bank_free"][bb])
                    t_tr = k.op(T, lambda E, t=t, kk=kk, bb=bb: E.transpose(
                        out=ps[:, bb, 0:128], in_=yf2[t][:, 128 * kk:128 * (kk + 1)],
                        identity=idn[:]), inc="st")
                    last_tr = t_tr
                    k.wait(V, "st", t_tr)
                    v_ev = k.op(V, lambda E, kk=kk, bb=bb, i=i, yT=yT: E.tensor_copy(
                        out=yT[:, kk, 128 * i:128 * (i + 1)], in_=ps[:, bb, 0:128]),
                        inc="sv")
                    st8["bank_free"][bb] = ("sv", v_ev)
                st8["yf_free"][t] = ("st", last_tr)
            return k.ct["sv"]

        def ffn_phase(src, dst, gu0, extra_pieces=None, first_prepped=False,
                      want_stats=False):
            gu = gu0
            extra = list(extra_pieces or [])
            for blk in range(NBLK):
                base = blk * BLKT
                if blk == 0 and not first_prepped:
                    yT_ready = ln_pass(src, NTL, yT_blk, slot0=blk * NTL)
                elif blk == 0:
                    yT_ready = ln_finish(src, NTL, yT_blk, slot0=0)
                else:
                    yT_ready = ln_finish(src, NTL, yT_blk, slot0=blk * NTL)
                routing_pass(yT_blk, yT_ready)
                k.op(V, lambda E: E.memset(acc[:], 0.0), inc="sv")
                stats_left = list(range(NTL)) if blk + 1 < NBLK else []
                prev = None
                step = 0
                for u in range(NUNIT):
                    if stats_left and u % 3 == 2:
                        i2 = stats_left.pop(0)
                        ln_stats_piece(src, (blk + 1) * NTL + i2,
                                       (blk + 1) * BLKT + 128 * i2)
                    elif blk + 1 == NBLK and extra and u % 3 == 2:
                        extra.pop(0)()
                    nq = 4 if u <= NEXP else 2
                    nk = 4 if u <= NEXP else 2
                    par_u = gu % 2
                    if gu == gu0 and blk == 0:
                        unit_weight_dma(gu, after_t=st8["wbuf_free_t"][gu % 2])
                        unit_weight_dma(gu + 1, after_t=st8["wbuf_free_t"][(gu + 1) % 2])
                    wrdy = st8["wready"][gu]
                    for w in range(NWIN):
                        par_step = step % 2
                        htr = gu_quarters(yT_blk, w, par_step, par_u, nq, yT_ready, wrdy)
                        if prev is not None:
                            lt = down_unit(*prev)
                            st8["ht_used_t"][prev[1]] = lt
                            if prev[0] == NWIN - 1:
                                st8["wbuf_free_t"][(gu - 1) % 2] = lt
                                nxt = gu + 1
                                if nxt < gu0 + NBLK * NUNIT and nxt not in st8["wready"]:
                                    unit_weight_dma(nxt, after_t=st8["wbuf_free_t"][nxt % 2])
                        prev = (w, par_step, par_u, nk, htr, cw_source(u))
                        step += 1
                    gu += 1
                lt = down_unit(*prev)
                st8["ht_used_t"][prev[1]] = lt
                st8["wbuf_free_t"][(gu - 1) % 2] = lt
                nxt = gu + 1
                if nxt < gu0 + NBLK * NUNIT and nxt not in st8["wready"]:
                    unit_weight_dma(nxt, after_t=st8["wbuf_free_t"][nxt % 2])
                prev = None
                while stats_left:
                    i2 = stats_left.pop(0)
                    ln_stats_piece(src, (blk + 1) * NTL + i2,
                                   (blk + 1) * BLKT + 128 * i2)
                if blk + 1 == NBLK:
                    while extra:
                        extra.pop(0)()
                k.wait(V, "sv", k.ct["sv"])
                for i in range(NTL):
                    t = i % 2
                    rows = base + 128 * i
                    Q = Sy if t == 0 else Sc
                    k.waitp(Q, st8["xt_free"][t])
                    xdma = k.op(Q, lambda E, t=t, rows=rows, src=src: E.dma_start(
                        out=xt2[t][:], in_=src[rows:rows + 128, :]), inc=f"sxt{t}", n=16)
                    k.wait(V, f"sxt{t}", xdma)
                    k.waitp(V, st8["ro_free"][t])
                    v_e = k.op(V, lambda E, t=t, i=i: E.scalar_tensor_tensor(
                        out=ro2[t][:], in0=acc[:, i, :], scalar=0.5, in1=xt2[t][:],
                        op0=ALU.mult, op1=ALU.add), inc="sv")
                    st8["xt_free"][t] = ("sv", v_e)
                    k.wait(Sc, "sv", v_e)
                    odma = k.op(Sc, lambda E, t=t, rows=rows, dst=dst: E.dma_start(
                        out=dst[rows:rows + 128, :], in_=ro2[t][:]), inc=f"sro{t}", n=16)
                    st8["ro_free"][t] = (f"sro{t}", odma)
                    st8["epi_dma"][blk % 2][i] = (f"sro{t}", odma)
                    if want_stats:
                        ln_stats_sbuf(ro2[t], base // 128 + i)
            return gu

        def headnorm_sq(srcT, pb, dst01):
            t_last_prev = st8.get("qsq_free_t", 0)
            for half in range(2):
                colsl = slice(2048 * half, 2048 * (half + 1))
                k.wait(V, "st", t_last_prev)
                v_sq = k.op(V, lambda E, colsl=colsl, pb=pb, srcT=srcT: E.tensor_mul(
                    out=qsq[pb:pb + 64, :], in0=srcT[pb:pb + 64, colsl],
                    in1=srcT[pb:pb + 64, colsl]), inc="sv")
                k.wait(T, "sv", v_sq)
                t_n = 0
                for c in range(4):
                    k.waitp(T, st8["bank_free"][4 + c])
                    t_n = k.op(T, lambda E, c=c, pb=pb: E.matmul(
                        out=ps[0:1, 4 + c, :], lhsT=onescol_b[pb:pb + 64, :],
                        rhs=qsq[pb:pb + 64, WT * c:WT * (c + 1)],
                        start=True, stop=True), inc="st")
                t_last_prev = t_n
                k.wait(V, "st", t_n)
                tgt = st_tmp[0:1, 0:1] if half == 0 else dst01[:]
                v_red = k.vop(lambda E, tgt=tgt: E.tensor_reduce(
                    out=tgt, in_=ps[0:1, 4:8, :], axis=AX.XY, op=ALU.max))
                for c in range(4):
                    st8["bank_free"][4 + c] = ("sv", v_red)
            st8["qsq_free_t"] = t_last_prev
            k.vop(lambda E, dst01=dst01: E.tensor_tensor(
                out=dst01[:], in0=dst01[:], in1=st_tmp[0:1, 0:1], op=ALU.max))

        def attention_phase():
            mask_dma = k.op(Sy, lambda E: E.dma_start(out=masks[:], in_=masks_c),
                            inc="sx", n=16)
            k.wait(V, "sx", mask_dma)
            yT_ready = ln_finish(dr1, 32, yT_full, slot0=0)
            v_ones = k.op(V, lambda E: E.memset(v2p[:, :, :, 64:65], 1.0), inc="sv")
            last_qkv_t = 0
            for hp in range(8):
                k.wait(G, "st", last_qkv_t)
                if hp == 1:
                    st8["wo_dma"] = k.op(G, lambda E: E.dma_start(
                        out=wo_sb[:],
                        in_=d["L0_attn_Wo"].rearrange("(a p) h -> p a h", p=128)),
                        inc="sw1", n=16)
                wsl_last = 0
                for dstw, nm in ((wq_sl, "L0_attn_Wq"), (wk_sl, "L0_attn_Wk"),
                                 (wv_sl, "L0_attn_Wv")):
                    srcw = d[nm][:, 128 * hp:128 * (hp + 1)].rearrange(
                        "(a p) c -> p a c", p=128)
                    wsl_last = k.op(G, lambda E, dstw=dstw, srcw=srcw: E.dma_start(
                        out=dstw[:], in_=srcw), inc="sw0", n=16)
                for wsl, dstt, scale in ((wq_sl, qT2, 0.125), (wk_sl, kT2, 1.0)):
                    for w8 in range(8):
                        bb = 6 + w8 % 2
                        k.waitp(T, st8["bank_free"][bb])
                        k.wait(T, "sw0", wsl_last)
                        k.wait(T, "sv", yT_ready)

                        def t_q(E, wsl=wsl, w8=w8, bb=bb):
                            ins = None
                            for kk in range(8):
                                ins = E.matmul(ps[:, bb, :], lhsT=wsl[:, kk, :],
                                               rhs=yT_full[:, kk, WT * w8:WT * (w8 + 1)],
                                               start=(kk == 0), stop=(kk == 7))
                            return ins
                        t_w = k.op(T, t_q, inc="st")
                        k.wait(Sc, "st", t_w)
                        scap = cst[:, 1:2] if scale == 0.125 else 1.0
                        s_w = k.op(Sc, lambda E, dstt=dstt, w8=w8, bb=bb, scap=scap:
                                   E.activation(
                                       out=dstt[:, WT * w8:WT * (w8 + 1)],
                                       in_=ps[:, bb, :], func=AF.Identity,
                                       scale=scap), inc="ss")
                        st8["bank_free"][bb] = ("ss", s_w)
                for tt in range(32):
                    bb = 6 + tt % 2
                    k.waitp(T, st8["bank_free"][bb])
                    k.wait(T, "sw0", wsl_last)
                    k.wait(T, "sv", v_ones)

                    def t_v(E, tt=tt, bb=bb):
                        ins = None
                        for kk in range(8):
                            ins = E.matmul(ps[:, bb, 0:128],
                                           lhsT=yT_full[:, kk, 128 * tt:128 * (tt + 1)],
                                           rhs=wv_sl[:, kk, :],
                                           start=(kk == 0), stop=(kk == 7))
                        return ins
                    t_w = k.op(T, t_v, inc="st")
                    k.wait(V, "st", t_w)
                    v_w = k.op(V, lambda E, tt=tt, bb=bb: E.tensor_copy(
                        out=v2p[:, tt, :, 0:64],
                        in_=ps[:, bb, 0:128].rearrange("p (h c) -> p h c", h=2)),
                        inc="sv")
                    st8["bank_free"][bb] = ("sv", v_w)
                qkv_done_v = k.ct["sv"]
                last_qkv_t = k.ct["st"]
                k.wait(V, "ss", k.ct["ss"])
                headnorm_sq(qT2, 0, csq)
                headnorm_sq(kT2, 0, csc)
                headnorm_sq(qT2, 64, csq2)
                headnorm_sq(kT2, 64, csc2)
                k.vop(lambda E: E.tensor_mul(out=csq[:], in0=csq[:], in1=csc[:]))
                k.vop(lambda E: E.tensor_mul(out=csq2[:], in0=csq2[:], in1=csc2[:]))
                v_c = k.vop(lambda E: E.tensor_tensor(
                    out=csq[:], in0=csq[:], in1=csq2[:], op=ALU.max))
                k.wait(Sc, "sv", v_c)
                s_ln = k.sop(lambda E: E.activation(
                    out=csc[:], in_=csq[:], func=AF.Ln))
                s_c = k.sop(lambda E: E.activation(
                    out=csq[:], in_=csc[:], func=AF.Exp, scale=cst[0:1, 2:3]))
                k.wait(V, "ss", s_c)
                v_c2 = k.vop(lambda E: E.tensor_scalar(
                    out=csc[:], in0=csq[:], scalar1=-1.0, scalar2=-0.1,
                    op0=ALU.mult, op1=ALU.add))
                k.wait(T, "sv", v_c2)
                k.waitp(T, st8["bank_free"][6])
                t_nc = k.op(T, lambda E: E.matmul(
                    out=ps[:, 6, 0:1], lhsT=onesr[:], rhs=csc[:],
                    start=True, stop=True), inc="st")
                k.wait(V, "st", t_nc)
                v_nc = k.vop(lambda E: E.tensor_copy(
                    out=negc2[0][:], in_=ps[:, 6, 0:1]))
                st8["bank_free"][6] = ("sv", v_nc)
                negc_v = k.ct["sv"]
                def normalize_store(w, t_fl, hp=hp):
                    for h01 in range(2):
                        head = 2 * hp + h01
                        ob = 4 + h01
                        k.wait(V, "st", t_fl)
                        k.waitp(V, st8["rrs_free"])
                        v_rr = k.vop(lambda E, ob=ob: E.reciprocal(
                            out=rrs[:], in_=ps[64:65, ob, :]))
                        k.wait(T, "sv", v_rr)
                        k.waitp(T, st8["bank_free"][6])
                        t_b2 = k.op(T, lambda E: E.matmul(
                            out=ps[0:64, 6, :], lhsT=onesr[0:1, 0:64], rhs=rrs[:],
                            start=True, stop=True), inc="st")
                        st8["rrs_free"] = ("st", t_b2)
                        k.wait(V, "st", t_b2)
                        v_rb = k.vop(lambda E: E.tensor_copy(
                            out=rb_sb[:], in_=ps[0:64, 6, :]))
                        st8["bank_free"][6] = ("sv", v_rb)
                        tb = h01
                        k.waitp(V, st8["ao_free"][tb])
                        v_ao = k.op(V, lambda E, tb=tb, ob=ob: E.tensor_mul(
                            out=ao2[tb][:], in0=ps[0:64, ob, :], in1=rb_sb[:]),
                            inc="sv")
                        st8["bank_free"][ob] = ("sv", v_ao)
                        k.wait(Sy, "sv", v_ao)
                        pr0 = 64 * (head % 2)
                        dst_ap = dat[4 * w:4 * (w + 1), pr0:pr0 + 64, head // 2, :]\
                            .rearrange("t r c -> r t c")
                        src_ap = ao2[tb][:].rearrange("r (t c) -> r t c", c=128)
                        o_dma = k.op(Sy, lambda E, dst_ap=dst_ap, src_ap=src_ap:
                                     E.dma_start(out=dst_ap, in_=src_ap),
                                     inc=f"sao{tb}", n=16)
                        st8["ao_free"][tb] = (f"sao{tb}", o_dma)
                    if hp == 7:
                        st8["wgate"][w] = (k.ct["sao0"], k.ct["sao1"])
                for w in range(8):
                    nkt = 4 * w + 4
                    nb2 = nkt // 2
                    exp_k = [[0] * nkt, [0] * nkt]
                    msk_v = [[0] * nkt, [0] * nkt]

                    def slot(h01, kt):
                        return 6 * h01 + 2 * ((kt // 2) % 3) + kt % 2

                    def emit_scores(b2, w=w):
                        out_t = [[0, 0], [0, 0]]
                        for kt01 in range(2):
                            kt = 2 * b2 + kt01
                            for h01 in range(2):
                                bb = 2 * h01 + kt01
                                pb = 64 * h01
                                if b2 >= 1:
                                    k.wait(T, "ss", exp_k[h01][kt - 2])
                                else:
                                    k.waitp(T, st8["bank_free"][bb])
                                k.wait(T, "sv", qkv_done_v)
                                out_t[h01][kt01] = k.op(
                                    T, lambda E, kt=kt, bb=bb, pb=pb, w=w: E.matmul(
                                        ps[:, bb, :],
                                        lhsT=kT2[pb:pb + 64, 128 * kt:128 * (kt + 1)],
                                        rhs=qT2[pb:pb + 64, WT * w:WT * (w + 1)],
                                        start=True, stop=True), inc="st")
                        return out_t

                    def emit_exp(h01, b2, sc_t2, w=w):
                        for kt01 in range(2):
                            kt = 2 * b2 + kt01
                            bb = 2 * h01 + kt01
                            sl = slot(h01, kt)
                            k.wait(Sc, "st", sc_t2[kt01])
                            k.wait(Sc, "sv", negc_v)
                            k.wait(Sc, "st", st8["eslot_t"][sl])
                            s_e = k.op(Sc, lambda E, bb=bb, sl=sl: E.activation(
                                out=expT[:, sl, :], in_=ps[:, bb, :],
                                func=AF.Exp, bias=negc2[0][:], scale=1.0), inc="ss")
                            exp_k[h01][kt] = s_e
                            st8["bank_free"][bb] = ("ss", s_e)
                            o = 128 * kt - 512 * w
                            if o >= 0:
                                mi = o // 128
                                k.wait(V, "ss", s_e)
                                msk_v[h01][kt] = k.op(V, lambda E, sl=sl, mi=mi:
                                                      E.tensor_mul(
                                                          out=expT[:, sl, :],
                                                          in0=expT[:, sl, :],
                                                          in1=masks[:, mi, :]),
                                                      inc="sv")

                    def emit_attnv(b2, w=w, nkt=nkt):
                        for h01 in range(2):
                            ob = 4 + h01
                            for kt in (2 * b2, 2 * b2 + 1):
                                sl = slot(h01, kt)
                                k.wait(T, "ss", exp_k[h01][kt])
                                if msk_v[h01][kt]:
                                    k.wait(T, "sv", msk_v[h01][kt])
                                if kt == 0:
                                    k.waitp(T, st8["bank_free"][ob])
                                t_a = k.op(T, lambda E, kt=kt, h01=h01, nkt=nkt, ob=ob, sl=sl:
                                           E.matmul(
                                               ps[0:65, ob, :], lhsT=v2p[:, kt, h01, :],
                                               rhs=expT[:, sl, :],
                                               start=(kt == 0), stop=(kt == nkt - 1)),
                                           inc="st")
                                st8["eslot_t"][sl] = t_a

                    for b2 in range(nb2):
                        sc_t2 = emit_scores(b2)
                        if b2 >= 2:
                            emit_attnv(b2 - 2)
                        emit_exp(0, b2, sc_t2[0])
                        emit_exp(1, b2, sc_t2[1])
                        if b2 == 1 and st8.get("pend_norm"):
                            st8["pend_norm"]()
                            st8["pend_norm"] = None
                    if st8.get("pend_norm"):
                        st8["pend_norm"]()
                        st8["pend_norm"] = None
                    for b2 in range(max(nb2 - 2, 0), nb2):
                        emit_attnv(b2)
                    t_fl = k.ct["st"]

                    def norm_emit(w=w, t_fl=t_fl):
                        normalize_store(w, t_fl)
                    if w == 7:
                        norm_emit()
                    else:
                        st8["pend_norm"] = norm_emit

            wo_last = st8["wo_dma"]
            r2dst = dout if debug_stop == "B" else dr2
            for i in range(32):
                t = i % 2
                rows = 128 * i
                g0, g1 = st8["wgate"][i // 4]
                k.wait(Sy, "sao0", g0)
                k.wait(Sy, "sao1", g1)
                k.waitp(Sy, st8["at_free"][t])
                a_dma = k.op(Sy, lambda E, t=t, i=i: E.dma_start(
                    out=at_sb[t][:], in_=dat[i]), inc=f"sat{t}", n=16)
                k.waitp(Sy, st8["xt_free"][t])
                x_dma = k.op(Sy, lambda E, t=t, rows=rows: E.dma_start(
                    out=xt2[t][:], in_=dr1[rows:rows + 128, :]), inc=f"sxt{t}", n=16)
                t_half = [0, 0]
                for j in range(2):
                    bb = 6 + j
                    k.waitp(T, st8["bank_free"][bb])
                    k.wait(T, f"sat{t}", a_dma)
                    k.wait(T, "sw1", wo_last)

                    def t_wo(E, t=t, j=j, bb=bb):
                        ins = None
                        for kk in range(8):
                            ins = E.matmul(ps[:, bb, :], lhsT=at_sb[t][:, kk, :],
                                           rhs=wo_sb[:, kk, WT * j:WT * (j + 1)],
                                           start=(kk == 0), stop=(kk == 7))
                        return ins
                    t_half[j] = k.op(T, t_wo, inc="st")
                st8["at_free"][t] = ("st", t_half[1])
                k.wait(Sc, f"sxt{t}", x_dma)
                k.waitp(Sc, st8["yf_free"][t])
                s_y = k.op(Sc, lambda E, t=t, i=i: E.activation(
                    out=yf2[t][:], in_=xt2[t][:], func=AF.Identity,
                    bias=nb_sl[:, i:i + 1], scale=rstd_sl[:, i:i + 1]), inc="ss")
                st8["xt_free"][t] = ("ss", s_y)
                k.wait(V, "ss", s_y)
                v_t1 = k.op(V, lambda E, t=t: E.tensor_add(
                    out=scrA[:], in0=yf2[t][:], in1=xt2[t][:]), inc="sv")
                st8["yf_free"][t] = ("sv", v_t1)
                k.wait(V, "sv", v_t1)
                k.waitp(V, st8["ro_free"][t])
                v_r = 0
                for j in range(2):
                    k.wait(V, "st", t_half[j])
                    v_r = k.op(V, lambda E, t=t, j=j: E.tensor_add(
                        out=ro2[t][:, WT * j:WT * (j + 1)],
                        in0=scrA[:, WT * j:WT * (j + 1)], in1=ps[:, 6 + j, :]),
                        inc="sv")
                    st8["bank_free"][6 + j] = ("sv", v_r)
                k.wait(Sc, "sv", v_r)
                o_dma = k.op(Sc, lambda E, t=t, rows=rows, r2dst=r2dst: E.dma_start(
                    out=r2dst[rows:rows + 128, :], in_=ro2[t][:]), inc=f"sro{t}", n=16)
                st8["ro_free"][t] = (f"sro{t}", o_dma)
                if debug_stop is None and i < 16:
                    ln_stats_sbuf(ro2[t], i)

        if debug_stop == "R":
            yT_ready = ln_pass(d["x"], NTL, yT_blk, slot0=0)
            rv = routing_pass(yT_blk, yT_ready)
            k.wait(V, "sv", rv)
            k.op(V, lambda E: E.tensor_copy(
                out=scrA[:, 0:512], in_=cw[:].rearrange("p a b -> p (a b)")))
            k.op(V, lambda E: E.tensor_copy(
                out=scrA[:, 512:768], in_=sel[:].rearrange("p a b -> p (a b)")))
            k.op(V, lambda E: E.tensor_copy(out=scrA[:, 768:784], in_=rstd_sl[:, 0:16]))
            k.op(V, lambda E: E.tensor_copy(out=scrA[:, 800:816], in_=nb_sl[:, 0:16]))
            k.op(V, lambda E: E.tensor_copy(out=scrA[:, 832:848], in_=sum_sl[:, 0:16]))
            k.op(V, lambda E: E.tensor_copy(out=scrA[:, 896:912], in_=mean_sl[:, 0:16]))
            k.op(V, lambda E: E.tensor_copy(out=scrA[:, 912:928], in_=var_sl[:, 0:16]))
            k.op(V, lambda E: E.tensor_copy(out=scrA[:, 928:944], in_=std_sl[:, 0:16]))
            v_d = k.op(V, lambda E: E.tensor_copy(out=scrA[:, 864:880],
                                                  in_=sumsq_sl[:, 0:16]), inc="sv")
            k.wait(Sy, "sv", v_d)
            k.op(Sy, lambda E: E.dma_start(out=dout[0:128, :], in_=scrA[:]),
                 inc="sx", n=16)
            for kk in range(8):
                k.op(V, lambda E, kk=kk: E.tensor_copy(
                    out=ro2[0][:, 128 * kk:128 * (kk + 1)], in_=yT_blk[:, kk, 0:128]))
            v_y = k.op(V, lambda E: E.engine_nop(), inc="sv")
            k.wait(Sy, "sv", v_y)
            k.op(Sy, lambda E: E.dma_start(out=dout[128:256, :], in_=ro2[0][:]),
                 inc="sx", n=16)
            for s in k.sems:
                k.wait(Sy, s, k.ct[s])
            with nc.Block() as block:
                for eng, name in ((block.sync, "sync"), (block.gpsimd, "gpsimd"),
                                  (block.vector, "vector"), (block.scalar, "scalar"),
                                  (block.tensor, "tensor")):
                    def mk(name):
                        def f(E):
                            for fn in k.q[name]:
                                fn(E)
                        return f
                    eng(mk(name))
            return nc
        gu = ffn_phase(d["x"], dout if debug_stop == "A" else dr1, 0,
                       want_stats=(debug_stop != "A"))
        if debug_stop != "A":
            k.full_barrier()
            attention_phase()
            if debug_stop != "B":
                k.full_barrier()
                ffn_phase(dr2, dout, gu, first_prepped=True)
        for s in k.sems:
            k.wait(Sy, s, k.ct[s])

        with nc.Block() as block:
            @block.sync
            def _(E):
                for fn in k.q["sync"]:
                    fn(E)

            @block.gpsimd
            def _(E):
                for fn in k.q["gpsimd"]:
                    fn(E)

            @block.vector
            def _(E):
                for fn in k.q["vector"]:
                    fn(E)

            @block.scalar
            def _(E):
                for fn in k.q["scalar"]:
                    fn(E)

            @block.tensor
            def _(E):
                for fn in k.q["tensor"]:
                    fn(E)

    return nc


def kernel(**inputs):
    x = np.ascontiguousarray(np.asarray(inputs["L0_attn_x"], dtype=np.float32))
    base = {n: np.ascontiguousarray(np.asarray(inputs[n], dtype=np.float32))
            for n in WNAMES}
    nc = build_program()
    in_maps = []
    for c in range(8):
        m = dict(base)
        m["L0_attn_x"] = np.ascontiguousarray(x[c])
        in_maps.append(m)
    res = run_bass_kernel_spmd(nc, in_maps, core_ids=list(range(8)))
    return np.stack([r["out"] for r in res.results], axis=0)
